# version 03
# baseline (speedup 1.0000x reference)
import numpy as np
import ml_dtypes
import concourse.bass as bass
import concourse.mybir as mybir
from concourse.bass_utils import run_bass_kernel_spmd
from contextlib import ExitStack

F32 = mybir.dt.float32
BF16 = mybir.dt.bfloat16
AF = mybir.ActivationFunctionType
ALU = mybir.AluOpType
AX = mybir.AxisListType

P = 128
D = 2048
NB = 4
TLOC = 256
NTOK = NB * TLOC
NT = NTOK // P
KD = D // P
NH = 16
HD = 128
NKV = 4
FFN = 4096
NE = 8
NCORE = 8
EPS = 1e-5
RS = 1.0 / float(np.sqrt(128.0))

ENGS = ("sync", "tensor", "vector", "scalar", "gpsimd")


class Op:
    __slots__ = ("eng", "fn", "reads", "writes", "kind", "needs_inc", "sem",
                 "val", "waits", "idx", "tag", "deps", "raw", "big")

    def __init__(self, eng, fn, reads, writes, kind, tag, big=False):
        self.eng = eng
        self.fn = fn
        self.reads = reads
        self.writes = writes
        self.kind = kind
        self.needs_inc = kind in ("d", "cc")
        self.sem = None
        self.val = None
        self.waits = []
        self.tag = tag
        self.big = big


class Sem:
    def __init__(self, h):
        self.h = h
        self.n = 0

    def inc(self, k):
        self.n += k
        return self.n


class Builder:
    def __init__(self):
        self.ops = []
        self.last_w = {}
        self.readers = {}

    def op(self, eng, fn, reads=(), writes=(), kind="c", tag="", big=False):
        reads = list(reads)
        writes = list(writes)
        for k in [k for k in reads if k and k[0] == "ps"]:
            reads.remove(k)
            if k not in writes:
                writes.append(k)
        o = Op(eng, fn, reads, writes, kind, tag, big)
        o.idx = len(self.ops)
        raw = set()
        for r in o.reads:
            j = self.last_w.get(r)
            if j is not None:
                raw.add(j)
        other = set()
        for w in o.writes:
            j = self.last_w.get(w)
            if j is not None:
                other.add(j)
            for j2 in self.readers.get(w, ()):
                other.add(j2)
        raw.discard(o.idx)
        other.discard(o.idx)
        o.raw = raw
        o.deps = raw | other
        self.ops.append(o)
        for r in o.reads:
            self.readers.setdefault(r, []).append(o.idx)
        for w in o.writes:
            self.last_w[w] = o.idx
            self.readers[w] = []
        return o

    @staticmethod
    def _skip(d, o, is_raw):
        if d.kind != "c" or d.eng != o.eng:
            return False
        if d.eng in ("tensor", "gpsimd", "sync"):
            return True
        if not is_raw:
            return True
        return d.big

    def resolve(self, csem, dsem, ccsem):
        for o in self.ops:
            for j in o.deps:
                d = self.ops[j]
                if self._skip(d, o, j in o.raw):
                    continue
                d.needs_inc = True
        for o in self.ops:
            if not o.needs_inc:
                continue
            if o.kind == "d":
                o.sem = dsem[o.eng]
                o.val = o.sem.inc(16)
            elif o.kind == "cc":
                o.sem = ccsem
                o.val = o.sem.inc(1)
            else:
                o.sem = csem[o.eng]
                o.val = o.sem.inc(1)
        marks = {e: {} for e in ENGS}
        for o in self.ops:
            m = marks[o.eng]
            for j in sorted(o.deps):
                d = self.ops[j]
                if self._skip(d, o, j in o.raw):
                    continue
                key = id(d.sem)
                if m.get(key, -1) >= d.val:
                    continue
                m[key] = d.val
                o.waits.append((d.sem, d.val))

    def emit(self, block):
        per_eng = {e: [] for e in ENGS}
        for o in self.ops:
            per_eng[o.eng].append(o)

        def runner(ops):
            def body(q):
                for o in ops:
                    for (s, v) in o.waits:
                        q.wait_ge(s.h, v)
                    last = o.fn(q)
                    if o.needs_inc:
                        assert last is not None, f"op {o.tag} needs inc but returned None"
                        last.then_inc(o.sem.h, 16 if o.kind == "d" else 1)
            return body
        for e in ENGS:
            getattr(block, e)(runner(per_eng[e]))


def rkeys(name, c0, c1, quantum=1024):
    return [(name, i) for i in range(c0 // quantum, (c1 + quantum - 1) // quantum)]


def stk(c):
    return ("st", c // 16)


def gkeys(c0, c1):
    return [("G", i) for i in range(c0 // 256, (c1 + 255) // 256)]


_STATIC_CONSTS = None


def _static_consts():
    global _STATIC_CONSTS
    if _STATIC_CONSTS is None:
        _STATIC_CONSTS = {
            "c_ident": np.eye(P, dtype=ml_dtypes.bfloat16),
            "c_triu": np.triu(np.ones((P, P), dtype=np.float32)),
            "c_mstrict": (-1e30 * np.tril(np.ones((P, P), dtype=np.float32), -1)).astype(np.float32),
            "c_ones": np.ones((P, P), dtype=np.float32),
            "c_onesrow": np.ones((1, P), dtype=ml_dtypes.bfloat16),
            "c_id8": np.eye(8, dtype=np.float32),
        }
    return _STATIC_CONSTS


def _mk_core_consts(core):
    m2 = np.zeros((8, 8), dtype=np.float32)
    for cpp in range(8):
        for r in range(8):
            if r < cpp < core:
                m2[cpp, r] = 1.0
    m2exp = np.tile(m2.T.reshape(1, 64), (64, 1)).astype(np.float32)
    mask1 = np.zeros((64, 8), dtype=np.float32)
    mask1[:, :core] = 1.0
    return m2exp, mask1


def build_program(debug=(), stages=("ffn0", "dn", "ffn1")):
    nc = bass.Bass("TRN2", target_bir_lowering=False, debug=False)
    dbg = set(debug)
    B = Builder()

    def din(name, shape, dt=F32):
        return nc.dram_tensor(name, shape, dt, kind="ExternalInput").ap()

    x_in = din("x", [NTOK, D])
    qkv_wt = din("qkv_wt", [D, 3072])
    proj_wt = din("proj_wt", [D, D])
    router_wt = din("router_wt", [NE, D])
    ln1_g = din("ln1_g", [D]); ln1_b = din("ln1_b", [D])
    ln2_g = din("ln2_g", [D]); ln2_b = din("ln2_b", [D])
    moe_w1 = din("moe_w1", [NE, D, FFN]).rearrange("e r c -> (e r) c")
    moe_w2 = din("moe_w2", [NE, FFN, D]).rearrange("e r c -> (e r) c")
    wq_d = din("wq", [D, D]); wk_d = din("wk", [D, D]); wv_d = din("wv", [D, D])
    bq_d = din("bq", [D]); bk_d = din("bk", [D]); bv_d = din("bv", [D])
    wab_d = din("wab", [D, 32]); bab_d = din("bab", [32])
    c_ident = din("c_ident", [P, P], BF16)
    c_triu = din("c_triu", [P, P])
    c_mstrict = din("c_mstrict", [P, P])
    c_ones = din("c_ones", [P, P])
    c_onesrow = din("c_onesrow", [1, P], BF16)
    c_id8 = din("c_id8", [8, 8])
    c_m2exp = din("c_m2exp", [64, 64])
    c_mask1 = din("c_mask1", [64, 8])

    y_out = nc.dram_tensor("y", [NTOK, D], F32, kind="ExternalOutput").ap()
    st_exp_h = nc.dram_tensor("st_exp", [64 * P, HD], BF16)
    st_gath_h = nc.dram_tensor("st_gath", [NCORE * 64 * P, HD], BF16)
    cs_exp_h = nc.dram_tensor("cs_exp", [64, 1], F32)
    cs_gath_h = nc.dram_tensor("cs_gath", [NCORE * 64, 1], F32)
    sclT_dram = nc.dram_tensor("sclT_d", [64, 8], F32)
    wcaches = [nc.dram_tensor(f"wcache{i}", [72 * 1024 * 1024], BF16).ap()
               for i in range(3)]

    es = ExitStack()

    def sb(name, pdim, cols, dt):
        return es.enter_context(nc.sbuf_tensor(name, [pdim, cols], dt))

    def mkpsum(name, cols, dt):
        return es.enter_context(nc.psum_tensor(name, [P, cols], dt))

    def mksem(name):
        return Sem(es.enter_context(nc.semaphore(name)))

    s_x = sb("s_x", P, NT * D, F32)
    s_A = sb("s_A", P, 16384, BF16)
    s_Bf = sb("s_Bf", P, 16384, BF16)
    s_C = sb("s_C", P, 16384, BF16)
    s_D = sb("s_D", P, 6656, BF16)
    s_E = sb("s_E", P, 8192, BF16)
    s_G = sb("s_G", P, 768, F32)
    s_stage = sb("s_stage", P, 1024, F32)
    s_cvt = sb("s_cvt", P, 1024, BF16)
    s_stats = sb("s_stats", P, 128, F32)
    s_gam = sb("s_gam", P, 4 * KD, F32)
    s_bq = sb("s_bq", P, KD, F32)
    s_gates = sb("s_gates", P, NT * 8, F32)
    s_brow = sb("s_brow", 1, 544, BF16)
    s_eps1 = sb("s_eps1", P, 1, F32)
    s_eps2 = sb("s_eps2", P, 1, F32)
    s_eps3 = sb("s_eps3", P, 1, F32)
    s_zero = sb("s_zero", P, 1, F32)
    s_one = sb("s_one", P, 1, F32)
    s_id = sb("s_id", P, P, BF16)
    s_triu = sb("s_triu", P, P, F32)
    s_mst = sb("s_mst", P, P, F32)
    s_onesf = sb("s_onesf", P, P, F32)
    s_onesrow = sb("s_onesrow", 1, P, BF16)
    s_router = sb("s_router", P, KD * 8, BF16)
    s_la = sb("s_la", P, NT * NH, F32)
    s_beta = sb("s_beta", P, NT * NH, F32)
    s_cla = sb("s_cla", P, NT * NH, F32)
    s_ncla = sb("s_ncla", P, NT * NH, F32)
    s_td0 = sb("s_td0", P, 64, F32)
    s_ces = sb("s_ces", P, 128, F32)
    s_csum = sb("s_csum", P, 64, F32)
    s_ab = sb("s_ab", P, KD * 32, BF16)
    s_id8 = sb("s_id8", 8, 8, F32)
    s_m2exp = sb("s_m2exp", 64, 64, F32)
    s_mask1 = sb("s_mask1", 64, 8, F32)
    s_csg = sb("s_csg", 64, 8, F32)
    s_L = sb("s_L", 64, 8, F32)
    s_scl = sb("s_scl", 64, 8, F32)
    s_sclT = sb("s_sclT", 8, 64, F32)

    pb = [mkpsum(f"pb{i}", 512, F32) for i in range(6)]
    ptA = mkpsum("ptA", 1024, BF16)
    ptB = mkpsum("ptB", 1024, BF16)
    PTN = {id(ptA): "ptA", id(ptB): "ptB"}

    csem = {e: mksem("c_" + e) for e in ENGS}
    dsem = {e: mksem("d_" + e) for e in ENGS}
    ccsem = mksem("cc")

    def OP(eng, fn, reads=(), writes=(), kind="c", tag="", big=False):
        return B.op(eng, fn, reads, writes, kind, tag, big)

    def DMA(eng, out, in_, reads, writes, tag="", slow=False):
        return OP(eng, lambda q: q.dma_start(out=out, in_=in_,
                                             allow_slow_non_contiguous=slow),
                  reads, writes, "d", tag)

    def pkey(pt_, c0=0):
        nm = PTN.get(id(pt_), pt_)
        return ("ps", nm, c0 // 512)

    bank_rot = [0]

    def next_bank(lo=0, hi=4):
        b = bank_rot[0] % (hi - lo) + lo
        bank_rot[0] += 1
        return b

    tr_rot = [0]

    def next_tr():
        t = tr_rot[0] % 4
        tr_rot[0] += 1
        return (ptA if t < 2 else ptB), (t % 2) * P

    dbg_ctr = [0]

    def dump(name, src_ap, read_keys):
        if name not in dbg:
            return
        shape = list(src_ap.shape)
        ap = nc.dram_tensor("dbg_" + name, shape, src_ap.dtype, kind="ExternalOutput").ap()
        DMA("gpsimd", ap, src_ap, list(read_keys), [("dbgsink", dbg_ctr[0])], tag="dbg_" + name)
        dbg_ctr[0] += 1

    GK = [("G", 0), ("G", 1), ("G", 2)]

    for r in range(NT):
        DMA("sync", s_x[:, r * D:(r + 1) * D], x_in[r * P:(r + 1) * P, :], [], [("x", r)])
    DMA("sync", s_id[:], c_ident, [], [("id",)])
    DMA("sync", s_triu[:], c_triu, [], [("triu",)])
    DMA("sync", s_mst[:], c_mstrict, [], [("mst",)])
    DMA("sync", s_onesf[:], c_ones, [], [("onesf",)])
    DMA("sync", s_onesrow[:], c_onesrow, [], [("onesrow",)])
    DMA("sync", s_id8[:], c_id8, [], [("id8",)])
    DMA("sync", s_m2exp[:], c_m2exp, [], [("m2exp",)])
    DMA("sync", s_mask1[:], c_mask1, [], [("mask1",)])
    for i, t in enumerate([ln1_g, ln1_b, ln2_g, ln2_b]):
        DMA("sync", s_gam[:, i * KD:(i + 1) * KD], t.rearrange("(f p) -> p f", p=P),
            [], [("gam", i)], slow=True)
    DMA("sync", s_bq[:], bq_d.rearrange("(f p) -> p f", p=P), [], [("bq",)], slow=True)
    DMA("sync", s_G[0:1, 0:32], bab_d.rearrange("(one d) -> one d", one=1), GK, GK, tag="babl")
    OP("vector", lambda q: q.tensor_copy(s_brow[0:1, 0:32], s_G[0:1, 0:32]),
       GK, [("brow", 0)], tag="babc")
    for hf in range(4):
        DMA("sync", s_G[0:8, 0:512], router_wt[:, hf * 512:(hf + 1) * 512], GK, GK,
            tag=f"routl{hf}")
        OP("vector", lambda q, hf=hf: q.tensor_copy(
            s_D[0:8, hf * 512:(hf + 1) * 512], s_G[0:8, 0:512]),
           GK, [("D", hf // 2)], tag=f"routc{hf}")
    for f in range(KD):
        pslot, pc = next_tr()
        pk = pkey(pslot, pc)
        OP("tensor", lambda q, f=f, pslot=pslot, pc=pc:
           q.transpose(pslot[:, pc:pc + 8], s_D[0:8, f * P:(f + 1) * P], s_id[0:8, 0:8]),
           [("D", 0), ("D", 1), ("id",)], [pk], tag=f"rtr{f}")
        OP("vector", lambda q, f=f, pslot=pslot, pc=pc:
           q.tensor_copy(s_router[:, f * 8:(f + 1) * 8], pslot[:, pc:pc + 8]),
           [pk], [("router",)], tag=f"rtrev{f}")
    for i in range(KD):
        DMA("sync", s_G[0:P, 0:32], wab_d[i * P:(i + 1) * P, :], GK, GK, tag=f"wabl{i}")
        OP("vector", lambda q, i=i: q.tensor_copy(s_ab[:, i * 32:(i + 1) * 32], s_G[:, 0:32]),
           GK, [("ab", i)], tag=f"wabc{i}")

    def f_eps(q):
        q.memset(s_eps1[:], EPS)
        q.memset(s_eps2[:], 1e-6)
        q.memset(s_eps3[:], 1e-24)
        q.memset(s_zero[:], 0.0)
        return q.memset(s_one[:], 1.0)
    OP("gpsimd", f_eps, [], [("eps",)])

    def ln_stats(src_ap, src_reads, so, tag=""):
        OP("vector", lambda q: q.tensor_reduce(out=s_stats[:, so:so + 1], in_=src_ap,
                                               axis=AX.X, op=ALU.add),
           src_reads, [stk(so), ("lnser",)], tag=tag + "s")
        for c4 in range(4):
            slot = 0
            OP("scalar", lambda q, c4=c4, slot=slot: q.activation(
                out=s_G[:, slot:slot + 512], in_=src_ap[:, c4 * 512:(c4 + 1) * 512],
                func=AF.Square),
               src_reads + gkeys(slot, slot + 512), gkeys(slot, slot + 512) + [("lnser",)],
               tag=tag + f"q{c4}")
            OP("vector", lambda q, c4=c4, slot=slot: q.tensor_reduce(
                out=s_stats[:, so + 16 + c4:so + 17 + c4], in_=s_G[:, slot:slot + 512],
                axis=AX.X, op=ALU.add),
               gkeys(slot, slot + 512), [stk(so + 16 + c4), ("lnser",)], tag=tag + f"r{c4}")

        OP("vector", lambda q: q.tensor_reduce(
            out=s_stats[:, so + 2:so + 3], in_=s_stats[:, so + 16:so + 20],
            axis=AX.X, op=ALU.add),
           [stk(so + 16), stk(so + 17), stk(so + 18), stk(so + 19)], [stk(so + 2)],
           tag=tag + "m0")
        OP("vector", lambda q: q.tensor_scalar(
            out=s_stats[:, so:so + 1], in0=s_stats[:, so:so + 1],
            scalar1=1.0 / D, scalar2=None, op0=ALU.mult),
           [stk(so)], [stk(so)], tag=tag + "m1")
        OP("vector", lambda q: q.tensor_tensor(
            out=s_stats[:, so + 3:so + 4], in0=s_stats[:, so:so + 1],
            in1=s_stats[:, so:so + 1], op=ALU.mult),
           [stk(so)], [stk(so + 3)], tag=tag + "m2")
        OP("vector", lambda q: q.tensor_scalar(
            out=s_stats[:, so + 2:so + 3], in0=s_stats[:, so + 2:so + 3],
            scalar1=1.0 / D, scalar2=None, op0=ALU.mult),
           [stk(so + 2)], [stk(so + 2)], tag=tag + "m3")
        OP("vector", lambda q: q.tensor_tensor(
            out=s_stats[:, so + 1:so + 2], in0=s_stats[:, so + 2:so + 3],
            in1=s_stats[:, so + 3:so + 4], op=ALU.subtract),
           [stk(so + 2), stk(so + 3)], [stk(so + 1)], tag=tag + "m4")
        OP("scalar", lambda q: q.activation(
            out=s_stats[:, so + 1:so + 2], in_=s_stats[:, so + 1:so + 2],
            func=AF.Ln, bias=s_eps1[:]),
           [stk(so + 1), ("eps",)], [stk(so + 1)], tag=tag + "r0")
        OP("scalar", lambda q: q.activation(
            out=s_stats[:, so + 1:so + 2], in_=s_stats[:, so + 1:so + 2],
            func=AF.Exp, scale=-0.5),
           [stk(so + 1)], [stk(so + 1)], tag=tag + "r1")

    stage_rot = [0]
    CVT = ["vector", "gpsimd", "scalar"]
    wc_alloc = [0]
    wc_blocks = {}
    WCCAP = 72 * 1024 * 1024
    ldq_rot = [0]

    def stream_win(name, src_fn, nseg, seg_cols, dst_ap, dst_keys, tag, tap=True):
        dst_keys = list(dst_keys)
        cols = nseg * seg_cols
        if name is not None and name in wc_blocks:
            off = wc_blocks[name]
            ti, to = divmod(off, WCCAP)
            blk = wcaches[ti][to:to + P * cols].rearrange("(p c) -> p c", p=P)
            OP("sync", lambda q: q.dma_start(out=dst_ap, in_=blk),
               [("wc", name)] + dst_keys, dst_keys, "d", tag + "R")
            return
        for i in range(nseg):
            srot = stage_rot[0] % 2
            stage_rot[0] += 1
            st_ap = s_stage[:, srot * 512:srot * 512 + seg_cols]
            ldq = ("sync", "scalar")[ldq_rot[0] % 2]
            ldq_rot[0] += 1

            def fld(q, st_ap=st_ap, i=i):
                s3 = src_fn(i)
                d = st_ap
                if len(s3.shape) == 3:
                    d = st_ap.rearrange("p (k f) -> p k f", f=s3.shape[2])
                return q.dma_start(out=d, in_=s3)
            OP(ldq, fld, [("stage", srot)], [("stage", srot)], "d", tag + f"l{i}")

            def fcv(q, st_ap=st_ap, i=i, seg_cols=seg_cols):
                d = dst_ap[:, i * seg_cols:(i + 1) * seg_cols]
                if hasattr(q, "tensor_copy"):
                    return q.tensor_copy(d, st_ap)
                return q.activation(out=d, in_=st_ap, func=AF.Copy)
            OP(("vector", "gpsimd")[i % 2], fcv, [("stage", srot)] + dst_keys, dst_keys,
               tag=tag + f"c{i}")
        if name is not None and tap:
            if name not in wc_blocks:
                need = P * cols
                if wc_alloc[0] % WCCAP + need > WCCAP:
                    wc_alloc[0] = (wc_alloc[0] // WCCAP + 1) * WCCAP
                wc_blocks[name] = wc_alloc[0]
                wc_alloc[0] += need
            off = wc_blocks[name]
            ti, to = divmod(off, WCCAP)
            blk = wcaches[ti][to:to + P * cols].rearrange("(p c) -> p c", p=P)
            OP("gpsimd", lambda q: q.dma_start(out=blk, in_=dst_ap),
               dst_keys, [("wc", name)], "d", tag + "T")

    def ffn_call(call):
        tg = f"f{call}"
        akeys = lambda r: rkeys("A", r * D, (r + 1) * D)
        bkeys_all = rkeys("B", 0, 16384)
        ckeys_all = rkeys("C", 0, 16384)

        for r in range(NT):
            so = (r % 2) * 32
            if r == 0:
                dump(f"xin{call}", s_x[:, 0:D], [("x", 0)])
            ln_stats(s_x[:, r * D:(r + 1) * D], [("x", r)], so, tag=f"{tg}ol{r}")
            if r == 0:
                dump(f"st{call}", s_stats[:, 0:32], [stk(0), stk(16)])
            OP("vector", lambda q, r=r, so=so: q.tensor_scalar(
                out=s_A[:, r * D:(r + 1) * D], in0=s_x[:, r * D:(r + 1) * D],
                scalar1=s_stats[:, so:so + 1], scalar2=s_stats[:, so + 1:so + 2],
                op0=ALU.subtract, op1=ALU.mult),
               [stk(so), stk(so + 1), ("x", r)], akeys(r), big=True, tag=f"{tg}lx{r}")

            OP("vector", lambda q, so=so: q.tensor_scalar(
                out=s_stats[:, so + 6:so + 7], in0=s_stats[:, so + 1:so + 2],
                scalar1=0.5, scalar2=1.0, op0=ALU.mult, op1=ALU.add),
               [stk(so + 1)], [stk(so + 6)], tag=f"{tg}xu0{r}")
            OP("vector", lambda q, so=so: q.tensor_tensor(
                out=s_stats[:, so + 7:so + 8], in0=s_stats[:, so:so + 1],
                in1=s_stats[:, so + 1:so + 2], op=ALU.mult),
               [stk(so), stk(so + 1)], [stk(so + 7)], tag=f"{tg}xu1{r}")
            OP("vector", lambda q, so=so: q.tensor_scalar(
                out=s_stats[:, so + 7:so + 8], in0=s_stats[:, so + 7:so + 8],
                scalar1=0.5, scalar2=None, op0=ALU.mult),
               [stk(so + 7)], [stk(so + 7)], tag=f"{tg}xu2{r}")
            OP("vector", lambda q, r=r, so=so: q.tensor_scalar(
                out=s_x[:, r * D:(r + 1) * D], in0=s_x[:, r * D:(r + 1) * D],
                scalar1=s_stats[:, so + 6:so + 7], scalar2=s_stats[:, so + 7:so + 8],
                op0=ALU.mult, op1=ALU.subtract),
               [stk(so + 6), stk(so + 7), ("x", r)], [("x", r)], big=True, tag=f"{tg}xu3{r}")

            so1 = 64 + (r % 2) * 32
            ln_stats(s_A[:, r * D:(r + 1) * D], akeys(r), so1, tag=f"{tg}l1{r}")
            OP("vector", lambda q, r=r, so1=so1: q.tensor_scalar(
                out=s_C[:, 0:D], in0=s_A[:, r * D:(r + 1) * D],
                scalar1=s_stats[:, so1:so1 + 1], scalar2=s_stats[:, so1 + 1:so1 + 2],
                op0=ALU.subtract, op1=ALU.mult),
               [stk(so1), stk(so1 + 1)] + akeys(r), rkeys("C", 0, D), big=True, tag=f"{tg}1a{r}")

            for f in range(KD):
                pslot, pc = next_tr()
                pk = pkey(pslot, pc)
                OP("tensor", lambda q, f=f, pslot=pslot, pc=pc:
                   q.transpose(pslot[:, pc:pc + P], s_C[:, f * P:(f + 1) * P], s_id[:]),
                   rkeys("C", 0, D) + [("id",)], [pk], tag=f"{tg}t1{r}_{f}")
                OP("vector", lambda q, r=r, f=f, pslot=pslot, pc=pc: q.tensor_scalar(
                    out=s_Bf[:, f * NTOK + r * P:f * NTOK + (r + 1) * P],
                    in0=pslot[:, pc:pc + P],
                    scalar1=s_gam[:, f:f + 1], scalar2=s_gam[:, KD + f:KD + f + 1],
                    op0=ALU.mult, op1=ALU.add),
                   [pk, ("gam", 0), ("gam", 1)], [("B", (f * NTOK + r * P) // 1024)],
                   tag=f"{tg}e1{r}_{f}")

        dump(f"lnx{call}", s_A[:, :], [("A", i) for i in range(16)])
        dump(f"ln1fm{call}", s_Bf[:, :], bkeys_all)

        pairs = [(s_, t_) for s_ in range(NB) for t_ in range(s_ + 1)]
        SC0, WS0, SM0 = 0, 80, 160
        for g in range(NKV):
            stream_win(f"qkvq{g}", lambda i, g=g: qkv_wt[i * P:(i + 1) * P,
                                                         768 * g:768 * g + 512],
                       KD, 512, s_E[:, 0:8192], rkeys("E", 0, 8192), f"{tg}qw{g}")
            ekeys = rkeys("E", 0, 8192)
            for r in range(NT):
                bank = next_bank()
                def f_qmm(q, r=r, bank=bank):
                    for kk in range(KD):
                        mm = q.matmul(pb[bank][:, 0:512],
                                      s_Bf[:, kk * NTOK + r * P:kk * NTOK + (r + 1) * P],
                                      s_E[:, kk * 512:(kk + 1) * 512],
                                      start=(kk == 0), stop=(kk == KD - 1))
                    return mm
                OP("tensor", f_qmm, bkeys_all + ekeys, [pkey(bank)], tag=f"{tg}qm{g}_{r}")
                OP("scalar", lambda q, r=r, bank=bank: q.activation(
                    out=s_D[:, r * 768:r * 768 + 512], in_=pb[bank][:, 0:512], func=AF.Copy),
                   [pkey(bank)], rkeys("D", r * 768, r * 768 + 512), tag=f"{tg}qe{g}_{r}")
            stream_win(f"qkvk{g}", lambda i, g=g: qkv_wt[i * 256:(i + 1) * 256,
                                                          768 * g + 512:768 * (g + 1)].rearrange(
                           "(k p) f -> p k f", p=P),
                       KD // 2, 512, s_E[:, 0:4096], rkeys("E", 0, 4096), f"{tg}kw{g}")
            ekeys2 = rkeys("E", 0, 4096)
            for r in range(NT):
                bank = next_bank()
                def f_kvmm(q, r=r, bank=bank):
                    for kk in range(KD):
                        mm = q.matmul(pb[bank][:, 0:256],
                                      s_Bf[:, kk * NTOK + r * P:kk * NTOK + (r + 1) * P],
                                      s_E[:, kk * 256:(kk + 1) * 256],
                                      start=(kk == 0), stop=(kk == KD - 1))
                    return mm
                OP("tensor", f_kvmm, bkeys_all + ekeys2, [pkey(bank)], tag=f"{tg}km{g}_{r}")
                OP("scalar", lambda q, r=r, bank=bank: q.activation(
                    out=s_D[:, r * 768 + 512:r * 768 + 768], in_=pb[bank][:, 0:256],
                    func=AF.Copy),
                   [pkey(bank)], rkeys("D", r * 768 + 512, r * 768 + 768), tag=f"{tg}ke{g}_{r}")

            dall = rkeys("D", 0, 6144)
            for pi, (s_, t_) in enumerate(pairs):
                for half in range(2):
                    rq = 2 * s_ + half
                    rk = 2 * t_ + half
                    def f_pr(q, rq=rq, rk=rk):
                        qv = s_D[:, rq * 768:rq * 768 + 512].rearrange("p (h d) -> p h d", h=4)
                        kv = s_D[:, rk * 768 + 512:rk * 768 + 640].rearrange(
                            "p (one d) -> p one d", one=1).broadcast_to([P, 4, HD])
                        pr = s_D[:, 6144:6656].rearrange("p (h d) -> p h d", h=4)
                        return q.tensor_tensor(out=pr, in0=qv, in1=kv, op=ALU.mult)
                    OP("vector", f_pr, dall, rkeys("D", 6144, 6656), tag=f"{tg}p{g}_{pi}_{half}")
                    sc = SC0 + half * 40 + pi * 4
                    OP("vector", lambda q, sc=sc: q.tensor_reduce(
                        out=s_G[:, sc:sc + 4],
                        in_=s_D[:, 6144:6656].rearrange("p (h d) -> p h d", h=4),
                        axis=AX.X, op=ALU.add),
                       rkeys("D", 6144, 6656) + GK, GK + [("sc", sc)], tag=f"{tg}r{g}_{pi}_{half}")

            screads = [("sc", SC0 + half * 40 + pi * 4) for half in range(2) for pi in range(10)]

            SMK = screads + GK + [("wsm",)]
            for half in range(2):
                base = SC0 + half * 40
                wbase = WS0 + half * 40
                for s_ in range(1, NB):
                    p0 = s_ * (s_ + 1) // 2
                    OP("vector", lambda q, base=base, p0=p0: q.tensor_copy(
                        s_G[:, SM0:SM0 + 4], s_G[:, base + p0 * 4:base + p0 * 4 + 4]),
                       SMK, GK + [("wsm",)], tag=f"{tg}sa{g}_{half}_{s_}")
                    for t_ in range(1, s_ + 1):
                        OP("vector", lambda q, base=base, p0=p0, t_=t_: q.tensor_tensor(
                            out=s_G[:, SM0:SM0 + 4], in0=s_G[:, SM0:SM0 + 4],
                            in1=s_G[:, base + (p0 + t_) * 4:base + (p0 + t_) * 4 + 4],
                            op=ALU.max),
                           SMK, GK + [("wsm",)], tag=f"{tg}sb{g}_{half}_{s_}_{t_}")

                    def f_subs(q, base=base, wbase=wbase, p0=p0, s_=s_):
                        for t_ in range(s_ + 1):
                            last = q.tensor_tensor(
                                out=s_G[:, wbase + (p0 + t_) * 4:wbase + (p0 + t_) * 4 + 4],
                                in0=s_G[:, base + (p0 + t_) * 4:base + (p0 + t_) * 4 + 4],
                                in1=s_G[:, SM0:SM0 + 4], op=ALU.subtract)
                        return last
                    OP("vector", f_subs, SMK, GK + [("wsm",)], tag=f"{tg}sc{g}_{half}_{s_}")

            def f_exp(q):
                for half in range(2):
                    wbase = WS0 + half * 40
                    for s_ in range(1, NB):
                        p0 = s_ * (s_ + 1) // 2
                        for t_ in range(s_ + 1):
                            c0 = wbase + (p0 + t_) * 4
                            last = q.activation(out=s_G[:, c0:c0 + 4], in_=s_G[:, c0:c0 + 4],
                                                func=AF.Exp, scale=RS)
                return last
            OP("scalar", f_exp, SMK, GK + [("wsm",)], tag=f"{tg}se{g}")

            for half in range(2):
                wbase = WS0 + half * 40
                for s_ in range(1, NB):
                    p0 = s_ * (s_ + 1) // 2
                    cols = [wbase + (p0 + t_) * 4 for t_ in range(s_ + 1)]
                    OP("vector", lambda q, cols=cols: q.tensor_copy(
                        s_G[:, SM0:SM0 + 4], s_G[:, cols[0]:cols[0] + 4]),
                       SMK, GK + [("wsm",)], tag=f"{tg}sd{g}_{half}_{s_}")
                    for t_ in range(1, s_ + 1):
                        OP("vector", lambda q, cols=cols, t_=t_: q.tensor_tensor(
                            out=s_G[:, SM0:SM0 + 4], in0=s_G[:, SM0:SM0 + 4],
                            in1=s_G[:, cols[t_]:cols[t_] + 4], op=ALU.add),
                           SMK, GK + [("wsm",)], tag=f"{tg}sf{g}_{half}_{s_}_{t_}")
                    OP("vector", lambda q: q.reciprocal(s_G[:, SM0 + 4:SM0 + 8],
                                                        s_G[:, SM0:SM0 + 4]),
                       SMK, GK + [("wsm",)], tag=f"{tg}sg2{g}_{half}_{s_}")

                    def f_mul(q, cols=cols, s_=s_):
                        for t_ in range(s_ + 1):
                            last = q.tensor_tensor(out=s_G[:, cols[t_]:cols[t_] + 4],
                                                   in0=s_G[:, cols[t_]:cols[t_] + 4],
                                                   in1=s_G[:, SM0 + 4:SM0 + 8], op=ALU.mult)
                        return last
                    OP("vector", f_mul, SMK, GK + [("wsm",)], tag=f"{tg}sh{g}_{half}_{s_}")

            for half in range(2):
                for s_ in range(NB):
                    rt = 2 * s_ + half
                    ckeys = rkeys("C", rt * D + g * 512, rt * D + (g + 1) * 512)
                    for t_ in range(s_ + 1):
                        rk = 2 * t_ + half
                        pi = s_ * (s_ + 1) // 2 + t_
                        for j in range(4):
                            wcol = WS0 + half * 40 + pi * 4 + j
                            yap = s_C[:, rt * D + (4 * g + j) * HD:rt * D + (4 * g + j + 1) * HD]
                            vap = s_D[:, rk * 768 + 640:rk * 768 + 768]
                            if s_ == 0:
                                OP("vector", lambda q, yap=yap, vap=vap: q.tensor_copy(yap, vap),
                                   dall, ckeys, tag=f"{tg}w0{g}_{half}_{j}")
                            elif t_ == 0:
                                OP("vector", lambda q, yap=yap, vap=vap, wcol=wcol:
                                   q.tensor_scalar(out=yap, in0=vap,
                                                   scalar1=s_G[:, wcol:wcol + 1],
                                                   scalar2=None, op0=ALU.mult),
                                   dall + [("wsm",)] + GK, ckeys, tag=f"{tg}wf{g}_{s_}_{half}_{j}")
                            else:
                                OP("vector", lambda q, yap=yap, vap=vap, wcol=wcol:
                                   q.scalar_tensor_tensor(
                                       out=yap, in0=vap, scalar=s_G[:, wcol:wcol + 1],
                                       in1=yap, op0=ALU.mult, op1=ALU.add),
                                   dall + [("wsm",)] + GK + ckeys, ckeys,
                                   tag=f"{tg}wa{g}_{s_}_{t_}_{half}_{j}")

        dump(f"yattn{call}", s_C[:, :], ckeys_all)

        for r in range(NT):
            for f in range(KD):
                pslot, pc = next_tr()
                pk = pkey(pslot, pc)
                OP("tensor", lambda q, r=r, f=f, pslot=pslot, pc=pc:
                   q.transpose(pslot[:, pc:pc + P], s_C[:, r * D + f * P:r * D + (f + 1) * P],
                               s_id[:]),
                   rkeys("C", r * D + f * P, r * D + (f + 1) * P) + [("id",)], [pk],
                   tag=f"{tg}yt{r}_{f}")
                OP("vector", lambda q, r=r, f=f, pslot=pslot, pc=pc:
                   q.tensor_copy(s_Bf[:, f * NTOK + r * P:f * NTOK + (r + 1) * P],
                                 pslot[:, pc:pc + P]),
                   [pk], [("B", (f * NTOK + r * P) // 1024)], tag=f"{tg}ye{r}_{f}")

        for win in range(4):
            stream_win(f"proj{win}", lambda i, win=win: proj_wt[i * P:(i + 1) * P,
                                                               win * 512:(win + 1) * 512],
                       KD, 512, s_E[:, 0:8192], rkeys("E", 0, 8192), f"{tg}pw{win}")
            ekeys = rkeys("E", 0, 8192)
            for r in range(NT):
                bank = next_bank()
                def f_pmm(q, r=r, bank=bank):
                    for kk in range(KD):
                        mm = q.matmul(pb[bank][:, 0:512],
                                      s_Bf[:, kk * NTOK + r * P:kk * NTOK + (r + 1) * P],
                                      s_E[:, kk * 512:(kk + 1) * 512],
                                      start=(kk == 0), stop=(kk == KD - 1))
                    return mm
                OP("tensor", f_pmm, bkeys_all + ekeys, [pkey(bank)], tag=f"{tg}pm{win}_{r}")
                xc = r * D + win * 512
                OP("vector", lambda q, bank=bank, xc=xc, r=r: q.scalar_tensor_tensor(
                    out=s_x[:, xc:xc + 512], in0=pb[bank][:, 0:512], scalar=0.5,
                    in1=s_x[:, xc:xc + 512], op0=ALU.mult, op1=ALU.add),
                   [pkey(bank), ("x", r)], [("x", r)], big=True, tag=f"{tg}px{win}_{r}")
                OP("vector", lambda q, bank=bank, xc=xc: q.scalar_tensor_tensor(
                    out=s_A[:, xc:xc + 512], in0=pb[bank][:, 0:512], scalar=1.0,
                    in1=s_A[:, xc:xc + 512], op0=ALU.mult, op1=ALU.add),
                   [pkey(bank)] + rkeys("A", xc, xc + 512), rkeys("A", xc, xc + 512),
                   big=True, tag=f"{tg}pa{win}_{r}")

        dump(f"hba{call}", s_A[:, :], [("A", i) for i in range(16)])

        for r in range(NT):
            so2 = (r % 2) * 32
            ln_stats(s_A[:, r * D:(r + 1) * D], akeys(r), so2, tag=f"{tg}l2{r}")
            OP("vector", lambda q, r=r, so2=so2: q.tensor_scalar(
                out=s_D[:, 0:D], in0=s_A[:, r * D:(r + 1) * D],
                scalar1=s_stats[:, so2:so2 + 1], scalar2=s_stats[:, so2 + 1:so2 + 2],
                op0=ALU.subtract, op1=ALU.mult),
               [stk(so2), stk(so2 + 1)] + akeys(r), rkeys("D", 0, D), big=True, tag=f"{tg}2a{r}")
            for f in range(KD):
                pslot, pc = next_tr()
                pk = pkey(pslot, pc)
                OP("tensor", lambda q, f=f, pslot=pslot, pc=pc:
                   q.transpose(pslot[:, pc:pc + P], s_D[:, f * P:(f + 1) * P], s_id[:]),
                   rkeys("D", 0, D) + [("id",)], [pk], tag=f"{tg}t2{r}_{f}")
                OP("vector", lambda q, r=r, f=f, pslot=pslot, pc=pc: q.tensor_scalar(
                    out=s_C[:, f * NTOK + r * P:f * NTOK + (r + 1) * P],
                    in0=pslot[:, pc:pc + P],
                    scalar1=s_gam[:, 2 * KD + f:2 * KD + f + 1],
                    scalar2=s_gam[:, 3 * KD + f:3 * KD + f + 1],
                    op0=ALU.mult, op1=ALU.add),
                   [pk, ("gam", 2), ("gam", 3)], [("C", (f * NTOK + r * P) // 1024)],
                   tag=f"{tg}e2{r}_{f}")

        dump(f"xmoe{call}", s_C[:, :], ckeys_all)

        for r in range(NT):
            bank = next_bank(4, 6)
            def f_rmm(q, r=r, bank=bank):
                for kk in range(KD):
                    mm = q.matmul(pb[bank][:, 0:8],
                                  s_C[:, kk * NTOK + r * P:kk * NTOK + (r + 1) * P],
                                  s_router[:, kk * 8:(kk + 1) * 8],
                                  start=(kk == 0), stop=(kk == KD - 1))
                return mm
            OP("tensor", f_rmm, ckeys_all + [("router",)], [pkey(bank)], tag=f"{tg}rm{r}")

            def VOP(fn, tg2):
                OP("vector", fn, GK + [pkey(bank), ("gate", r)],
                   GK + [("gate", r)], tag=tg2)
            VOP(lambda q, bank=bank: q.tensor_reduce(
                out=s_G[:, 8:9], in_=pb[bank][:, 0:8], axis=AX.X, op=ALU.max),
                f"{tg}ga{r}")
            VOP(lambda q, bank=bank: q.tensor_scalar(
                out=s_G[:, 0:8], in0=pb[bank][:, 0:8], scalar1=s_G[:, 8:9],
                scalar2=None, op0=ALU.subtract), f"{tg}gb{r}")
            OP("scalar", lambda q: q.activation(out=s_G[:, 0:8], in_=s_G[:, 0:8],
                                                func=AF.Exp),
               GK, GK, tag=f"{tg}gc{r}")
            VOP(lambda q: q.tensor_reduce(out=s_G[:, 9:10], in_=s_G[:, 0:8],
                                          axis=AX.X, op=ALU.add), f"{tg}gd{r}")
            VOP(lambda q: q.reciprocal(s_G[:, 10:11], s_G[:, 9:10]), f"{tg}ge2{r}")
            VOP(lambda q: q.tensor_scalar(out=s_G[:, 16:24], in0=s_G[:, 0:8],
                                          scalar1=s_G[:, 10:11], scalar2=None,
                                          op0=ALU.mult), f"{tg}gf{r}")
            VOP(lambda q: q.tensor_reduce(out=s_G[:, 11:12], in_=s_G[:, 16:24],
                                          axis=AX.X, op=ALU.max), f"{tg}gg{r}")
            VOP(lambda q: q.tensor_scalar(out=s_G[:, 24:32], in0=s_G[:, 16:24],
                                          scalar1=s_G[:, 11:12], scalar2=None,
                                          op0=ALU.is_lt), f"{tg}gh{r}")
            VOP(lambda q: q.tensor_tensor(out=s_G[:, 24:32], in0=s_G[:, 16:24],
                                          in1=s_G[:, 24:32], op=ALU.mult), f"{tg}gi{r}")
            VOP(lambda q: q.tensor_reduce(out=s_G[:, 12:13], in_=s_G[:, 24:32],
                                          axis=AX.X, op=ALU.max), f"{tg}gj{r}")
            VOP(lambda q: q.tensor_scalar(out=s_G[:, 24:32], in0=s_G[:, 16:24],
                                          scalar1=s_G[:, 11:12], scalar2=None,
                                          op0=ALU.is_ge), f"{tg}gk{r}")
            VOP(lambda q: q.tensor_reduce(out=s_G[:, 13:14], in_=s_G[:, 24:32],
                                          axis=AX.X, op=ALU.add), f"{tg}gl{r}")
            VOP(lambda q: q.tensor_scalar(out=s_G[:, 14:15], in0=s_G[:, 13:14],
                                          scalar1=2.0, scalar2=None, op0=ALU.is_ge),
                f"{tg}gm{r}")
            VOP(lambda q: q.tensor_tensor(out=s_G[:, 15:16], in0=s_G[:, 11:12],
                                          in1=s_G[:, 12:13], op=ALU.subtract), f"{tg}gn{r}")
            VOP(lambda q: q.tensor_tensor(out=s_G[:, 15:16], in0=s_G[:, 15:16],
                                          in1=s_G[:, 14:15], op=ALU.mult), f"{tg}go{r}")
            VOP(lambda q: q.tensor_tensor(out=s_G[:, 12:13], in0=s_G[:, 12:13],
                                          in1=s_G[:, 15:16], op=ALU.add), f"{tg}gp{r}")
            VOP(lambda q: q.tensor_scalar(out=s_G[:, 24:32], in0=s_G[:, 16:24],
                                          scalar1=s_G[:, 12:13], scalar2=None,
                                          op0=ALU.is_ge), f"{tg}gq{r}")
            VOP(lambda q: q.tensor_tensor(out=s_G[:, 8:9], in0=s_G[:, 11:12],
                                          in1=s_G[:, 12:13], op=ALU.add), f"{tg}gr{r}")
            VOP(lambda q: q.reciprocal(s_G[:, 9:10], s_G[:, 8:9]), f"{tg}gs{r}")
            VOP(lambda q: q.tensor_scalar(out=s_G[:, 9:10], in0=s_G[:, 9:10],
                                          scalar1=0.5, scalar2=None, op0=ALU.mult),
                f"{tg}gt{r}")
            VOP(lambda q, r=r: q.tensor_tensor(out=s_gates[:, r * 8:(r + 1) * 8],
                                               in0=s_G[:, 16:24], in1=s_G[:, 24:32],
                                               op=ALU.mult), f"{tg}gu{r}")
            VOP(lambda q, r=r: q.tensor_scalar(out=s_gates[:, r * 8:(r + 1) * 8],
                                               in0=s_gates[:, r * 8:(r + 1) * 8],
                                               scalar1=s_G[:, 9:10], scalar2=None,
                                               op0=ALU.mult), f"{tg}gv2{r}")

        dump(f"gates{call}", s_gates[:, :], [("gate", r) for r in range(NT)])

        def h1g_ap(m, c0, c1):
            if m < 16:
                return s_A[:, m * NTOK + c0:m * NTOK + c1]
            return s_Bf[:, (m - 16) * NTOK + c0:(m - 16) * NTOK + c1]

        def h1g_keys(m, c0, c1):
            if m < 16:
                return rkeys("A", m * NTOK + c0, m * NTOK + c1)
            return rkeys("B", (m - 16) * NTOK + c0, (m - 16) * NTOK + c1)

        h1keys = [("A", i) for i in range(16)] + [("B", i) for i in range(16)]

        for e in range(NE):
            for mg in range(16):
                eh = mg % 2
                stream_win(f"w1_{e}_{mg}",
                           lambda i, e=e, mg=mg: moe_w1[e * D + i * 256:e * D + (i + 1) * 256,
                                                        mg * 256:(mg + 1) * 256].rearrange(
                               "(k p) f -> p k f", p=P),
                           KD // 2, 512, s_E[:, eh * 4096:(eh + 1) * 4096],
                           rkeys("E", eh * 4096, (eh + 1) * 4096), f"{tg}w1_{e}_{mg}")
                eke = rkeys("E", eh * 4096, (eh + 1) * 4096)
                for m in (2 * mg, 2 * mg + 1):
                    for half in range(2):
                        bank = next_bank()
                        def f_h1(q, m=m, half=half, bank=bank, eh=eh):
                            for kk in range(KD):
                                mm = q.matmul(
                                    pb[bank][:, 0:512],
                                    s_E[:, eh * 4096 + kk * 256 + (m % 2) * 128:
                                         eh * 4096 + kk * 256 + (m % 2) * 128 + 128],
                                    s_C[:, kk * NTOK + half * 512:kk * NTOK + (half + 1) * 512],
                                    start=(kk == 0), stop=(kk == KD - 1))
                            return mm
                        OP("tensor", f_h1, ckeys_all + eke, [pkey(bank)],
                           tag=f"{tg}h1_{e}_{m}_{half}")
                        OP("scalar", lambda q, m=m, half=half, bank=bank: q.activation(
                            out=h1g_ap(m, half * 512, (half + 1) * 512),
                            in_=pb[bank][:, 0:512], func=AF.Gelu),
                           [pkey(bank)], h1g_keys(m, half * 512, (half + 1) * 512),
                           tag=f"{tg}ge_{e}_{m}_{half}")

            for ow in range(16):
                eh = ow % 2
                stream_win(f"w2_{e}_{ow}",
                           lambda i, e=e, ow=ow: moe_w2[e * FFN + i * 512:e * FFN + (i + 1) * 512,
                                                        ow * 128:(ow + 1) * 128].rearrange(
                               "(k p) f -> p k f", p=P),
                           8, 512, s_E[:, eh * 4096:(eh + 1) * 4096],
                           rkeys("E", eh * 4096, (eh + 1) * 4096), f"{tg}w2_{e}_{ow}")
                wke = rkeys("E", eh * 4096, (eh + 1) * 4096)
                for r in range(NT):
                    bank = next_bank()
                    def f_y(q, r=r, bank=bank, eh=eh):
                        for k2 in range(32):
                            mm = q.matmul(
                                pb[bank][:, 0:128],
                                h1g_ap(k2, r * P, (r + 1) * P),
                                s_E[:, eh * 4096 + k2 * 128:eh * 4096 + (k2 + 1) * 128],
                                start=(k2 == 0), stop=(k2 == 31))
                        return mm
                    OP("tensor", f_y, h1keys + wke, [pkey(bank)], tag=f"{tg}y_{e}_{ow}_{r}")
                    trot = 256 + ((ow * NT + r) % 2) * 128
                    OP("scalar", lambda q, e=e, r=r, bank=bank, trot=trot: q.activation(
                        out=s_G[:, trot:trot + 128], in_=pb[bank][:, 0:128], func=AF.Copy,
                        scale=s_gates[:, r * 8 + e:r * 8 + e + 1]),
                       [pkey(bank), ("gate", r)] + gkeys(trot, trot + 128),
                       gkeys(trot, trot + 128), tag=f"{tg}gv_{e}_{ow}_{r}")
                    xc = r * D + ow * 128
                    OP("gpsimd", lambda q, xc=xc, r=r, trot=trot: q.tensor_tensor(
                        out=s_x[:, xc:xc + 128], in0=s_x[:, xc:xc + 128],
                        in1=s_G[:, trot:trot + 128], op=ALU.add),
                       gkeys(trot, trot + 128) + [("x", r)], [("x", r)],
                       tag=f"{tg}xa_{e}_{ow}_{r}")

    def deltanet():
        tg = "dn"
        akeys_all = [("A", i) for i in range(16)]

        for r in range(NT):
            so = (r % 2) * 32
            ln_stats(s_x[:, r * D:(r + 1) * D], [("x", r)], so, tag=f"{tg}l{r}")
            OP("vector", lambda q, r=r, so=so: q.tensor_scalar(
                out=s_D[:, 0:D], in0=s_x[:, r * D:(r + 1) * D],
                scalar1=s_stats[:, so:so + 1], scalar2=s_stats[:, so + 1:so + 2],
                op0=ALU.subtract, op1=ALU.mult),
               [stk(so), stk(so + 1), ("x", r)], rkeys("D", 0, D), big=True, tag=f"{tg}a{r}")
            for f in range(KD):
                pslot, pc = next_tr()
                pk = pkey(pslot, pc)
                OP("tensor", lambda q, f=f, pslot=pslot, pc=pc:
                   q.transpose(pslot[:, pc:pc + P], s_D[:, f * P:(f + 1) * P], s_id[:]),
                   rkeys("D", 0, D) + [("id",)], [pk], tag=f"{tg}t{r}_{f}")
                OP("vector", lambda q, r=r, f=f, pslot=pslot, pc=pc:
                   q.tensor_copy(s_A[:, f * NTOK + r * P:f * NTOK + (r + 1) * P],
                                 pslot[:, pc:pc + P]),
                   [pk], [("A", (f * NTOK + r * P) // 1024)], tag=f"{tg}e{r}_{f}")

        dump("lnd", s_A[:, :], akeys_all)

        abk = [("ab", i) for i in range(KD)]
        for r in range(NT):
            bank = next_bank(4, 6)
            def f_ab(q, r=r, bank=bank):
                for kk in range(KD):
                    q.matmul(pb[bank][:, 0:32],
                             s_A[:, kk * NTOK + r * P:kk * NTOK + (r + 1) * P],
                             s_ab[:, kk * 32:(kk + 1) * 32],
                             start=(kk == 0), stop=False)
                return q.matmul(pb[bank][:, 0:32], s_onesrow[:],
                                s_brow[0:1, 0:32], start=False, stop=True)
            OP("tensor", f_ab, akeys_all + abk + [("onesrow",), ("brow", 0)], [pkey(bank)],
               tag=f"{tg}ab{r}")

            OP("scalar", lambda q, bank=bank: q.activation(
                out=s_G[:, 0:16], in_=pb[bank][:, 0:16], func=AF.Sigmoid),
               [pkey(bank)] + GK, GK, tag=f"{tg}sga{r}")
            OP("scalar", lambda q, r=r: q.activation(
                out=s_la[:, r * 16:(r + 1) * 16], in_=s_G[:, 0:16],
                func=AF.Ln, bias=s_eps2[:]),
               [("eps",)] + GK, GK + [("la", r)], tag=f"{tg}sgb{r}")

            OP("vector", lambda q, r=r, bank=bank: q.tensor_scalar(
                out=s_G[:, 16:32], in0=pb[bank][:, 16:32], scalar1=s_zero[:], scalar2=None,
                op0=ALU.max),
               [pkey(bank), ("eps",)] + GK, GK, tag=f"{tg}s1{r}")

            OP("scalar", lambda q, bank=bank: q.activation(
                out=s_G[:, 32:48], in_=pb[bank][:, 16:32], func=AF.Abs),
               [pkey(bank)] + GK, GK, tag=f"{tg}s2a{r}")
            OP("scalar", lambda q: q.activation(
                out=s_G[:, 32:48], in_=s_G[:, 32:48], func=AF.Exp, scale=-1.0),
               GK, GK, tag=f"{tg}s2b{r}")
            OP("scalar", lambda q: q.activation(
                out=s_G[:, 32:48], in_=s_G[:, 32:48], func=AF.Ln, bias=s_one[:]),
               [("eps",)] + GK, GK, tag=f"{tg}s2c{r}")
            OP("vector", lambda q, r=r: q.tensor_tensor(
                out=s_beta[:, r * 16:(r + 1) * 16], in0=s_G[:, 16:32], in1=s_G[:, 32:48],
                op=ALU.add),
               GK, [("beta", r)], tag=f"{tg}bt{r}")

            bank2 = next_bank(4, 6)
            OP("tensor", lambda q, r=r, bank2=bank2: q.matmul(
                pb[bank2][:, 0:16], s_triu[:], s_la[:, r * 16:(r + 1) * 16],
                start=True, stop=True),
               [("triu",), ("la", r)], [pkey(bank2)], tag=f"{tg}cl{r}")

            def f_ce(q, r=r, bank2=bank2):
                q.tensor_copy(s_cla[:, r * 16:(r + 1) * 16], pb[bank2][:, 0:16])
                return q.tensor_scalar(out=s_ncla[:, r * 16:(r + 1) * 16],
                                       in0=pb[bank2][:, 0:16],
                                       scalar1=-1.0, scalar2=None, op0=ALU.mult)
            OP("vector", f_ce, [pkey(bank2)], [("cla", r)], tag=f"{tg}ne{r}")

        for hg in range(8):
            h0 = 2 * hg
            stream_win(None, lambda i, hg=hg: wq_d[i * 256:(i + 1) * 256,
                                                   hg * 256:(hg + 1) * 256].rearrange(
                           "(k p) f -> p k f", p=P),
                       KD // 2, 512, s_E[:, 0:4096], rkeys("E", 0, 4096), f"{tg}wq{hg}")
            eke = rkeys("E", 0, 4096)
            for hp in range(2):
                h = h0 + hp
                for half in range(2):
                    bank = next_bank()
                    def f_qp(q, hp=hp, half=half, bank=bank):
                        for kk in range(KD):
                            mm = q.matmul(
                                pb[bank][:, 0:512],
                                s_E[:, kk * 256 + hp * 128:kk * 256 + (hp + 1) * 128],
                                s_A[:, kk * NTOK + half * 512:kk * NTOK + (half + 1) * 512],
                                start=(kk == 0), stop=(kk == KD - 1))
                        return mm
                    OP("tensor", f_qp, akeys_all + eke, [pkey(bank)], tag=f"{tg}qp{h}_{half}")
                    OP("vector", lambda q, h=h, half=half, bank=bank: q.tensor_scalar(
                        out=s_Bf[:, h * NTOK + half * 512:h * NTOK + (half + 1) * 512],
                        in0=pb[bank][:, 0:512], scalar1=s_bq[:, h:h + 1], scalar2=None,
                        op0=ALU.add),
                       [pkey(bank), ("bq",)], [("B", (h * NTOK + half * 512) // 1024)],
                       tag=f"{tg}qe{h}_{half}")

            DMA("sync", s_G[0:1, 0:256],
                bk_d.rearrange("(one d) -> one d", one=1)[0:1, hg * 256:(hg + 1) * 256], GK, GK,
                tag=f"{tg}bkl{hg}")
            OP("vector", lambda q: q.tensor_copy(s_brow[0:1, 32:288], s_G[0:1, 0:256]),
               GK, [("brow", 1)], tag=f"{tg}bkc{hg}")
            DMA("sync", s_G[0:1, 0:256],
                bv_d.rearrange("(one d) -> one d", one=1)[0:1, hg * 256:(hg + 1) * 256], GK, GK,
                tag=f"{tg}bvl{hg}")
            OP("vector", lambda q: q.tensor_copy(s_brow[0:1, 288:544], s_G[0:1, 0:256]),
               GK, [("brow", 2)], tag=f"{tg}bvc{hg}")

            stream_win(None, lambda i, hg=hg: wk_d[i * 256:(i + 1) * 256,
                                                   hg * 256:(hg + 1) * 256].rearrange(
                           "(k p) f -> p k f", p=P),
                       KD // 2, 512, s_E[:, 4096:8192], rkeys("E", 4096, 8192), f"{tg}wk{hg}")
            eke2 = rkeys("E", 4096, 8192)
            for r in range(NT):
                bank = next_bank()
                def f_kp(q, r=r, bank=bank):
                    for kk in range(KD):
                        q.matmul(pb[bank][:, 0:256],
                                 s_A[:, kk * NTOK + r * P:kk * NTOK + (r + 1) * P],
                                 s_E[:, 4096 + kk * 256:4096 + (kk + 1) * 256],
                                 start=(kk == 0), stop=False)
                    return q.matmul(pb[bank][:, 0:256], s_onesrow[:], s_brow[0:1, 32:288],
                                    start=False, stop=True)
                OP("tensor", f_kp, akeys_all + eke2 + [("onesrow",), ("brow", 1)], [pkey(bank)],
                   tag=f"{tg}kp{r}")
                OP("scalar", lambda q, r=r, bank=bank: q.activation(
                    out=s_D[:, r * 256:(r + 1) * 256], in_=pb[bank][:, 0:256], func=AF.Copy),
                   [pkey(bank)], rkeys("D", r * 256, (r + 1) * 256), tag=f"{tg}kc{r}")

                OP("vector", lambda q, r=r: q.tensor_tensor(
                    out=s_D[:, 6144:6400].rearrange("p (h d) -> p h d", h=2),
                    in0=s_D[:, r * 256:(r + 1) * 256].rearrange("p (h d) -> p h d", h=2),
                    in1=s_D[:, r * 256:(r + 1) * 256].rearrange("p (h d) -> p h d", h=2),
                    op=ALU.mult),
                   rkeys("D", r * 256, (r + 1) * 256), [("D", 6)], tag=f"{tg}n1a{r}")
                OP("vector", lambda q: q.tensor_reduce(
                    out=s_stats[:, 64:66],
                    in_=s_D[:, 6144:6400].rearrange("p (h d) -> p h d", h=2),
                    axis=AX.X, op=ALU.add),
                   [("D", 6)], [stk(64)], tag=f"{tg}n1b{r}")
                OP("vector", lambda q: q.tensor_scalar(
                    out=s_stats[:, 64:66], in0=s_stats[:, 64:66],
                    scalar1=s_eps3[:], scalar2=None, op0=ALU.max),
                   [("eps",), stk(64)], [stk(64)], tag=f"{tg}n1c{r}")

                OP("scalar", lambda q: q.activation(
                    out=s_stats[:, 80:82], in_=s_stats[:, 64:66], func=AF.Ln),
                   [stk(64)], [stk(80)], tag=f"{tg}n2a{r}")
                OP("scalar", lambda q: q.activation(
                    out=s_stats[:, 80:82], in_=s_stats[:, 80:82], func=AF.Exp, scale=-0.5),
                   [stk(80)], [stk(80)], tag=f"{tg}n2b{r}")

                OP("vector", lambda q, r=r, hg=hg: q.tensor_tensor(
                    out=s_stats[:, 82:84], in0=s_stats[:, 80:82],
                    in1=s_beta[:, r * 16 + 2 * hg:r * 16 + 2 * hg + 2], op=ALU.mult),
                   [stk(80), ("beta", r)], [stk(82)], tag=f"{tg}n3a{r}")

                def f_kn3b(q, r=r):
                    for j in range(2):
                        last = q.tensor_scalar(
                            out=s_D[:, r * 256 + j * 128:r * 256 + (j + 1) * 128],
                            in0=s_D[:, r * 256 + j * 128:r * 256 + (j + 1) * 128],
                            scalar1=s_stats[:, 82 + j:83 + j], scalar2=None, op0=ALU.mult)
                    return last
                OP("vector", f_kn3b, [stk(82)] + rkeys("D", r * 256, (r + 1) * 256),
                   rkeys("D", r * 256, (r + 1) * 256), tag=f"{tg}n3b{r}")

                for j in range(2):
                    pslot, pc = next_tr()
                    pk = pkey(pslot, pc)
                    OP("tensor", lambda q, r=r, j=j, pslot=pslot, pc=pc: q.transpose(
                        pslot[:, pc:pc + P], s_D[:, r * 256 + j * 128:r * 256 + (j + 1) * 128],
                        s_id[:]),
                       rkeys("D", r * 256, (r + 1) * 256) + [("id",)], [pk],
                       tag=f"{tg}bt{r}_{j}")
                    OP("vector", lambda q, r=r, j=j, pslot=pslot, pc=pc: q.tensor_copy(
                        s_D[:, 2048 + j * 1024 + r * P:2048 + j * 1024 + (r + 1) * P],
                        pslot[:, pc:pc + P]),
                       [pk], [("D", (2048 + j * 1024 + r * P) // 1024)], tag=f"{tg}be{r}_{j}")

            stream_win(None, lambda i, hg=hg: wv_d[i * 256:(i + 1) * 256,
                                                   hg * 256:(hg + 1) * 256].rearrange(
                           "(k p) f -> p k f", p=P),
                       KD // 2, 512, s_E[:, 0:4096], rkeys("E", 0, 4096), f"{tg}wv{hg}")
            for r in range(NT):
                bank = next_bank()
                def f_vp(q, r=r, bank=bank):
                    for kk in range(KD):
                        q.matmul(pb[bank][:, 0:256],
                                 s_A[:, kk * NTOK + r * P:kk * NTOK + (r + 1) * P],
                                 s_E[:, kk * 256:(kk + 1) * 256],
                                 start=(kk == 0), stop=False)
                    return q.matmul(pb[bank][:, 0:256], s_onesrow[:], s_brow[0:1, 288:544],
                                    start=False, stop=True)
                OP("tensor", f_vp, akeys_all + eke + [("onesrow",), ("brow", 2)], [pkey(bank)],
                   tag=f"{tg}vp{r}")
                OP("scalar", lambda q, r=r, bank=bank: q.activation(
                    out=s_D[:, 4096 + r * 256:4096 + (r + 1) * 256], in_=pb[bank][:, 0:256],
                    func=AF.Copy),
                   [pkey(bank)], rkeys("D", 4096 + r * 256, 4096 + (r + 1) * 256),
                   tag=f"{tg}ve{r}")

            for r in range(NT):
                c01 = r & 1
                for j in range(2):
                    h = h0 + j
                    pair = (r >> 1) * 16 + h
                    cbank = 4 if (r * 2 + j) % 2 == 0 else 5
                    abank = 9 - cbank
                    lacol = s_la[:, r * 16 + h:r * 16 + h + 1]
                    nclacol = s_ncla[:, r * 16 + h:r * 16 + h + 1]
                    OP("vector", lambda q, lacol=lacol: q.tensor_scalar(
                        out=s_G[:, 0:128], in0=s_triu[:], scalar1=lacol, scalar2=None,
                        op0=ALU.mult),
                       [("triu",), ("la", r)] + GK, GK, tag=f"{tg}c1_{r}_{h}")
                    OP("tensor", lambda q, cbank=cbank: q.matmul(
                        pb[cbank][:, 0:128], s_onesf[:], s_G[:, 0:128], start=True, stop=True),
                       [("onesf",)] + GK, [pkey(cbank)], tag=f"{tg}c2_{r}_{h}")
                    OP("vector", lambda q, cbank=cbank: q.tensor_tensor(
                        out=s_G[:, 128:256], in0=pb[cbank][:, 0:128], in1=s_mst[:], op=ALU.add),
                       [pkey(cbank), ("mst",)] + GK, GK, tag=f"{tg}c3_{r}_{h}")

                    def f_dt(q, nclacol=nclacol, pair=pair, c01=c01):
                        q.activation(out=s_G[:, 256:384], in_=s_G[:, 128:256],
                                     func=AF.Exp, bias=nclacol)
                        if c01 == 0:
                            q.activation(out=s_td0[:, pair:pair + 1],
                                         in_=s_G[:, 255:256], func=AF.Exp)
                        q.activation(out=s_stats[:, 96:97], in_=s_G[:, 255:256],
                                     func=AF.Exp, bias=nclacol)
                        return q.activation(out=s_ces[:, pair * 2 + c01:pair * 2 + c01 + 1],
                                            in_=s_G[:, 255:256], func=AF.Copy)
                    OP("scalar", f_dt, [("cla", r)] + GK,
                       GK + [("td0", pair), stk(96), ("ce", pair * 2 + c01)],
                       tag=f"{tg}dt_{r}_{h}")
                    OP("tensor", lambda q, j=j, h=h, r=r, abank=abank: q.matmul(
                        pb[abank][:, 0:128],
                        s_D[:, 2048 + j * 1024 + r * P:2048 + j * 1024 + (r + 1) * P],
                        s_Bf[:, h * NTOK + r * P:h * NTOK + (r + 1) * P],
                        start=True, stop=True),
                       [("D", (2048 + j * 1024 + r * P) // 1024),
                        ("B", (h * NTOK + r * P) // 1024)], [pkey(abank)],
                       tag=f"{tg}at_{r}_{h}")
                    OP("vector", lambda q, abank=abank: q.tensor_tensor(
                        out=s_D[:, 6144:6272], in0=pb[abank][:, 0:128],
                        in1=s_G[:, 256:384], op=ALU.mult),
                       [pkey(abank)] + GK, [("D", 6)], tag=f"{tg}am_{r}_{h}")
                    OP("vector", lambda q, r=r, j=j: q.tensor_scalar(
                        out=s_D[:, 6272:6400],
                        in0=s_D[:, r * 256 + j * 128:r * 256 + (j + 1) * 128],
                        scalar1=s_stats[:, 96:97], scalar2=None, op0=ALU.mult),
                       [stk(96)] + rkeys("D", r * 256, (r + 1) * 256), [("D", 6)],
                       tag=f"{tg}kw_{r}_{h}")
                    OP("tensor", lambda q, r=r, j=j, cbank=cbank: q.matmul(
                        pb[cbank][:, 0:128], s_D[:, 6144:6272],
                        s_D[:, 4096 + r * 256 + j * 128:4096 + r * 256 + (j + 1) * 128],
                        start=True, stop=True),
                       [("D", 6), ("D", (4096 + r * 256 + j * 128) // 1024)], [pkey(cbank)],
                       tag=f"{tg}yl_{r}_{h}")
                    xc = r * D + h * HD
                    OP("vector", lambda q, xc=xc, cbank=cbank, r=r: q.scalar_tensor_tensor(
                        out=s_x[:, xc:xc + HD], in0=pb[cbank][:, 0:128], scalar=1.0,
                        in1=s_x[:, xc:xc + HD], op0=ALU.mult, op1=ALU.add),
                       [pkey(cbank), ("x", r)], [("x", r)], tag=f"{tg}yx_{r}_{h}")
                    OP("tensor", lambda q, r=r, j=j, abank=abank: q.matmul(
                        pb[abank][:, 0:128], s_D[:, 6272:6400],
                        s_D[:, 4096 + r * 256 + j * 128:4096 + r * 256 + (j + 1) * 128],
                        start=True, stop=True),
                       [("D", 6), ("D", (4096 + r * 256 + j * 128) // 1024)], [pkey(abank)],
                       tag=f"{tg}dl_{r}_{h}")
                    if c01 == 0:
                        def f_st0(q, pair=pair, abank=abank):
                            q.tensor_copy(s_C[:, pair * P:(pair + 1) * P], pb[abank][:, 0:128])
                            return q.tensor_copy(s_C[:, 8192 + pair * P:8192 + (pair + 1) * P],
                                                 pb[abank][:, 0:128])
                        OP("vector", f_st0, [pkey(abank)],
                           rkeys("C", pair * P, (pair + 1) * P) +
                           rkeys("C", 8192 + pair * P, 8192 + (pair + 1) * P) +
                           [("dT0", pair), ("St", pair)], tag=f"{tg}s0_{r}_{h}")
                    else:
                        OP("scalar", lambda q, pair=pair: q.activation(
                            out=s_stats[:, 97:98], in_=s_ces[:, pair * 2 + 1:pair * 2 + 2],
                            func=AF.Exp),
                           [("ce", pair * 2 + 1)], [stk(97)], tag=f"{tg}t1_{r}_{h}")

                        OP("vector", lambda q, pair=pair: q.tensor_scalar(
                            out=s_D[:, 6400:6528],
                            in0=s_C[:, 8192 + pair * P:8192 + (pair + 1) * P],
                            scalar1=s_stats[:, 97:98], scalar2=None, op0=ALU.mult),
                           [("St", pair), stk(97)] +
                           rkeys("C", 8192 + pair * P, 8192 + (pair + 1) * P),
                           [("D", 6)], tag=f"{tg}s1a_{r}_{h}")
                        OP("vector", lambda q, pair=pair, abank=abank: q.tensor_tensor(
                            out=s_C[:, 8192 + pair * P:8192 + (pair + 1) * P],
                            in0=pb[abank][:, 0:128], in1=s_D[:, 6400:6528], op=ALU.add),
                           [pkey(abank), ("D", 6)] +
                           rkeys("C", 8192 + pair * P, 8192 + (pair + 1) * P),
                           rkeys("C", 8192 + pair * P, 8192 + (pair + 1) * P) +
                           [("St", pair)], tag=f"{tg}s1b_{r}_{h}")
                        OP("vector", lambda q, pair=pair: q.tensor_tensor(
                            out=s_csum[:, pair:pair + 1], in0=s_ces[:, pair * 2:pair * 2 + 1],
                            in1=s_ces[:, pair * 2 + 1:pair * 2 + 2], op=ALU.add),
                           [("ce", pair * 2), ("ce", pair * 2 + 1)], [("cs", pair)],
                           tag=f"{tg}cs_{r}_{h}")

        dump("dnstloc", s_C[:, 8192:16384], [("St", pair) for pair in range(64)])
        dump("dncsum", s_csum[:, 0:64], [("cs", pair) for pair in range(64)])

        stw = [("St", pair) for pair in range(64)]
        DMA("gpsimd", st_exp_h.ap().rearrange("(pair u) d -> u pair d", u=P),
            s_C[:, 8192:16384].rearrange("u (pair d) -> u pair d", d=HD),
            stw + rkeys("C", 8192, 16384), [("stexp",)], tag="stexp")
        csw = [("cs", pair) for pair in range(64)]
        DMA("gpsimd", cs_exp_h.ap().rearrange("(one pair) one2 -> one pair one2", one=1),
            s_csum[0:1, 0:64].rearrange("one (pair one2) -> one pair one2", one2=1),
            csw, [("csexp",)], tag="csexp")
        OP("gpsimd", lambda q: q.collective_compute(
            "AllGather", ALU.bypass, ins=[st_exp_h.ap().opt()], outs=[st_gath_h.ap().opt()],
            replica_groups=[list(range(NCORE))]),
           [("stexp",)], [("stg",)], kind="cc", tag="ccst")
        OP("gpsimd", lambda q: q.collective_compute(
            "AllGather", ALU.bypass, ins=[cs_exp_h.ap().opt()], outs=[cs_gath_h.ap().opt()],
            replica_groups=[list(range(NCORE))]),
           [("csexp",)], [("csgat",)], kind="cc", tag="cccs")

        DMA("gpsimd", s_csg[:].rearrange("p (r one) -> p r one", one=1),
            cs_gath_h.ap().rearrange("(r p) one -> p r one", p=64),
            [("csgat",)], [("csgs",)], tag="csgl", slow=True)

        for rr in range(8):
            OP("vector", lambda q, rr=rr: q.tensor_tensor(
                out=s_G[0:64, 0:8], in0=s_csg[:],
                in1=s_m2exp[:, rr * 8:(rr + 1) * 8], op=ALU.mult),
               [("csgs",), ("m2exp",)] + GK, GK, tag=f"dnLa{rr}")
            OP("vector", lambda q, rr=rr: q.tensor_reduce(
                out=s_L[:, rr:rr + 1], in_=s_G[0:64, 0:8], axis=AX.X, op=ALU.add),
               GK, GK + [("L",)], tag=f"dnLb{rr}")
        OP("scalar", lambda q: q.activation(out=s_scl[:], in_=s_L[:], func=AF.Exp),
           [("L",)], [("scl",)], tag="dnsE")
        OP("vector", lambda q: q.tensor_tensor(out=s_scl[:], in0=s_scl[:], in1=s_mask1[:],
                                               op=ALU.mult),
           [("scl",), ("mask1",)], [("scl",)], tag="dnsM")
        DMA("gpsimd", sclT_dram.ap(), s_scl[:], [("scl",)], [("sclTd",)], tag="sclTd")
        DMA("gpsimd", s_sclT[:], sclT_dram.ap().rearrange("p r -> r p"),
            [("sclTd",)], [("sclT",)], tag="sclTl", slow=True)

        def f_bc(q):
            for rr in range(8):
                last = q.tensor_scalar(out=s_G[0:8, rr * 64:(rr + 1) * 64],
                                       in0=s_sclT[:], scalar1=s_id8[:, rr:rr + 1],
                                       scalar2=None, op0=ALU.mult)
            return last
        OP("vector", f_bc, [("sclT",), ("id8",)] + GK, GK + [("bcm",)], tag="dnbcm")
        for rr in range(8):
            OP("tensor", lambda q, rr=rr: q.matmul(
                pb[4][:, rr * 64:(rr + 1) * 64], s_onesf[0:8, :],
                s_G[0:8, rr * 64:(rr + 1) * 64], start=True, stop=True),
               [("onesf",), ("bcm",)] + GK, [pkey(4)], tag=f"dnbc{rr}")

        for blk in range(16):
            for rr in range(8):
                gb = blk % 2
                DMA("gpsimd",
                    s_A[:, gb * 1024:gb * 1024 + 512].rearrange("u (pair d) -> u pair d", d=HD),
                    st_gath_h.ap()[rr * 8192 + blk * 512:rr * 8192 + (blk + 1) * 512, :]
                    .rearrange("(pair u) d -> u pair d", u=P),
                    [("stg",), ("A", gb)], [("A", gb)], tag=f"stgl{blk}_{rr}")
                for pp in range(4):
                    pair = blk * 4 + pp
                    acc = s_G[:, pp * 128:(pp + 1) * 128]
                    if rr == 0:
                        OP("vector", lambda q, acc=acc, gb=gb, pp=pp, rr=rr, pair=pair:
                           q.tensor_scalar(
                               out=acc,
                               in0=s_A[:, gb * 1024 + pp * 128:gb * 1024 + (pp + 1) * 128],
                               scalar1=pb[4][:, rr * 64 + pair:rr * 64 + pair + 1],
                               scalar2=None, op0=ALU.mult),
                           [("A", gb), pkey(4)] + GK, GK + [("acc", pp)],
                           tag=f"cb{blk}_{rr}_{pp}")
                    else:
                        OP("vector", lambda q, acc=acc, gb=gb, pp=pp, rr=rr, pair=pair:
                           q.scalar_tensor_tensor(
                               out=acc,
                               in0=s_A[:, gb * 1024 + pp * 128:gb * 1024 + (pp + 1) * 128],
                               scalar=pb[4][:, rr * 64 + pair:rr * 64 + pair + 1],
                               in1=acc, op0=ALU.mult, op1=ALU.add),
                           [("A", gb), pkey(4), ("acc", pp)] + GK, GK + [("acc", pp)],
                           tag=f"cb{blk}_{rr}_{pp}")
            for pp in range(4):
                pair = blk * 4 + pp
                OP("vector", lambda q, pp=pp, pair=pair: q.tensor_copy(
                    s_C[:, 8192 + pair * P:8192 + (pair + 1) * P],
                    s_G[:, pp * 128:(pp + 1) * 128]),
                   [("acc", pp)] + GK,
                   rkeys("C", 8192 + pair * P, 8192 + (pair + 1) * P) + [("Stin", pair)],
                   tag=f"cbc{blk}_{pp}")

        dump("dnstin", s_C[:, 8192:16384], [("Stin", pair) for pair in range(64)])
        dump("dnscl", s_scl[:, 0:8], [("scl",)])

        for r in range(NT):
            c01 = r & 1
            for h in range(NH):
                pair = (r >> 1) * 16 + h
                bank = next_bank()
                if c01 == 0:
                    OP("tensor", lambda q, h=h, r=r, bank=bank, pair=pair: q.matmul(
                        pb[bank][:, 0:128],
                        s_Bf[:, h * NTOK + r * P:h * NTOK + (r + 1) * P],
                        s_C[:, 8192 + pair * P:8192 + (pair + 1) * P],
                        start=True, stop=True),
                       [("B", (h * NTOK + r * P) // 1024), ("Stin", pair)] +
                       rkeys("C", 8192 + pair * P, 8192 + (pair + 1) * P),
                       [pkey(bank)], tag=f"{tg}yg_{r}_{h}")
                else:
                    OP("vector", lambda q, pair=pair: q.tensor_scalar(
                        out=s_D[:, 6400:6528],
                        in0=s_C[:, 8192 + pair * P:8192 + (pair + 1) * P],
                        scalar1=s_td0[:, pair:pair + 1], scalar2=None, op0=ALU.mult),
                       [("Stin", pair), ("td0", pair)] +
                       rkeys("C", 8192 + pair * P, 8192 + (pair + 1) * P),
                       [("D", 6)], tag=f"{tg}sTa_{r}_{h}")
                    OP("vector", lambda q, pair=pair: q.tensor_tensor(
                        out=s_D[:, 6400:6528], in0=s_D[:, 6400:6528],
                        in1=s_C[:, pair * P:(pair + 1) * P], op=ALU.add),
                       [("dT0", pair), ("D", 6)] + rkeys("C", pair * P, (pair + 1) * P),
                       [("D", 6)], tag=f"{tg}sTb_{r}_{h}")
                    OP("tensor", lambda q, h=h, r=r, bank=bank: q.matmul(
                        pb[bank][:, 0:128],
                        s_Bf[:, h * NTOK + r * P:h * NTOK + (r + 1) * P],
                        s_D[:, 6400:6528], start=True, stop=True),
                       [("B", (h * NTOK + r * P) // 1024), ("D", 6)], [pkey(bank)],
                       tag=f"{tg}yg_{r}_{h}")
                OP("scalar", lambda q, r=r, h=h: q.activation(
                    out=s_stats[:, 98:99], in_=s_cla[:, r * 16 + h:r * 16 + h + 1], func=AF.Exp),
                   [("cla", r)], [stk(98)], tag=f"{tg}cd_{r}_{h}")
                xc = r * D + h * HD
                OP("vector", lambda q, xc=xc, bank=bank, r=r: q.scalar_tensor_tensor(
                    out=s_x[:, xc:xc + HD], in0=pb[bank][:, 0:128],
                    scalar=s_stats[:, 98:99], in1=s_x[:, xc:xc + HD],
                    op0=ALU.mult, op1=ALU.add),
                   [pkey(bank), stk(98), ("x", r)], [("x", r)], tag=f"{tg}gx_{r}_{h}")

    if "ffn0" in stages:
        ffn_call(0)
    if "dn" in stages:
        deltanet()
    if "ffn1" in stages:
        ffn_call(1)

    for r in range(NT):
        DMA("sync", y_out[r * P:(r + 1) * P, :], s_x[:, r * D:(r + 1) * D],
            [("x", r)], [("yst", r)], tag=f"st{r}")
    OP("sync", lambda q: None, [("yst", r) for r in range(NT)], [], tag="fin")

    B.resolve(csem, dsem, ccsem)
    with es, nc.Block() as block:
        B.emit(block)
    return nc


STAGES = ("ffn0", "dn", "ffn1")


def kernel(**inputs):
    x = np.asarray(inputs["L0_attn_x"])
    consts = _static_consts()
    base = {
        "qkv_wt": np.ascontiguousarray(np.asarray(inputs["L0_ffn_qkv_weight"]).T),
        "proj_wt": np.ascontiguousarray(np.asarray(inputs["L0_ffn_proj_weight"]).T),
        "router_wt": np.ascontiguousarray(np.asarray(inputs["L0_ffn_router_weight"]).T),
        "ln1_g": np.asarray(inputs["L0_ffn_ln1_weight"]),
        "ln1_b": np.asarray(inputs["L0_ffn_ln1_bias"]),
        "ln2_g": np.asarray(inputs["L0_ffn_ln2_weight"]),
        "ln2_b": np.asarray(inputs["L0_ffn_ln2_bias"]),
        "moe_w1": np.asarray(inputs["L0_ffn_moe_w1"]),
        "moe_w2": np.asarray(inputs["L0_ffn_moe_w2"]),
        "wq": np.asarray(inputs["L0_attn_Wq"]),
        "wk": np.asarray(inputs["L0_attn_Wk"]),
        "wv": np.asarray(inputs["L0_attn_Wv"]),
        "bq": np.asarray(inputs["L0_attn_bq"]),
        "bk": np.asarray(inputs["L0_attn_bk"]),
        "bv": np.asarray(inputs["L0_attn_bv"]),
        "wab": np.ascontiguousarray(np.concatenate(
            [np.asarray(inputs["L0_attn_Walpha"]), np.asarray(inputs["L0_attn_Wbeta"])], axis=1)),
        "bab": np.ascontiguousarray(np.concatenate(
            [np.asarray(inputs["L0_attn_balpha"]), np.asarray(inputs["L0_attn_bbeta"])])),
    }
    base.update(consts)

    nc = build_program(stages=STAGES)
    in_maps = []
    for c in range(NCORE):
        m2exp, mask1 = _mk_core_consts(c)
        m = dict(base)
        m["c_m2exp"] = m2exp
        m["c_mask1"] = mask1
        m["x"] = np.ascontiguousarray(x[:, c * TLOC:(c + 1) * TLOC, :].reshape(NTOK, D))
        in_maps.append(m)
    res = run_bass_kernel_spmd(nc, in_maps, core_ids=list(range(NCORE)))
    out = np.empty((NB, 2048, D), dtype=np.float32)
    for c in range(NCORE):
        out[:, c * TLOC:(c + 1) * TLOC, :] = np.asarray(
            res.results[c]["y"], np.float32).reshape(NB, TLOC, D)
    return out


# version 04
# speedup vs baseline: 1.3915x; 1.3915x over previous
import numpy as np
import ml_dtypes
import concourse.bass as bass
import concourse.mybir as mybir
from concourse.bass_utils import run_bass_kernel_spmd
from contextlib import ExitStack

F32 = mybir.dt.float32
BF16 = mybir.dt.bfloat16
AF = mybir.ActivationFunctionType
ALU = mybir.AluOpType
AX = mybir.AxisListType

P = 128
D = 2048
NB = 4
TLOC = 256
NTOK = NB * TLOC
NT = NTOK // P
KD = D // P
NH = 16
HD = 128
NKV = 4
FFN = 4096
NE = 8
NCORE = 8
EPS = 1e-5
RS = 1.0 / float(np.sqrt(128.0))

ENGS = ("sync", "tensor", "vector", "scalar", "gpsimd")


class Op:
    __slots__ = ("eng", "fn", "reads", "writes", "kind", "needs_inc", "sem",
                 "val", "waits", "idx", "tag", "deps", "raw", "big")

    def __init__(self, eng, fn, reads, writes, kind, tag, big=False):
        self.eng = eng
        self.fn = fn
        self.reads = reads
        self.writes = writes
        self.kind = kind
        self.needs_inc = kind in ("d", "cc")
        self.sem = None
        self.val = None
        self.waits = []
        self.tag = tag
        self.big = big


class Sem:
    def __init__(self, h):
        self.h = h
        self.n = 0

    def inc(self, k):
        self.n += k
        return self.n


class Builder:
    def __init__(self):
        self.ops = []
        self.last_w = {}
        self.readers = {}

    def op(self, eng, fn, reads=(), writes=(), kind="c", tag="", big=False):
        reads = list(reads)
        writes = list(writes)
        for k in [k for k in reads if k and k[0] == "ps"]:
            reads.remove(k)
            if k not in writes:
                writes.append(k)
        o = Op(eng, fn, reads, writes, kind, tag, big)
        o.idx = len(self.ops)
        raw = set()
        for r in o.reads:
            j = self.last_w.get(r)
            if j is not None:
                raw.add(j)
        other = set()
        for w in o.writes:
            j = self.last_w.get(w)
            if j is not None:
                other.add(j)
            for j2 in self.readers.get(w, ()):
                other.add(j2)
        raw.discard(o.idx)
        other.discard(o.idx)
        o.raw = raw
        o.deps = raw | other
        self.ops.append(o)
        for r in o.reads:
            self.readers.setdefault(r, []).append(o.idx)
        for w in o.writes:
            self.last_w[w] = o.idx
            self.readers[w] = []
        return o

    @staticmethod
    def _skip(d, o, is_raw):
        if d.kind != "c" or d.eng != o.eng:
            return False
        if d.eng in ("tensor", "gpsimd", "sync"):
            return True
        if not is_raw:
            return True
        return d.big

    def resolve(self, csem, dsem, ccsem):
        for o in self.ops:
            for j in o.deps:
                d = self.ops[j]
                if self._skip(d, o, j in o.raw):
                    continue
                d.needs_inc = True
        for o in self.ops:
            if not o.needs_inc:
                continue
            if o.kind == "d":
                o.sem = dsem[o.eng]
                o.val = o.sem.inc(16)
            elif o.kind == "cc":
                o.sem = ccsem
                o.val = o.sem.inc(1)
            else:
                o.sem = csem[o.eng]
                o.val = o.sem.inc(1)
        marks = {e: {} for e in ENGS}
        for o in self.ops:
            m = marks[o.eng]
            for j in sorted(o.deps):
                d = self.ops[j]
                if self._skip(d, o, j in o.raw):
                    continue
                key = id(d.sem)
                if m.get(key, -1) >= d.val:
                    continue
                m[key] = d.val
                o.waits.append((d.sem, d.val))

    def emit(self, block):
        per_eng = {e: [] for e in ENGS}
        for o in self.ops:
            per_eng[o.eng].append(o)

        def runner(ops):
            def body(q):
                for o in ops:
                    for (s, v) in o.waits:
                        q.wait_ge(s.h, v)
                    last = o.fn(q)
                    if o.needs_inc:
                        assert last is not None, f"op {o.tag} needs inc but returned None"
                        last.then_inc(o.sem.h, 16 if o.kind == "d" else 1)
            return body
        for e in ENGS:
            getattr(block, e)(runner(per_eng[e]))


def rkeys(name, c0, c1, quantum=1024):
    return [(name, i) for i in range(c0 // quantum, (c1 + quantum - 1) // quantum)]


def stk(c):
    return ("st", c // 16)


def gkeys(c0, c1):
    return [("G", i) for i in range(c0 // 256, (c1 + 255) // 256)]


_STATIC_CONSTS = None


def _static_consts():
    global _STATIC_CONSTS
    if _STATIC_CONSTS is None:
        _STATIC_CONSTS = {
            "c_ident": np.eye(P, dtype=ml_dtypes.bfloat16),
            "c_triu": np.triu(np.ones((P, P), dtype=np.float32)),
            "c_mstrict": (-1e30 * np.tril(np.ones((P, P), dtype=np.float32), -1)).astype(np.float32),
            "c_ones": np.ones((P, P), dtype=np.float32),
            "c_onesrow": np.ones((1, P), dtype=ml_dtypes.bfloat16),
            "c_id8": np.eye(8, dtype=np.float32),
        }
    return _STATIC_CONSTS


def _mk_core_consts(core):
    m2 = np.zeros((8, 8), dtype=np.float32)
    for cpp in range(8):
        for r in range(8):
            if r < cpp < core:
                m2[cpp, r] = 1.0
    m2exp = np.tile(m2.T.reshape(1, 64), (64, 1)).astype(np.float32)
    mask1 = np.zeros((64, 8), dtype=np.float32)
    mask1[:, :core] = 1.0
    return m2exp, mask1


def build_program(debug=(), stages=("ffn0", "dn", "ffn1")):
    nc = bass.Bass("TRN2", target_bir_lowering=False, debug=False)
    dbg = set(debug)
    B = Builder()

    def din(name, shape, dt=F32):
        return nc.dram_tensor(name, shape, dt, kind="ExternalInput").ap()

    x_in = din("x", [NTOK, D])
    qkv_wt = din("qkv_wt", [D, 3072])
    proj_wt = din("proj_wt", [D, D])
    router_wt = din("router_wt", [NE, D])
    ln1_g = din("ln1_g", [D]); ln1_b = din("ln1_b", [D])
    ln2_g = din("ln2_g", [D]); ln2_b = din("ln2_b", [D])
    moe_w1 = din("moe_w1", [NE, D, FFN]).rearrange("e r c -> (e r) c")
    moe_w2 = din("moe_w2", [NE, FFN, D]).rearrange("e r c -> (e r) c")
    wq_d = din("wq", [D, D]); wk_d = din("wk", [D, D]); wv_d = din("wv", [D, D])
    bq_d = din("bq", [D]); bk_d = din("bk", [D]); bv_d = din("bv", [D])
    wab_d = din("wab", [D, 32]); bab_d = din("bab", [32])
    c_ident = din("c_ident", [P, P], BF16)
    c_triu = din("c_triu", [P, P])
    c_mstrict = din("c_mstrict", [P, P])
    c_ones = din("c_ones", [P, P])
    c_onesrow = din("c_onesrow", [1, P], BF16)
    c_id8 = din("c_id8", [8, 8])
    c_m2exp = din("c_m2exp", [64, 64])
    c_mask1 = din("c_mask1", [64, 8])

    y_out = nc.dram_tensor("y", [NTOK, D], F32, kind="ExternalOutput").ap()
    st_exp_h = nc.dram_tensor("st_exp", [64 * P, HD], BF16)
    st_gath_h = nc.dram_tensor("st_gath", [NCORE * 64 * P, HD], BF16)
    cs_exp_h = nc.dram_tensor("cs_exp", [64, 1], F32)
    cs_gath_h = nc.dram_tensor("cs_gath", [NCORE * 64, 1], F32)
    sclT_dram = nc.dram_tensor("sclT_d", [64, 8], F32)
    wcaches = [nc.dram_tensor(f"wcache{i}", [72 * 1024 * 1024], BF16).ap()
               for i in range(3)]

    es = ExitStack()

    def sb(name, pdim, cols, dt):
        return es.enter_context(nc.sbuf_tensor(name, [pdim, cols], dt))

    def mkpsum(name, cols, dt):
        return es.enter_context(nc.psum_tensor(name, [P, cols], dt))

    def mksem(name):
        return Sem(es.enter_context(nc.semaphore(name)))

    s_x = sb("s_x", P, NT * D, F32)
    s_A = sb("s_A", P, 16384, BF16)
    s_Bf = sb("s_Bf", P, 16384, BF16)
    s_C = sb("s_C", P, 16384, BF16)
    s_D = sb("s_D", P, 6656, BF16)
    s_E = sb("s_E", P, 8192, BF16)
    s_G = sb("s_G", P, 768, F32)
    s_stage = sb("s_stage", P, 1024, F32)
    s_cvt = sb("s_cvt", P, 1024, BF16)
    s_stats = sb("s_stats", P, 128, F32)
    s_gam = sb("s_gam", P, 4 * KD, F32)
    s_bq = sb("s_bq", P, KD, F32)
    s_gates = sb("s_gates", P, NT * 8, F32)
    s_brow = sb("s_brow", 1, 544, BF16)
    s_eps1 = sb("s_eps1", P, 1, F32)
    s_eps2 = sb("s_eps2", P, 1, F32)
    s_eps3 = sb("s_eps3", P, 1, F32)
    s_zero = sb("s_zero", P, 1, F32)
    s_one = sb("s_one", P, 1, F32)
    s_id = sb("s_id", P, P, BF16)
    s_triu = sb("s_triu", P, P, F32)
    s_mst = sb("s_mst", P, P, F32)
    s_onesf = sb("s_onesf", P, P, F32)
    s_onesrow = sb("s_onesrow", 1, P, BF16)
    s_router = sb("s_router", P, KD * 8, BF16)
    s_la = sb("s_la", P, NT * NH, F32)
    s_beta = sb("s_beta", P, NT * NH, F32)
    s_cla = sb("s_cla", P, NT * NH, F32)
    s_ncla = sb("s_ncla", P, NT * NH, F32)
    s_td0 = sb("s_td0", P, 64, F32)
    s_ces = sb("s_ces", P, 128, F32)
    s_csum = sb("s_csum", P, 64, F32)
    s_ab = sb("s_ab", P, KD * 32, BF16)
    s_id8 = sb("s_id8", 8, 8, F32)
    s_m2exp = sb("s_m2exp", 64, 64, F32)
    s_mask1 = sb("s_mask1", 64, 8, F32)
    s_csg = sb("s_csg", 64, 8, F32)
    s_L = sb("s_L", 64, 8, F32)
    s_scl = sb("s_scl", 64, 8, F32)
    s_sclT = sb("s_sclT", 8, 64, F32)

    pb = [mkpsum(f"pb{i}", 512, F32) for i in range(6)]
    ptA = mkpsum("ptA", 1024, BF16)
    ptB = mkpsum("ptB", 1024, BF16)
    PTN = {id(ptA): "ptA", id(ptB): "ptB"}

    csem = {e: mksem("c_" + e) for e in ENGS}
    dsem = {e: mksem("d_" + e) for e in ENGS}
    ccsem = mksem("cc")

    def OP(eng, fn, reads=(), writes=(), kind="c", tag="", big=False):
        return B.op(eng, fn, reads, writes, kind, tag, big)

    def DMA(eng, out, in_, reads, writes, tag="", slow=False):
        return OP(eng, lambda q: q.dma_start(out=out, in_=in_,
                                             allow_slow_non_contiguous=slow),
                  reads, writes, "d", tag)

    def pkey(pt_, c0=0):
        nm = PTN.get(id(pt_), pt_)
        return ("ps", nm, c0 // 512)

    bank_rot = [0]

    def next_bank(lo=0, hi=4):
        b = bank_rot[0] % (hi - lo) + lo
        bank_rot[0] += 1
        return b

    tr_rot = [0]

    def next_tr():
        t = tr_rot[0] % 4
        tr_rot[0] += 1
        return (ptA if t < 2 else ptB), (t % 2) * P

    dbg_ctr = [0]

    def dump(name, src_ap, read_keys):
        if name not in dbg:
            return
        shape = list(src_ap.shape)
        ap = nc.dram_tensor("dbg_" + name, shape, src_ap.dtype, kind="ExternalOutput").ap()
        DMA("gpsimd", ap, src_ap, list(read_keys), [("dbgsink", dbg_ctr[0])], tag="dbg_" + name)
        dbg_ctr[0] += 1

    GK = [("G", 0), ("G", 1), ("G", 2)]

    for r in range(NT):
        DMA("sync", s_x[:, r * D:(r + 1) * D], x_in[r * P:(r + 1) * P, :], [], [("x", r)])
    DMA("sync", s_id[:], c_ident, [], [("id",)])
    DMA("sync", s_triu[:], c_triu, [], [("triu",)])
    DMA("sync", s_mst[:], c_mstrict, [], [("mst",)])
    DMA("sync", s_onesf[:], c_ones, [], [("onesf",)])
    DMA("sync", s_onesrow[:], c_onesrow, [], [("onesrow",)])
    DMA("sync", s_id8[:], c_id8, [], [("id8",)])
    DMA("sync", s_m2exp[:], c_m2exp, [], [("m2exp",)])
    DMA("sync", s_mask1[:], c_mask1, [], [("mask1",)])
    for i, t in enumerate([ln1_g, ln1_b, ln2_g, ln2_b]):
        DMA("sync", s_gam[:, i * KD:(i + 1) * KD], t.rearrange("(f p) -> p f", p=P),
            [], [("gam", i)], slow=True)
    DMA("sync", s_bq[:], bq_d.rearrange("(f p) -> p f", p=P), [], [("bq",)], slow=True)
    DMA("sync", s_G[0:1, 0:32], bab_d.rearrange("(one d) -> one d", one=1), GK, GK, tag="babl")
    OP("vector", lambda q: q.tensor_copy(s_brow[0:1, 0:32], s_G[0:1, 0:32]),
       GK, [("brow", 0)], tag="babc")
    for hf in range(4):
        DMA("sync", s_G[0:8, 0:512], router_wt[:, hf * 512:(hf + 1) * 512], GK, GK,
            tag=f"routl{hf}")
        OP("vector", lambda q, hf=hf: q.tensor_copy(
            s_D[0:8, hf * 512:(hf + 1) * 512], s_G[0:8, 0:512]),
           GK, [("D", hf // 2)], tag=f"routc{hf}")
    for f in range(KD):
        pslot, pc = next_tr()
        pk = pkey(pslot, pc)
        OP("tensor", lambda q, f=f, pslot=pslot, pc=pc:
           q.transpose(pslot[:, pc:pc + 8], s_D[0:8, f * P:(f + 1) * P], s_id[0:8, 0:8]),
           [("D", 0), ("D", 1), ("id",)], [pk], tag=f"rtr{f}")
        OP("vector", lambda q, f=f, pslot=pslot, pc=pc:
           q.tensor_copy(s_router[:, f * 8:(f + 1) * 8], pslot[:, pc:pc + 8]),
           [pk], [("router",)], tag=f"rtrev{f}")
    for i in range(KD):
        DMA("sync", s_G[0:P, 0:32], wab_d[i * P:(i + 1) * P, :], GK, GK, tag=f"wabl{i}")
        OP("vector", lambda q, i=i: q.tensor_copy(s_ab[:, i * 32:(i + 1) * 32], s_G[:, 0:32]),
           GK, [("ab", i)], tag=f"wabc{i}")

    def f_eps(q):
        q.memset(s_eps1[:], EPS)
        q.memset(s_eps2[:], 1e-6)
        q.memset(s_eps3[:], 1e-24)
        q.memset(s_zero[:], 0.0)
        return q.memset(s_one[:], 1.0)
    OP("gpsimd", f_eps, [], [("eps",)])

    def ln_stats(src_ap, src_reads, so, tag=""):
        OP("vector", lambda q: q.tensor_reduce(out=s_stats[:, so:so + 1], in_=src_ap,
                                               axis=AX.X, op=ALU.add),
           src_reads, [stk(so), ("lnser",)], tag=tag + "s")
        for c4 in range(4):
            slot = 0
            OP("scalar", lambda q, c4=c4, slot=slot: q.activation(
                out=s_G[:, slot:slot + 512], in_=src_ap[:, c4 * 512:(c4 + 1) * 512],
                func=AF.Square),
               src_reads + gkeys(slot, slot + 512), gkeys(slot, slot + 512) + [("lnser",)],
               tag=tag + f"q{c4}")
            OP("vector", lambda q, c4=c4, slot=slot: q.tensor_reduce(
                out=s_stats[:, so + 16 + c4:so + 17 + c4], in_=s_G[:, slot:slot + 512],
                axis=AX.X, op=ALU.add),
               gkeys(slot, slot + 512), [stk(so + 16 + c4), ("lnser",)], tag=tag + f"r{c4}")

        OP("vector", lambda q: q.tensor_reduce(
            out=s_stats[:, so + 2:so + 3], in_=s_stats[:, so + 16:so + 20],
            axis=AX.X, op=ALU.add),
           [stk(so + 16), stk(so + 17), stk(so + 18), stk(so + 19)], [stk(so + 2)],
           tag=tag + "m0")
        OP("vector", lambda q: q.tensor_scalar(
            out=s_stats[:, so:so + 1], in0=s_stats[:, so:so + 1],
            scalar1=1.0 / D, scalar2=None, op0=ALU.mult),
           [stk(so)], [stk(so)], tag=tag + "m1")
        OP("vector", lambda q: q.tensor_tensor(
            out=s_stats[:, so + 3:so + 4], in0=s_stats[:, so:so + 1],
            in1=s_stats[:, so:so + 1], op=ALU.mult),
           [stk(so)], [stk(so + 3)], tag=tag + "m2")
        OP("vector", lambda q: q.tensor_scalar(
            out=s_stats[:, so + 2:so + 3], in0=s_stats[:, so + 2:so + 3],
            scalar1=1.0 / D, scalar2=None, op0=ALU.mult),
           [stk(so + 2)], [stk(so + 2)], tag=tag + "m3")
        OP("vector", lambda q: q.tensor_tensor(
            out=s_stats[:, so + 1:so + 2], in0=s_stats[:, so + 2:so + 3],
            in1=s_stats[:, so + 3:so + 4], op=ALU.subtract),
           [stk(so + 2), stk(so + 3)], [stk(so + 1)], tag=tag + "m4")
        OP("scalar", lambda q: q.activation(
            out=s_stats[:, so + 1:so + 2], in_=s_stats[:, so + 1:so + 2],
            func=AF.Ln, bias=s_eps1[:]),
           [stk(so + 1), ("eps",)], [stk(so + 1)], tag=tag + "r0")
        OP("scalar", lambda q: q.activation(
            out=s_stats[:, so + 1:so + 2], in_=s_stats[:, so + 1:so + 2],
            func=AF.Exp, scale=-0.5),
           [stk(so + 1)], [stk(so + 1)], tag=tag + "r1")

    stage_rot = [0]
    CVT = ["vector", "gpsimd", "scalar"]
    wc_alloc = [0]
    wc_blocks = {}
    WCCAP = 72 * 1024 * 1024
    ldq_rot = [0]

    def stream_win(name, src_fn, nseg, seg_cols, dst_ap, dst_keys, tag, tap=True,
                   cvt_eng="vector"):
        dst_keys = list(dst_keys)
        cols = nseg * seg_cols
        if name is not None and name in wc_blocks:
            off = wc_blocks[name]
            ti, to = divmod(off, WCCAP)
            blk = wcaches[ti][to:to + P * cols].rearrange("(p c) -> p c", p=P)
            OP("sync", lambda q: q.dma_start(out=dst_ap, in_=blk),
               [("wc", name)] + dst_keys, dst_keys, "d", tag + "R")
            return
        for i in range(nseg):
            srot = stage_rot[0] % 2
            stage_rot[0] += 1
            st_ap = s_stage[:, srot * 512:srot * 512 + seg_cols]
            ldq = "sync"

            def fld(q, st_ap=st_ap, i=i):
                s3 = src_fn(i)
                d = st_ap
                if len(s3.shape) == 3:
                    d = st_ap.rearrange("p (k f) -> p k f", f=s3.shape[2])
                return q.dma_start(out=d, in_=s3)
            OP(ldq, fld, [("stage", srot)], [("stage", srot)], "d", tag + f"l{i}")

            def fcv(q, st_ap=st_ap, i=i, seg_cols=seg_cols):
                d = dst_ap[:, i * seg_cols:(i + 1) * seg_cols]
                if hasattr(q, "tensor_copy"):
                    return q.tensor_copy(d, st_ap)
                return q.activation(out=d, in_=st_ap, func=AF.Copy)
            OP(cvt_eng, fcv, [("stage", srot)] + dst_keys, dst_keys, tag=tag + f"c{i}")
        if name is not None and tap:
            if name not in wc_blocks:
                need = P * cols
                if wc_alloc[0] % WCCAP + need > WCCAP:
                    wc_alloc[0] = (wc_alloc[0] // WCCAP + 1) * WCCAP
                wc_blocks[name] = wc_alloc[0]
                wc_alloc[0] += need
            off = wc_blocks[name]
            ti, to = divmod(off, WCCAP)
            blk = wcaches[ti][to:to + P * cols].rearrange("(p c) -> p c", p=P)
            OP("gpsimd", lambda q: q.dma_start(out=blk, in_=dst_ap),
               dst_keys, [("wc", name)], "d", tag + "T")

    def ffn_call(call):
        tg = f"f{call}"
        akeys = lambda r: rkeys("A", r * D, (r + 1) * D)
        bkeys_all = rkeys("B", 0, 16384)
        ckeys_all = rkeys("C", 0, 16384)

        for r in range(NT):
            so = (r % 2) * 32
            if r == 0:
                dump(f"xin{call}", s_x[:, 0:D], [("x", 0)])
            ln_stats(s_x[:, r * D:(r + 1) * D], [("x", r)], so, tag=f"{tg}ol{r}")
            if r == 0:
                dump(f"st{call}", s_stats[:, 0:32], [stk(0), stk(16)])
            OP("vector", lambda q, r=r, so=so: q.tensor_scalar(
                out=s_A[:, r * D:(r + 1) * D], in0=s_x[:, r * D:(r + 1) * D],
                scalar1=s_stats[:, so:so + 1], scalar2=s_stats[:, so + 1:so + 2],
                op0=ALU.subtract, op1=ALU.mult),
               [stk(so), stk(so + 1), ("x", r)], akeys(r), big=True, tag=f"{tg}lx{r}")

            OP("vector", lambda q, so=so: q.tensor_scalar(
                out=s_stats[:, so + 6:so + 7], in0=s_stats[:, so + 1:so + 2],
                scalar1=0.5, scalar2=1.0, op0=ALU.mult, op1=ALU.add),
               [stk(so + 1)], [stk(so + 6)], tag=f"{tg}xu0{r}")
            OP("vector", lambda q, so=so: q.tensor_tensor(
                out=s_stats[:, so + 7:so + 8], in0=s_stats[:, so:so + 1],
                in1=s_stats[:, so + 1:so + 2], op=ALU.mult),
               [stk(so), stk(so + 1)], [stk(so + 7)], tag=f"{tg}xu1{r}")
            OP("vector", lambda q, so=so: q.tensor_scalar(
                out=s_stats[:, so + 7:so + 8], in0=s_stats[:, so + 7:so + 8],
                scalar1=0.5, scalar2=None, op0=ALU.mult),
               [stk(so + 7)], [stk(so + 7)], tag=f"{tg}xu2{r}")
            OP("vector", lambda q, r=r, so=so: q.tensor_scalar(
                out=s_x[:, r * D:(r + 1) * D], in0=s_x[:, r * D:(r + 1) * D],
                scalar1=s_stats[:, so + 6:so + 7], scalar2=s_stats[:, so + 7:so + 8],
                op0=ALU.mult, op1=ALU.subtract),
               [stk(so + 6), stk(so + 7), ("x", r)], [("x", r)], big=True, tag=f"{tg}xu3{r}")

            so1 = 64 + (r % 2) * 32
            ln_stats(s_A[:, r * D:(r + 1) * D], akeys(r), so1, tag=f"{tg}l1{r}")
            OP("vector", lambda q, r=r, so1=so1: q.tensor_scalar(
                out=s_C[:, 0:D], in0=s_A[:, r * D:(r + 1) * D],
                scalar1=s_stats[:, so1:so1 + 1], scalar2=s_stats[:, so1 + 1:so1 + 2],
                op0=ALU.subtract, op1=ALU.mult),
               [stk(so1), stk(so1 + 1)] + akeys(r), rkeys("C", 0, D), big=True, tag=f"{tg}1a{r}")

            for f in range(KD):
                pslot, pc = next_tr()
                pk = pkey(pslot, pc)
                OP("tensor", lambda q, f=f, pslot=pslot, pc=pc:
                   q.transpose(pslot[:, pc:pc + P], s_C[:, f * P:(f + 1) * P], s_id[:]),
                   rkeys("C", 0, D) + [("id",)], [pk], tag=f"{tg}t1{r}_{f}")
                OP("vector", lambda q, r=r, f=f, pslot=pslot, pc=pc: q.tensor_scalar(
                    out=s_Bf[:, f * NTOK + r * P:f * NTOK + (r + 1) * P],
                    in0=pslot[:, pc:pc + P],
                    scalar1=s_gam[:, f:f + 1], scalar2=s_gam[:, KD + f:KD + f + 1],
                    op0=ALU.mult, op1=ALU.add),
                   [pk, ("gam", 0), ("gam", 1)], [("B", (f * NTOK + r * P) // 1024)],
                   tag=f"{tg}e1{r}_{f}")

        dump(f"lnx{call}", s_A[:, :], [("A", i) for i in range(16)])
        dump(f"ln1fm{call}", s_Bf[:, :], bkeys_all)

        pairs = [(s_, t_) for s_ in range(NB) for t_ in range(s_ + 1)]
        SC0, WS0, SM0 = 0, 80, 160
        for g in range(NKV):
            stream_win(f"qkvq{g}", lambda i, g=g: qkv_wt[i * P:(i + 1) * P,
                                                         768 * g:768 * g + 512],
                       KD, 512, s_E[:, 0:8192], rkeys("E", 0, 8192), f"{tg}qw{g}",
                       cvt_eng="gpsimd")
            ekeys = rkeys("E", 0, 8192)
            for r in range(NT):
                bank = next_bank()
                def f_qmm(q, r=r, bank=bank):
                    for kk in range(KD):
                        mm = q.matmul(pb[bank][:, 0:512],
                                      s_Bf[:, kk * NTOK + r * P:kk * NTOK + (r + 1) * P],
                                      s_E[:, kk * 512:(kk + 1) * 512],
                                      start=(kk == 0), stop=(kk == KD - 1))
                    return mm
                OP("tensor", f_qmm, bkeys_all + ekeys, [pkey(bank)], tag=f"{tg}qm{g}_{r}")
                OP("scalar", lambda q, r=r, bank=bank: q.activation(
                    out=s_D[:, r * 768:r * 768 + 512], in_=pb[bank][:, 0:512], func=AF.Copy),
                   [pkey(bank)], rkeys("D", r * 768, r * 768 + 512), tag=f"{tg}qe{g}_{r}")
            stream_win(f"qkvk{g}", lambda i, g=g: qkv_wt[i * 256:(i + 1) * 256,
                                                          768 * g + 512:768 * (g + 1)].rearrange(
                           "(k p) f -> p k f", p=P),
                       KD // 2, 512, s_E[:, 0:4096], rkeys("E", 0, 4096), f"{tg}kw{g}",
                       cvt_eng="gpsimd")
            ekeys2 = rkeys("E", 0, 4096)
            for r in range(NT):
                bank = next_bank()
                def f_kvmm(q, r=r, bank=bank):
                    for kk in range(KD):
                        mm = q.matmul(pb[bank][:, 0:256],
                                      s_Bf[:, kk * NTOK + r * P:kk * NTOK + (r + 1) * P],
                                      s_E[:, kk * 256:(kk + 1) * 256],
                                      start=(kk == 0), stop=(kk == KD - 1))
                    return mm
                OP("tensor", f_kvmm, bkeys_all + ekeys2, [pkey(bank)], tag=f"{tg}km{g}_{r}")
                OP("scalar", lambda q, r=r, bank=bank: q.activation(
                    out=s_D[:, r * 768 + 512:r * 768 + 768], in_=pb[bank][:, 0:256],
                    func=AF.Copy),
                   [pkey(bank)], rkeys("D", r * 768 + 512, r * 768 + 768), tag=f"{tg}ke{g}_{r}")

            dall = rkeys("D", 0, 6144)
            for pi, (s_, t_) in enumerate(pairs):
                for half in range(2):
                    rq = 2 * s_ + half
                    rk = 2 * t_ + half
                    def f_pr(q, rq=rq, rk=rk):
                        qv = s_D[:, rq * 768:rq * 768 + 512].rearrange("p (h d) -> p h d", h=4)
                        kv = s_D[:, rk * 768 + 512:rk * 768 + 640].rearrange(
                            "p (one d) -> p one d", one=1).broadcast_to([P, 4, HD])
                        pr = s_D[:, 6144:6656].rearrange("p (h d) -> p h d", h=4)
                        return q.tensor_tensor(out=pr, in0=qv, in1=kv, op=ALU.mult)
                    OP("vector", f_pr, dall, rkeys("D", 6144, 6656), tag=f"{tg}p{g}_{pi}_{half}")
                    sc = SC0 + half * 40 + pi * 4
                    OP("vector", lambda q, sc=sc: q.tensor_reduce(
                        out=s_G[:, sc:sc + 4],
                        in_=s_D[:, 6144:6656].rearrange("p (h d) -> p h d", h=4),
                        axis=AX.X, op=ALU.add),
                       rkeys("D", 6144, 6656) + GK, GK + [("sc", sc)], tag=f"{tg}r{g}_{pi}_{half}")

            screads = [("sc", SC0 + half * 40 + pi * 4) for half in range(2) for pi in range(10)]

            SMK = screads + GK + [("wsm",)]
            for half in range(2):
                base = SC0 + half * 40
                wbase = WS0 + half * 40
                for s_ in range(1, NB):
                    p0 = s_ * (s_ + 1) // 2
                    OP("vector", lambda q, base=base, p0=p0: q.tensor_copy(
                        s_G[:, SM0:SM0 + 4], s_G[:, base + p0 * 4:base + p0 * 4 + 4]),
                       SMK, GK + [("wsm",)], tag=f"{tg}sa{g}_{half}_{s_}")
                    for t_ in range(1, s_ + 1):
                        OP("vector", lambda q, base=base, p0=p0, t_=t_: q.tensor_tensor(
                            out=s_G[:, SM0:SM0 + 4], in0=s_G[:, SM0:SM0 + 4],
                            in1=s_G[:, base + (p0 + t_) * 4:base + (p0 + t_) * 4 + 4],
                            op=ALU.max),
                           SMK, GK + [("wsm",)], tag=f"{tg}sb{g}_{half}_{s_}_{t_}")

                    def f_subs(q, base=base, wbase=wbase, p0=p0, s_=s_):
                        for t_ in range(s_ + 1):
                            last = q.tensor_tensor(
                                out=s_G[:, wbase + (p0 + t_) * 4:wbase + (p0 + t_) * 4 + 4],
                                in0=s_G[:, base + (p0 + t_) * 4:base + (p0 + t_) * 4 + 4],
                                in1=s_G[:, SM0:SM0 + 4], op=ALU.subtract)
                        return last
                    OP("vector", f_subs, SMK, GK + [("wsm",)], tag=f"{tg}sc{g}_{half}_{s_}")

            def f_exp(q):
                for half in range(2):
                    wbase = WS0 + half * 40
                    for s_ in range(1, NB):
                        p0 = s_ * (s_ + 1) // 2
                        for t_ in range(s_ + 1):
                            c0 = wbase + (p0 + t_) * 4
                            last = q.activation(out=s_G[:, c0:c0 + 4], in_=s_G[:, c0:c0 + 4],
                                                func=AF.Exp, scale=RS)
                return last
            OP("scalar", f_exp, SMK, GK + [("wsm",)], tag=f"{tg}se{g}")

            for half in range(2):
                wbase = WS0 + half * 40
                for s_ in range(1, NB):
                    p0 = s_ * (s_ + 1) // 2
                    cols = [wbase + (p0 + t_) * 4 for t_ in range(s_ + 1)]
                    OP("vector", lambda q, cols=cols: q.tensor_copy(
                        s_G[:, SM0:SM0 + 4], s_G[:, cols[0]:cols[0] + 4]),
                       SMK, GK + [("wsm",)], tag=f"{tg}sd{g}_{half}_{s_}")
                    for t_ in range(1, s_ + 1):
                        OP("vector", lambda q, cols=cols, t_=t_: q.tensor_tensor(
                            out=s_G[:, SM0:SM0 + 4], in0=s_G[:, SM0:SM0 + 4],
                            in1=s_G[:, cols[t_]:cols[t_] + 4], op=ALU.add),
                           SMK, GK + [("wsm",)], tag=f"{tg}sf{g}_{half}_{s_}_{t_}")
                    OP("vector", lambda q: q.reciprocal(s_G[:, SM0 + 4:SM0 + 8],
                                                        s_G[:, SM0:SM0 + 4]),
                       SMK, GK + [("wsm",)], tag=f"{tg}sg2{g}_{half}_{s_}")

                    def f_mul(q, cols=cols, s_=s_):
                        for t_ in range(s_ + 1):
                            last = q.tensor_tensor(out=s_G[:, cols[t_]:cols[t_] + 4],
                                                   in0=s_G[:, cols[t_]:cols[t_] + 4],
                                                   in1=s_G[:, SM0 + 4:SM0 + 8], op=ALU.mult)
                        return last
                    OP("vector", f_mul, SMK, GK + [("wsm",)], tag=f"{tg}sh{g}_{half}_{s_}")

            for half in range(2):
                for s_ in range(NB):
                    rt = 2 * s_ + half
                    ckeys = rkeys("C", rt * D + g * 512, rt * D + (g + 1) * 512)
                    for t_ in range(s_ + 1):
                        rk = 2 * t_ + half
                        pi = s_ * (s_ + 1) // 2 + t_
                        for j in range(4):
                            wcol = WS0 + half * 40 + pi * 4 + j
                            yap = s_C[:, rt * D + (4 * g + j) * HD:rt * D + (4 * g + j + 1) * HD]
                            vap = s_D[:, rk * 768 + 640:rk * 768 + 768]
                            if s_ == 0:
                                OP("vector", lambda q, yap=yap, vap=vap: q.tensor_copy(yap, vap),
                                   dall, ckeys, tag=f"{tg}w0{g}_{half}_{j}")
                            elif t_ == 0:
                                OP("vector", lambda q, yap=yap, vap=vap, wcol=wcol:
                                   q.tensor_scalar(out=yap, in0=vap,
                                                   scalar1=s_G[:, wcol:wcol + 1],
                                                   scalar2=None, op0=ALU.mult),
                                   dall + [("wsm",)] + GK, ckeys, tag=f"{tg}wf{g}_{s_}_{half}_{j}")
                            else:
                                OP("vector", lambda q, yap=yap, vap=vap, wcol=wcol:
                                   q.scalar_tensor_tensor(
                                       out=yap, in0=vap, scalar=s_G[:, wcol:wcol + 1],
                                       in1=yap, op0=ALU.mult, op1=ALU.add),
                                   dall + [("wsm",)] + GK + ckeys, ckeys,
                                   tag=f"{tg}wa{g}_{s_}_{t_}_{half}_{j}")

        dump(f"yattn{call}", s_C[:, :], ckeys_all)

        for r in range(NT):
            for f in range(KD):
                pslot, pc = next_tr()
                pk = pkey(pslot, pc)
                OP("tensor", lambda q, r=r, f=f, pslot=pslot, pc=pc:
                   q.transpose(pslot[:, pc:pc + P], s_C[:, r * D + f * P:r * D + (f + 1) * P],
                               s_id[:]),
                   rkeys("C", r * D + f * P, r * D + (f + 1) * P) + [("id",)], [pk],
                   tag=f"{tg}yt{r}_{f}")
                OP("vector", lambda q, r=r, f=f, pslot=pslot, pc=pc:
                   q.tensor_copy(s_Bf[:, f * NTOK + r * P:f * NTOK + (r + 1) * P],
                                 pslot[:, pc:pc + P]),
                   [pk], [("B", (f * NTOK + r * P) // 1024)], tag=f"{tg}ye{r}_{f}")

        for win in range(4):
            stream_win(f"proj{win}", lambda i, win=win: proj_wt[i * P:(i + 1) * P,
                                                               win * 512:(win + 1) * 512],
                       KD, 512, s_E[:, 0:8192], rkeys("E", 0, 8192), f"{tg}pw{win}",
                       cvt_eng="gpsimd")
            ekeys = rkeys("E", 0, 8192)
            for r in range(NT):
                bank = next_bank()
                def f_pmm(q, r=r, bank=bank):
                    for kk in range(KD):
                        mm = q.matmul(pb[bank][:, 0:512],
                                      s_Bf[:, kk * NTOK + r * P:kk * NTOK + (r + 1) * P],
                                      s_E[:, kk * 512:(kk + 1) * 512],
                                      start=(kk == 0), stop=(kk == KD - 1))
                    return mm
                OP("tensor", f_pmm, bkeys_all + ekeys, [pkey(bank)], tag=f"{tg}pm{win}_{r}")
                xc = r * D + win * 512
                OP("vector", lambda q, bank=bank, xc=xc, r=r: q.scalar_tensor_tensor(
                    out=s_x[:, xc:xc + 512], in0=pb[bank][:, 0:512], scalar=0.5,
                    in1=s_x[:, xc:xc + 512], op0=ALU.mult, op1=ALU.add),
                   [pkey(bank), ("x", r)], [("x", r)], big=True, tag=f"{tg}px{win}_{r}")
                OP("vector", lambda q, bank=bank, xc=xc: q.scalar_tensor_tensor(
                    out=s_A[:, xc:xc + 512], in0=pb[bank][:, 0:512], scalar=1.0,
                    in1=s_A[:, xc:xc + 512], op0=ALU.mult, op1=ALU.add),
                   [pkey(bank)] + rkeys("A", xc, xc + 512), rkeys("A", xc, xc + 512),
                   big=True, tag=f"{tg}pa{win}_{r}")

        dump(f"hba{call}", s_A[:, :], [("A", i) for i in range(16)])

        for r in range(NT):
            so2 = (r % 2) * 32
            ln_stats(s_A[:, r * D:(r + 1) * D], akeys(r), so2, tag=f"{tg}l2{r}")
            OP("vector", lambda q, r=r, so2=so2: q.tensor_scalar(
                out=s_D[:, 0:D], in0=s_A[:, r * D:(r + 1) * D],
                scalar1=s_stats[:, so2:so2 + 1], scalar2=s_stats[:, so2 + 1:so2 + 2],
                op0=ALU.subtract, op1=ALU.mult),
               [stk(so2), stk(so2 + 1)] + akeys(r), rkeys("D", 0, D), big=True, tag=f"{tg}2a{r}")
            for f in range(KD):
                pslot, pc = next_tr()
                pk = pkey(pslot, pc)
                OP("tensor", lambda q, f=f, pslot=pslot, pc=pc:
                   q.transpose(pslot[:, pc:pc + P], s_D[:, f * P:(f + 1) * P], s_id[:]),
                   rkeys("D", 0, D) + [("id",)], [pk], tag=f"{tg}t2{r}_{f}")
                OP("vector", lambda q, r=r, f=f, pslot=pslot, pc=pc: q.tensor_scalar(
                    out=s_C[:, f * NTOK + r * P:f * NTOK + (r + 1) * P],
                    in0=pslot[:, pc:pc + P],
                    scalar1=s_gam[:, 2 * KD + f:2 * KD + f + 1],
                    scalar2=s_gam[:, 3 * KD + f:3 * KD + f + 1],
                    op0=ALU.mult, op1=ALU.add),
                   [pk, ("gam", 2), ("gam", 3)], [("C", (f * NTOK + r * P) // 1024)],
                   tag=f"{tg}e2{r}_{f}")

        dump(f"xmoe{call}", s_C[:, :], ckeys_all)

        for r in range(NT):
            bank = next_bank(4, 6)
            def f_rmm(q, r=r, bank=bank):
                for kk in range(KD):
                    mm = q.matmul(pb[bank][:, 0:8],
                                  s_C[:, kk * NTOK + r * P:kk * NTOK + (r + 1) * P],
                                  s_router[:, kk * 8:(kk + 1) * 8],
                                  start=(kk == 0), stop=(kk == KD - 1))
                return mm
            OP("tensor", f_rmm, ckeys_all + [("router",)], [pkey(bank)], tag=f"{tg}rm{r}")

            def VOP(fn, tg2):
                OP("vector", fn, GK + [pkey(bank), ("gate", r)],
                   GK + [("gate", r)], tag=tg2)
            VOP(lambda q, bank=bank: q.tensor_reduce(
                out=s_G[:, 8:9], in_=pb[bank][:, 0:8], axis=AX.X, op=ALU.max),
                f"{tg}ga{r}")
            VOP(lambda q, bank=bank: q.tensor_scalar(
                out=s_G[:, 0:8], in0=pb[bank][:, 0:8], scalar1=s_G[:, 8:9],
                scalar2=None, op0=ALU.subtract), f"{tg}gb{r}")
            OP("scalar", lambda q: q.activation(out=s_G[:, 0:8], in_=s_G[:, 0:8],
                                                func=AF.Exp),
               GK, GK, tag=f"{tg}gc{r}")
            VOP(lambda q: q.tensor_reduce(out=s_G[:, 9:10], in_=s_G[:, 0:8],
                                          axis=AX.X, op=ALU.add), f"{tg}gd{r}")
            VOP(lambda q: q.reciprocal(s_G[:, 10:11], s_G[:, 9:10]), f"{tg}ge2{r}")
            VOP(lambda q: q.tensor_scalar(out=s_G[:, 16:24], in0=s_G[:, 0:8],
                                          scalar1=s_G[:, 10:11], scalar2=None,
                                          op0=ALU.mult), f"{tg}gf{r}")
            VOP(lambda q: q.tensor_reduce(out=s_G[:, 11:12], in_=s_G[:, 16:24],
                                          axis=AX.X, op=ALU.max), f"{tg}gg{r}")
            VOP(lambda q: q.tensor_scalar(out=s_G[:, 24:32], in0=s_G[:, 16:24],
                                          scalar1=s_G[:, 11:12], scalar2=None,
                                          op0=ALU.is_lt), f"{tg}gh{r}")
            VOP(lambda q: q.tensor_tensor(out=s_G[:, 24:32], in0=s_G[:, 16:24],
                                          in1=s_G[:, 24:32], op=ALU.mult), f"{tg}gi{r}")
            VOP(lambda q: q.tensor_reduce(out=s_G[:, 12:13], in_=s_G[:, 24:32],
                                          axis=AX.X, op=ALU.max), f"{tg}gj{r}")
            VOP(lambda q: q.tensor_scalar(out=s_G[:, 24:32], in0=s_G[:, 16:24],
                                          scalar1=s_G[:, 11:12], scalar2=None,
                                          op0=ALU.is_ge), f"{tg}gk{r}")
            VOP(lambda q: q.tensor_reduce(out=s_G[:, 13:14], in_=s_G[:, 24:32],
                                          axis=AX.X, op=ALU.add), f"{tg}gl{r}")
            VOP(lambda q: q.tensor_scalar(out=s_G[:, 14:15], in0=s_G[:, 13:14],
                                          scalar1=2.0, scalar2=None, op0=ALU.is_ge),
                f"{tg}gm{r}")
            VOP(lambda q: q.tensor_tensor(out=s_G[:, 15:16], in0=s_G[:, 11:12],
                                          in1=s_G[:, 12:13], op=ALU.subtract), f"{tg}gn{r}")
            VOP(lambda q: q.tensor_tensor(out=s_G[:, 15:16], in0=s_G[:, 15:16],
                                          in1=s_G[:, 14:15], op=ALU.mult), f"{tg}go{r}")
            VOP(lambda q: q.tensor_tensor(out=s_G[:, 12:13], in0=s_G[:, 12:13],
                                          in1=s_G[:, 15:16], op=ALU.add), f"{tg}gp{r}")
            VOP(lambda q: q.tensor_scalar(out=s_G[:, 24:32], in0=s_G[:, 16:24],
                                          scalar1=s_G[:, 12:13], scalar2=None,
                                          op0=ALU.is_ge), f"{tg}gq{r}")
            VOP(lambda q: q.tensor_tensor(out=s_G[:, 8:9], in0=s_G[:, 11:12],
                                          in1=s_G[:, 12:13], op=ALU.add), f"{tg}gr{r}")
            VOP(lambda q: q.reciprocal(s_G[:, 9:10], s_G[:, 8:9]), f"{tg}gs{r}")
            VOP(lambda q: q.tensor_scalar(out=s_G[:, 9:10], in0=s_G[:, 9:10],
                                          scalar1=0.5, scalar2=None, op0=ALU.mult),
                f"{tg}gt{r}")
            VOP(lambda q, r=r: q.tensor_tensor(out=s_gates[:, r * 8:(r + 1) * 8],
                                               in0=s_G[:, 16:24], in1=s_G[:, 24:32],
                                               op=ALU.mult), f"{tg}gu{r}")
            VOP(lambda q, r=r: q.tensor_scalar(out=s_gates[:, r * 8:(r + 1) * 8],
                                               in0=s_gates[:, r * 8:(r + 1) * 8],
                                               scalar1=s_G[:, 9:10], scalar2=None,
                                               op0=ALU.mult), f"{tg}gv2{r}")

        dump(f"gates{call}", s_gates[:, :], [("gate", r) for r in range(NT)])

        def h1g_ap(m, c0, c1):
            if m < 16:
                return s_A[:, m * NTOK + c0:m * NTOK + c1]
            return s_Bf[:, (m - 16) * NTOK + c0:(m - 16) * NTOK + c1]

        def h1g_keys(m, c0, c1):
            if m < 16:
                return rkeys("A", m * NTOK + c0, m * NTOK + c1)
            return rkeys("B", (m - 16) * NTOK + c0, (m - 16) * NTOK + c1)

        h1keys = [("A", i) for i in range(16)] + [("B", i) for i in range(16)]

        for e in range(NE):
            for mg in range(16):
                eh = mg % 2
                stream_win(f"w1_{e}_{mg}",
                           lambda i, e=e, mg=mg: moe_w1[e * D + i * 256:e * D + (i + 1) * 256,
                                                        mg * 256:(mg + 1) * 256].rearrange(
                               "(k p) f -> p k f", p=P),
                           KD // 2, 512, s_E[:, eh * 4096:(eh + 1) * 4096],
                           rkeys("E", eh * 4096, (eh + 1) * 4096), f"{tg}w1_{e}_{mg}",
                           cvt_eng="vector")
                eke = rkeys("E", eh * 4096, (eh + 1) * 4096)
                for m in (2 * mg, 2 * mg + 1):
                    for half in range(2):
                        bank = next_bank()
                        def f_h1(q, m=m, half=half, bank=bank, eh=eh):
                            for kk in range(KD):
                                mm = q.matmul(
                                    pb[bank][:, 0:512],
                                    s_E[:, eh * 4096 + kk * 256 + (m % 2) * 128:
                                         eh * 4096 + kk * 256 + (m % 2) * 128 + 128],
                                    s_C[:, kk * NTOK + half * 512:kk * NTOK + (half + 1) * 512],
                                    start=(kk == 0), stop=(kk == KD - 1))
                            return mm
                        OP("tensor", f_h1, ckeys_all + eke, [pkey(bank)],
                           tag=f"{tg}h1_{e}_{m}_{half}")
                        OP("scalar", lambda q, m=m, half=half, bank=bank: q.activation(
                            out=h1g_ap(m, half * 512, (half + 1) * 512),
                            in_=pb[bank][:, 0:512], func=AF.Gelu),
                           [pkey(bank)], h1g_keys(m, half * 512, (half + 1) * 512),
                           tag=f"{tg}ge_{e}_{m}_{half}")

            for ow in range(16):
                eh = ow % 2
                stream_win(f"w2_{e}_{ow}",
                           lambda i, e=e, ow=ow: moe_w2[e * FFN + i * 512:e * FFN + (i + 1) * 512,
                                                        ow * 128:(ow + 1) * 128].rearrange(
                               "(k p) f -> p k f", p=P),
                           8, 512, s_E[:, eh * 4096:(eh + 1) * 4096],
                           rkeys("E", eh * 4096, (eh + 1) * 4096), f"{tg}w2_{e}_{ow}",
                           cvt_eng="vector")
                wke = rkeys("E", eh * 4096, (eh + 1) * 4096)
                for r in range(NT):
                    bank = next_bank()
                    def f_y(q, r=r, bank=bank, eh=eh):
                        for k2 in range(32):
                            mm = q.matmul(
                                pb[bank][:, 0:128],
                                h1g_ap(k2, r * P, (r + 1) * P),
                                s_E[:, eh * 4096 + k2 * 128:eh * 4096 + (k2 + 1) * 128],
                                start=(k2 == 0), stop=(k2 == 31))
                        return mm
                    OP("tensor", f_y, h1keys + wke, [pkey(bank)], tag=f"{tg}y_{e}_{ow}_{r}")
                    trot = 256 + ((ow * NT + r) % 2) * 128
                    OP("scalar", lambda q, e=e, r=r, bank=bank, trot=trot: q.activation(
                        out=s_G[:, trot:trot + 128], in_=pb[bank][:, 0:128], func=AF.Copy,
                        scale=s_gates[:, r * 8 + e:r * 8 + e + 1]),
                       [pkey(bank), ("gate", r)] + gkeys(trot, trot + 128),
                       gkeys(trot, trot + 128), tag=f"{tg}gv_{e}_{ow}_{r}")
                    xc = r * D + ow * 128
                    OP("gpsimd", lambda q, xc=xc, r=r, trot=trot: q.tensor_tensor(
                        out=s_x[:, xc:xc + 128], in0=s_x[:, xc:xc + 128],
                        in1=s_G[:, trot:trot + 128], op=ALU.add),
                       gkeys(trot, trot + 128) + [("x", r)], [("x", r)],
                       tag=f"{tg}xa_{e}_{ow}_{r}")

    def deltanet():
        tg = "dn"
        akeys_all = [("A", i) for i in range(16)]

        for r in range(NT):
            so = (r % 2) * 32
            ln_stats(s_x[:, r * D:(r + 1) * D], [("x", r)], so, tag=f"{tg}l{r}")
            OP("vector", lambda q, r=r, so=so: q.tensor_scalar(
                out=s_D[:, 0:D], in0=s_x[:, r * D:(r + 1) * D],
                scalar1=s_stats[:, so:so + 1], scalar2=s_stats[:, so + 1:so + 2],
                op0=ALU.subtract, op1=ALU.mult),
               [stk(so), stk(so + 1), ("x", r)], rkeys("D", 0, D), big=True, tag=f"{tg}a{r}")
            for f in range(KD):
                pslot, pc = next_tr()
                pk = pkey(pslot, pc)
                OP("tensor", lambda q, f=f, pslot=pslot, pc=pc:
                   q.transpose(pslot[:, pc:pc + P], s_D[:, f * P:(f + 1) * P], s_id[:]),
                   rkeys("D", 0, D) + [("id",)], [pk], tag=f"{tg}t{r}_{f}")
                OP("vector", lambda q, r=r, f=f, pslot=pslot, pc=pc:
                   q.tensor_copy(s_A[:, f * NTOK + r * P:f * NTOK + (r + 1) * P],
                                 pslot[:, pc:pc + P]),
                   [pk], [("A", (f * NTOK + r * P) // 1024)], tag=f"{tg}e{r}_{f}")

        dump("lnd", s_A[:, :], akeys_all)

        abk = [("ab", i) for i in range(KD)]
        for r in range(NT):
            bank = next_bank(4, 6)
            def f_ab(q, r=r, bank=bank):
                for kk in range(KD):
                    q.matmul(pb[bank][:, 0:32],
                             s_A[:, kk * NTOK + r * P:kk * NTOK + (r + 1) * P],
                             s_ab[:, kk * 32:(kk + 1) * 32],
                             start=(kk == 0), stop=False)
                return q.matmul(pb[bank][:, 0:32], s_onesrow[:],
                                s_brow[0:1, 0:32], start=False, stop=True)
            OP("tensor", f_ab, akeys_all + abk + [("onesrow",), ("brow", 0)], [pkey(bank)],
               tag=f"{tg}ab{r}")

            OP("scalar", lambda q, bank=bank: q.activation(
                out=s_G[:, 0:16], in_=pb[bank][:, 0:16], func=AF.Sigmoid),
               [pkey(bank)] + GK, GK, tag=f"{tg}sga{r}")
            OP("scalar", lambda q, r=r: q.activation(
                out=s_la[:, r * 16:(r + 1) * 16], in_=s_G[:, 0:16],
                func=AF.Ln, bias=s_eps2[:]),
               [("eps",)] + GK, GK + [("la", r)], tag=f"{tg}sgb{r}")

            OP("vector", lambda q, r=r, bank=bank: q.tensor_scalar(
                out=s_G[:, 16:32], in0=pb[bank][:, 16:32], scalar1=s_zero[:], scalar2=None,
                op0=ALU.max),
               [pkey(bank), ("eps",)] + GK, GK, tag=f"{tg}s1{r}")

            OP("scalar", lambda q, bank=bank: q.activation(
                out=s_G[:, 32:48], in_=pb[bank][:, 16:32], func=AF.Abs),
               [pkey(bank)] + GK, GK, tag=f"{tg}s2a{r}")
            OP("scalar", lambda q: q.activation(
                out=s_G[:, 32:48], in_=s_G[:, 32:48], func=AF.Exp, scale=-1.0),
               GK, GK, tag=f"{tg}s2b{r}")
            OP("scalar", lambda q: q.activation(
                out=s_G[:, 32:48], in_=s_G[:, 32:48], func=AF.Ln, bias=s_one[:]),
               [("eps",)] + GK, GK, tag=f"{tg}s2c{r}")
            OP("vector", lambda q, r=r: q.tensor_tensor(
                out=s_beta[:, r * 16:(r + 1) * 16], in0=s_G[:, 16:32], in1=s_G[:, 32:48],
                op=ALU.add),
               GK, [("beta", r)], tag=f"{tg}bt{r}")

            bank2 = next_bank(4, 6)
            OP("tensor", lambda q, r=r, bank2=bank2: q.matmul(
                pb[bank2][:, 0:16], s_triu[:], s_la[:, r * 16:(r + 1) * 16],
                start=True, stop=True),
               [("triu",), ("la", r)], [pkey(bank2)], tag=f"{tg}cl{r}")

            def f_ce(q, r=r, bank2=bank2):
                q.tensor_copy(s_cla[:, r * 16:(r + 1) * 16], pb[bank2][:, 0:16])
                return q.tensor_scalar(out=s_ncla[:, r * 16:(r + 1) * 16],
                                       in0=pb[bank2][:, 0:16],
                                       scalar1=-1.0, scalar2=None, op0=ALU.mult)
            OP("vector", f_ce, [pkey(bank2)], [("cla", r)], tag=f"{tg}ne{r}")

        for hg in range(8):
            h0 = 2 * hg
            stream_win(None, lambda i, hg=hg: wq_d[i * 256:(i + 1) * 256,
                                                   hg * 256:(hg + 1) * 256].rearrange(
                           "(k p) f -> p k f", p=P),
                       KD // 2, 512, s_E[:, 0:4096], rkeys("E", 0, 4096), f"{tg}wq{hg}",
                       cvt_eng="gpsimd")
            eke = rkeys("E", 0, 4096)
            for hp in range(2):
                h = h0 + hp
                for half in range(2):
                    bank = next_bank()
                    def f_qp(q, hp=hp, half=half, bank=bank):
                        for kk in range(KD):
                            mm = q.matmul(
                                pb[bank][:, 0:512],
                                s_E[:, kk * 256 + hp * 128:kk * 256 + (hp + 1) * 128],
                                s_A[:, kk * NTOK + half * 512:kk * NTOK + (half + 1) * 512],
                                start=(kk == 0), stop=(kk == KD - 1))
                        return mm
                    OP("tensor", f_qp, akeys_all + eke, [pkey(bank)], tag=f"{tg}qp{h}_{half}")
                    OP("vector", lambda q, h=h, half=half, bank=bank: q.tensor_scalar(
                        out=s_Bf[:, h * NTOK + half * 512:h * NTOK + (half + 1) * 512],
                        in0=pb[bank][:, 0:512], scalar1=s_bq[:, h:h + 1], scalar2=None,
                        op0=ALU.add),
                       [pkey(bank), ("bq",)], [("B", (h * NTOK + half * 512) // 1024)],
                       tag=f"{tg}qe{h}_{half}")

            DMA("sync", s_G[0:1, 0:256],
                bk_d.rearrange("(one d) -> one d", one=1)[0:1, hg * 256:(hg + 1) * 256], GK, GK,
                tag=f"{tg}bkl{hg}")
            OP("vector", lambda q: q.tensor_copy(s_brow[0:1, 32:288], s_G[0:1, 0:256]),
               GK, [("brow", 1)], tag=f"{tg}bkc{hg}")
            DMA("sync", s_G[0:1, 0:256],
                bv_d.rearrange("(one d) -> one d", one=1)[0:1, hg * 256:(hg + 1) * 256], GK, GK,
                tag=f"{tg}bvl{hg}")
            OP("vector", lambda q: q.tensor_copy(s_brow[0:1, 288:544], s_G[0:1, 0:256]),
               GK, [("brow", 2)], tag=f"{tg}bvc{hg}")

            stream_win(None, lambda i, hg=hg: wk_d[i * 256:(i + 1) * 256,
                                                   hg * 256:(hg + 1) * 256].rearrange(
                           "(k p) f -> p k f", p=P),
                       KD // 2, 512, s_E[:, 4096:8192], rkeys("E", 4096, 8192), f"{tg}wk{hg}",
                       cvt_eng="gpsimd")
            eke2 = rkeys("E", 4096, 8192)
            for r in range(NT):
                bank = next_bank()
                def f_kp(q, r=r, bank=bank):
                    for kk in range(KD):
                        q.matmul(pb[bank][:, 0:256],
                                 s_A[:, kk * NTOK + r * P:kk * NTOK + (r + 1) * P],
                                 s_E[:, 4096 + kk * 256:4096 + (kk + 1) * 256],
                                 start=(kk == 0), stop=False)
                    return q.matmul(pb[bank][:, 0:256], s_onesrow[:], s_brow[0:1, 32:288],
                                    start=False, stop=True)
                OP("tensor", f_kp, akeys_all + eke2 + [("onesrow",), ("brow", 1)], [pkey(bank)],
                   tag=f"{tg}kp{r}")
                OP("scalar", lambda q, r=r, bank=bank: q.activation(
                    out=s_D[:, r * 256:(r + 1) * 256], in_=pb[bank][:, 0:256], func=AF.Copy),
                   [pkey(bank)], rkeys("D", r * 256, (r + 1) * 256), tag=f"{tg}kc{r}")

                OP("vector", lambda q, r=r: q.tensor_tensor(
                    out=s_D[:, 6144:6400].rearrange("p (h d) -> p h d", h=2),
                    in0=s_D[:, r * 256:(r + 1) * 256].rearrange("p (h d) -> p h d", h=2),
                    in1=s_D[:, r * 256:(r + 1) * 256].rearrange("p (h d) -> p h d", h=2),
                    op=ALU.mult),
                   rkeys("D", r * 256, (r + 1) * 256), [("D", 6)], tag=f"{tg}n1a{r}")
                OP("vector", lambda q: q.tensor_reduce(
                    out=s_stats[:, 64:66],
                    in_=s_D[:, 6144:6400].rearrange("p (h d) -> p h d", h=2),
                    axis=AX.X, op=ALU.add),
                   [("D", 6)], [stk(64)], tag=f"{tg}n1b{r}")
                OP("vector", lambda q: q.tensor_scalar(
                    out=s_stats[:, 64:66], in0=s_stats[:, 64:66],
                    scalar1=s_eps3[:], scalar2=None, op0=ALU.max),
                   [("eps",), stk(64)], [stk(64)], tag=f"{tg}n1c{r}")

                OP("scalar", lambda q: q.activation(
                    out=s_stats[:, 80:82], in_=s_stats[:, 64:66], func=AF.Ln),
                   [stk(64)], [stk(80)], tag=f"{tg}n2a{r}")
                OP("scalar", lambda q: q.activation(
                    out=s_stats[:, 80:82], in_=s_stats[:, 80:82], func=AF.Exp, scale=-0.5),
                   [stk(80)], [stk(80)], tag=f"{tg}n2b{r}")

                OP("vector", lambda q, r=r, hg=hg: q.tensor_tensor(
                    out=s_stats[:, 82:84], in0=s_stats[:, 80:82],
                    in1=s_beta[:, r * 16 + 2 * hg:r * 16 + 2 * hg + 2], op=ALU.mult),
                   [stk(80), ("beta", r)], [stk(82)], tag=f"{tg}n3a{r}")

                def f_kn3b(q, r=r):
                    for j in range(2):
                        last = q.tensor_scalar(
                            out=s_D[:, r * 256 + j * 128:r * 256 + (j + 1) * 128],
                            in0=s_D[:, r * 256 + j * 128:r * 256 + (j + 1) * 128],
                            scalar1=s_stats[:, 82 + j:83 + j], scalar2=None, op0=ALU.mult)
                    return last
                OP("vector", f_kn3b, [stk(82)] + rkeys("D", r * 256, (r + 1) * 256),
                   rkeys("D", r * 256, (r + 1) * 256), tag=f"{tg}n3b{r}")

                for j in range(2):
                    pslot, pc = next_tr()
                    pk = pkey(pslot, pc)
                    OP("tensor", lambda q, r=r, j=j, pslot=pslot, pc=pc: q.transpose(
                        pslot[:, pc:pc + P], s_D[:, r * 256 + j * 128:r * 256 + (j + 1) * 128],
                        s_id[:]),
                       rkeys("D", r * 256, (r + 1) * 256) + [("id",)], [pk],
                       tag=f"{tg}bt{r}_{j}")
                    OP("vector", lambda q, r=r, j=j, pslot=pslot, pc=pc: q.tensor_copy(
                        s_D[:, 2048 + j * 1024 + r * P:2048 + j * 1024 + (r + 1) * P],
                        pslot[:, pc:pc + P]),
                       [pk], [("D", (2048 + j * 1024 + r * P) // 1024)], tag=f"{tg}be{r}_{j}")

            stream_win(None, lambda i, hg=hg: wv_d[i * 256:(i + 1) * 256,
                                                   hg * 256:(hg + 1) * 256].rearrange(
                           "(k p) f -> p k f", p=P),
                       KD // 2, 512, s_E[:, 0:4096], rkeys("E", 0, 4096), f"{tg}wv{hg}",
                       cvt_eng="gpsimd")
            for r in range(NT):
                bank = next_bank()
                def f_vp(q, r=r, bank=bank):
                    for kk in range(KD):
                        q.matmul(pb[bank][:, 0:256],
                                 s_A[:, kk * NTOK + r * P:kk * NTOK + (r + 1) * P],
                                 s_E[:, kk * 256:(kk + 1) * 256],
                                 start=(kk == 0), stop=False)
                    return q.matmul(pb[bank][:, 0:256], s_onesrow[:], s_brow[0:1, 288:544],
                                    start=False, stop=True)
                OP("tensor", f_vp, akeys_all + eke + [("onesrow",), ("brow", 2)], [pkey(bank)],
                   tag=f"{tg}vp{r}")
                OP("scalar", lambda q, r=r, bank=bank: q.activation(
                    out=s_D[:, 4096 + r * 256:4096 + (r + 1) * 256], in_=pb[bank][:, 0:256],
                    func=AF.Copy),
                   [pkey(bank)], rkeys("D", 4096 + r * 256, 4096 + (r + 1) * 256),
                   tag=f"{tg}ve{r}")

            for r in range(NT):
                c01 = r & 1
                for j in range(2):
                    h = h0 + j
                    pair = (r >> 1) * 16 + h
                    cbank = 4 if (r * 2 + j) % 2 == 0 else 5
                    abank = 9 - cbank
                    lacol = s_la[:, r * 16 + h:r * 16 + h + 1]
                    nclacol = s_ncla[:, r * 16 + h:r * 16 + h + 1]
                    OP("vector", lambda q, lacol=lacol: q.tensor_scalar(
                        out=s_G[:, 0:128], in0=s_triu[:], scalar1=lacol, scalar2=None,
                        op0=ALU.mult),
                       [("triu",), ("la", r)] + GK, GK, tag=f"{tg}c1_{r}_{h}")
                    OP("tensor", lambda q, cbank=cbank: q.matmul(
                        pb[cbank][:, 0:128], s_onesf[:], s_G[:, 0:128], start=True, stop=True),
                       [("onesf",)] + GK, [pkey(cbank)], tag=f"{tg}c2_{r}_{h}")
                    OP("vector", lambda q, cbank=cbank: q.tensor_tensor(
                        out=s_G[:, 128:256], in0=pb[cbank][:, 0:128], in1=s_mst[:], op=ALU.add),
                       [pkey(cbank), ("mst",)] + GK, GK, tag=f"{tg}c3_{r}_{h}")

                    def f_dt(q, nclacol=nclacol, pair=pair, c01=c01):
                        q.activation(out=s_G[:, 256:384], in_=s_G[:, 128:256],
                                     func=AF.Exp, bias=nclacol)
                        if c01 == 0:
                            q.activation(out=s_td0[:, pair:pair + 1],
                                         in_=s_G[:, 255:256], func=AF.Exp)
                        q.activation(out=s_stats[:, 96:97], in_=s_G[:, 255:256],
                                     func=AF.Exp, bias=nclacol)
                        return q.activation(out=s_ces[:, pair * 2 + c01:pair * 2 + c01 + 1],
                                            in_=s_G[:, 255:256], func=AF.Copy)
                    OP("scalar", f_dt, [("cla", r)] + GK,
                       GK + [("td0", pair), stk(96), ("ce", pair * 2 + c01)],
                       tag=f"{tg}dt_{r}_{h}")
                    OP("tensor", lambda q, j=j, h=h, r=r, abank=abank: q.matmul(
                        pb[abank][:, 0:128],
                        s_D[:, 2048 + j * 1024 + r * P:2048 + j * 1024 + (r + 1) * P],
                        s_Bf[:, h * NTOK + r * P:h * NTOK + (r + 1) * P],
                        start=True, stop=True),
                       [("D", (2048 + j * 1024 + r * P) // 1024),
                        ("B", (h * NTOK + r * P) // 1024)], [pkey(abank)],
                       tag=f"{tg}at_{r}_{h}")
                    OP("vector", lambda q, abank=abank: q.tensor_tensor(
                        out=s_D[:, 6144:6272], in0=pb[abank][:, 0:128],
                        in1=s_G[:, 256:384], op=ALU.mult),
                       [pkey(abank)] + GK, [("D", 6)], tag=f"{tg}am_{r}_{h}")
                    OP("vector", lambda q, r=r, j=j: q.tensor_scalar(
                        out=s_D[:, 6272:6400],
                        in0=s_D[:, r * 256 + j * 128:r * 256 + (j + 1) * 128],
                        scalar1=s_stats[:, 96:97], scalar2=None, op0=ALU.mult),
                       [stk(96)] + rkeys("D", r * 256, (r + 1) * 256), [("D", 6)],
                       tag=f"{tg}kw_{r}_{h}")
                    OP("tensor", lambda q, r=r, j=j, cbank=cbank: q.matmul(
                        pb[cbank][:, 0:128], s_D[:, 6144:6272],
                        s_D[:, 4096 + r * 256 + j * 128:4096 + r * 256 + (j + 1) * 128],
                        start=True, stop=True),
                       [("D", 6), ("D", (4096 + r * 256 + j * 128) // 1024)], [pkey(cbank)],
                       tag=f"{tg}yl_{r}_{h}")
                    xc = r * D + h * HD
                    OP("vector", lambda q, xc=xc, cbank=cbank, r=r: q.scalar_tensor_tensor(
                        out=s_x[:, xc:xc + HD], in0=pb[cbank][:, 0:128], scalar=1.0,
                        in1=s_x[:, xc:xc + HD], op0=ALU.mult, op1=ALU.add),
                       [pkey(cbank), ("x", r)], [("x", r)], tag=f"{tg}yx_{r}_{h}")
                    OP("tensor", lambda q, r=r, j=j, abank=abank: q.matmul(
                        pb[abank][:, 0:128], s_D[:, 6272:6400],
                        s_D[:, 4096 + r * 256 + j * 128:4096 + r * 256 + (j + 1) * 128],
                        start=True, stop=True),
                       [("D", 6), ("D", (4096 + r * 256 + j * 128) // 1024)], [pkey(abank)],
                       tag=f"{tg}dl_{r}_{h}")
                    if c01 == 0:
                        def f_st0(q, pair=pair, abank=abank):
                            q.tensor_copy(s_C[:, pair * P:(pair + 1) * P], pb[abank][:, 0:128])
                            return q.tensor_copy(s_C[:, 8192 + pair * P:8192 + (pair + 1) * P],
                                                 pb[abank][:, 0:128])
                        OP("vector", f_st0, [pkey(abank)],
                           rkeys("C", pair * P, (pair + 1) * P) +
                           rkeys("C", 8192 + pair * P, 8192 + (pair + 1) * P) +
                           [("dT0", pair), ("St", pair)], tag=f"{tg}s0_{r}_{h}")
                    else:
                        OP("scalar", lambda q, pair=pair: q.activation(
                            out=s_stats[:, 97:98], in_=s_ces[:, pair * 2 + 1:pair * 2 + 2],
                            func=AF.Exp),
                           [("ce", pair * 2 + 1)], [stk(97)], tag=f"{tg}t1_{r}_{h}")

                        OP("vector", lambda q, pair=pair: q.tensor_scalar(
                            out=s_D[:, 6400:6528],
                            in0=s_C[:, 8192 + pair * P:8192 + (pair + 1) * P],
                            scalar1=s_stats[:, 97:98], scalar2=None, op0=ALU.mult),
                           [("St", pair), stk(97)] +
                           rkeys("C", 8192 + pair * P, 8192 + (pair + 1) * P),
                           [("D", 6)], tag=f"{tg}s1a_{r}_{h}")
                        OP("vector", lambda q, pair=pair, abank=abank: q.tensor_tensor(
                            out=s_C[:, 8192 + pair * P:8192 + (pair + 1) * P],
                            in0=pb[abank][:, 0:128], in1=s_D[:, 6400:6528], op=ALU.add),
                           [pkey(abank), ("D", 6)] +
                           rkeys("C", 8192 + pair * P, 8192 + (pair + 1) * P),
                           rkeys("C", 8192 + pair * P, 8192 + (pair + 1) * P) +
                           [("St", pair)], tag=f"{tg}s1b_{r}_{h}")
                        OP("vector", lambda q, pair=pair: q.tensor_tensor(
                            out=s_csum[:, pair:pair + 1], in0=s_ces[:, pair * 2:pair * 2 + 1],
                            in1=s_ces[:, pair * 2 + 1:pair * 2 + 2], op=ALU.add),
                           [("ce", pair * 2), ("ce", pair * 2 + 1)], [("cs", pair)],
                           tag=f"{tg}cs_{r}_{h}")

        dump("dnstloc", s_C[:, 8192:16384], [("St", pair) for pair in range(64)])
        dump("dncsum", s_csum[:, 0:64], [("cs", pair) for pair in range(64)])

        stw = [("St", pair) for pair in range(64)]
        DMA("gpsimd", st_exp_h.ap().rearrange("(pair u) d -> u pair d", u=P),
            s_C[:, 8192:16384].rearrange("u (pair d) -> u pair d", d=HD),
            stw + rkeys("C", 8192, 16384), [("stexp",)], tag="stexp")
        csw = [("cs", pair) for pair in range(64)]
        DMA("gpsimd", cs_exp_h.ap().rearrange("(one pair) one2 -> one pair one2", one=1),
            s_csum[0:1, 0:64].rearrange("one (pair one2) -> one pair one2", one2=1),
            csw, [("csexp",)], tag="csexp")
        OP("gpsimd", lambda q: q.collective_compute(
            "AllGather", ALU.bypass, ins=[st_exp_h.ap().opt()], outs=[st_gath_h.ap().opt()],
            replica_groups=[list(range(NCORE))]),
           [("stexp",)], [("stg",)], kind="cc", tag="ccst")
        OP("gpsimd", lambda q: q.collective_compute(
            "AllGather", ALU.bypass, ins=[cs_exp_h.ap().opt()], outs=[cs_gath_h.ap().opt()],
            replica_groups=[list(range(NCORE))]),
           [("csexp",)], [("csgat",)], kind="cc", tag="cccs")

        DMA("gpsimd", s_csg[:].rearrange("p (r one) -> p r one", one=1),
            cs_gath_h.ap().rearrange("(r p) one -> p r one", p=64),
            [("csgat",)], [("csgs",)], tag="csgl", slow=True)

        for rr in range(8):
            OP("vector", lambda q, rr=rr: q.tensor_tensor(
                out=s_G[0:64, 0:8], in0=s_csg[:],
                in1=s_m2exp[:, rr * 8:(rr + 1) * 8], op=ALU.mult),
               [("csgs",), ("m2exp",)] + GK, GK, tag=f"dnLa{rr}")
            OP("vector", lambda q, rr=rr: q.tensor_reduce(
                out=s_L[:, rr:rr + 1], in_=s_G[0:64, 0:8], axis=AX.X, op=ALU.add),
               GK, GK + [("L",)], tag=f"dnLb{rr}")
        OP("scalar", lambda q: q.activation(out=s_scl[:], in_=s_L[:], func=AF.Exp),
           [("L",)], [("scl",)], tag="dnsE")
        OP("vector", lambda q: q.tensor_tensor(out=s_scl[:], in0=s_scl[:], in1=s_mask1[:],
                                               op=ALU.mult),
           [("scl",), ("mask1",)], [("scl",)], tag="dnsM")
        DMA("gpsimd", sclT_dram.ap(), s_scl[:], [("scl",)], [("sclTd",)], tag="sclTd")
        DMA("gpsimd", s_sclT[:], sclT_dram.ap().rearrange("p r -> r p"),
            [("sclTd",)], [("sclT",)], tag="sclTl", slow=True)

        def f_bc(q):
            for rr in range(8):
                last = q.tensor_scalar(out=s_G[0:8, rr * 64:(rr + 1) * 64],
                                       in0=s_sclT[:], scalar1=s_id8[:, rr:rr + 1],
                                       scalar2=None, op0=ALU.mult)
            return last
        OP("vector", f_bc, [("sclT",), ("id8",)] + GK, GK + [("bcm",)], tag="dnbcm")
        for rr in range(8):
            OP("tensor", lambda q, rr=rr: q.matmul(
                pb[4][:, rr * 64:(rr + 1) * 64], s_onesf[0:8, :],
                s_G[0:8, rr * 64:(rr + 1) * 64], start=True, stop=True),
               [("onesf",), ("bcm",)] + GK, [pkey(4)], tag=f"dnbc{rr}")

        for blk in range(16):
            for rr in range(8):
                gb = blk % 2
                DMA("gpsimd",
                    s_A[:, gb * 1024:gb * 1024 + 512].rearrange("u (pair d) -> u pair d", d=HD),
                    st_gath_h.ap()[rr * 8192 + blk * 512:rr * 8192 + (blk + 1) * 512, :]
                    .rearrange("(pair u) d -> u pair d", u=P),
                    [("stg",), ("A", gb)], [("A", gb)], tag=f"stgl{blk}_{rr}")
                for pp in range(4):
                    pair = blk * 4 + pp
                    acc = s_G[:, pp * 128:(pp + 1) * 128]
                    if rr == 0:
                        OP("vector", lambda q, acc=acc, gb=gb, pp=pp, rr=rr, pair=pair:
                           q.tensor_scalar(
                               out=acc,
                               in0=s_A[:, gb * 1024 + pp * 128:gb * 1024 + (pp + 1) * 128],
                               scalar1=pb[4][:, rr * 64 + pair:rr * 64 + pair + 1],
                               scalar2=None, op0=ALU.mult),
                           [("A", gb), pkey(4)] + GK, GK + [("acc", pp)],
                           tag=f"cb{blk}_{rr}_{pp}")
                    else:
                        OP("vector", lambda q, acc=acc, gb=gb, pp=pp, rr=rr, pair=pair:
                           q.scalar_tensor_tensor(
                               out=acc,
                               in0=s_A[:, gb * 1024 + pp * 128:gb * 1024 + (pp + 1) * 128],
                               scalar=pb[4][:, rr * 64 + pair:rr * 64 + pair + 1],
                               in1=acc, op0=ALU.mult, op1=ALU.add),
                           [("A", gb), pkey(4), ("acc", pp)] + GK, GK + [("acc", pp)],
                           tag=f"cb{blk}_{rr}_{pp}")
            for pp in range(4):
                pair = blk * 4 + pp
                OP("vector", lambda q, pp=pp, pair=pair: q.tensor_copy(
                    s_C[:, 8192 + pair * P:8192 + (pair + 1) * P],
                    s_G[:, pp * 128:(pp + 1) * 128]),
                   [("acc", pp)] + GK,
                   rkeys("C", 8192 + pair * P, 8192 + (pair + 1) * P) + [("Stin", pair)],
                   tag=f"cbc{blk}_{pp}")

        dump("dnstin", s_C[:, 8192:16384], [("Stin", pair) for pair in range(64)])
        dump("dnscl", s_scl[:, 0:8], [("scl",)])

        for r in range(NT):
            c01 = r & 1
            for h in range(NH):
                pair = (r >> 1) * 16 + h
                bank = next_bank()
                if c01 == 0:
                    OP("tensor", lambda q, h=h, r=r, bank=bank, pair=pair: q.matmul(
                        pb[bank][:, 0:128],
                        s_Bf[:, h * NTOK + r * P:h * NTOK + (r + 1) * P],
                        s_C[:, 8192 + pair * P:8192 + (pair + 1) * P],
                        start=True, stop=True),
                       [("B", (h * NTOK + r * P) // 1024), ("Stin", pair)] +
                       rkeys("C", 8192 + pair * P, 8192 + (pair + 1) * P),
                       [pkey(bank)], tag=f"{tg}yg_{r}_{h}")
                else:
                    OP("vector", lambda q, pair=pair: q.tensor_scalar(
                        out=s_D[:, 6400:6528],
                        in0=s_C[:, 8192 + pair * P:8192 + (pair + 1) * P],
                        scalar1=s_td0[:, pair:pair + 1], scalar2=None, op0=ALU.mult),
                       [("Stin", pair), ("td0", pair)] +
                       rkeys("C", 8192 + pair * P, 8192 + (pair + 1) * P),
                       [("D", 6)], tag=f"{tg}sTa_{r}_{h}")
                    OP("vector", lambda q, pair=pair: q.tensor_tensor(
                        out=s_D[:, 6400:6528], in0=s_D[:, 6400:6528],
                        in1=s_C[:, pair * P:(pair + 1) * P], op=ALU.add),
                       [("dT0", pair), ("D", 6)] + rkeys("C", pair * P, (pair + 1) * P),
                       [("D", 6)], tag=f"{tg}sTb_{r}_{h}")
                    OP("tensor", lambda q, h=h, r=r, bank=bank: q.matmul(
                        pb[bank][:, 0:128],
                        s_Bf[:, h * NTOK + r * P:h * NTOK + (r + 1) * P],
                        s_D[:, 6400:6528], start=True, stop=True),
                       [("B", (h * NTOK + r * P) // 1024), ("D", 6)], [pkey(bank)],
                       tag=f"{tg}yg_{r}_{h}")
                OP("scalar", lambda q, r=r, h=h: q.activation(
                    out=s_stats[:, 98:99], in_=s_cla[:, r * 16 + h:r * 16 + h + 1], func=AF.Exp),
                   [("cla", r)], [stk(98)], tag=f"{tg}cd_{r}_{h}")
                xc = r * D + h * HD
                OP("vector", lambda q, xc=xc, bank=bank, r=r: q.scalar_tensor_tensor(
                    out=s_x[:, xc:xc + HD], in0=pb[bank][:, 0:128],
                    scalar=s_stats[:, 98:99], in1=s_x[:, xc:xc + HD],
                    op0=ALU.mult, op1=ALU.add),
                   [pkey(bank), stk(98), ("x", r)], [("x", r)], tag=f"{tg}gx_{r}_{h}")

    if "ffn0" in stages:
        ffn_call(0)
    if "dn" in stages:
        deltanet()
    if "ffn1" in stages:
        ffn_call(1)

    for r in range(NT):
        DMA("sync", y_out[r * P:(r + 1) * P, :], s_x[:, r * D:(r + 1) * D],
            [("x", r)], [("yst", r)], tag=f"st{r}")
    OP("sync", lambda q: None, [("yst", r) for r in range(NT)], [], tag="fin")

    B.resolve(csem, dsem, ccsem)
    with es, nc.Block() as block:
        B.emit(block)
    return nc


STAGES = ("ffn0", "dn", "ffn1")


def kernel(**inputs):
    x = np.asarray(inputs["L0_attn_x"])
    consts = _static_consts()
    base = {
        "qkv_wt": np.ascontiguousarray(np.asarray(inputs["L0_ffn_qkv_weight"]).T),
        "proj_wt": np.ascontiguousarray(np.asarray(inputs["L0_ffn_proj_weight"]).T),
        "router_wt": np.ascontiguousarray(np.asarray(inputs["L0_ffn_router_weight"]).T),
        "ln1_g": np.asarray(inputs["L0_ffn_ln1_weight"]),
        "ln1_b": np.asarray(inputs["L0_ffn_ln1_bias"]),
        "ln2_g": np.asarray(inputs["L0_ffn_ln2_weight"]),
        "ln2_b": np.asarray(inputs["L0_ffn_ln2_bias"]),
        "moe_w1": np.asarray(inputs["L0_ffn_moe_w1"]),
        "moe_w2": np.asarray(inputs["L0_ffn_moe_w2"]),
        "wq": np.asarray(inputs["L0_attn_Wq"]),
        "wk": np.asarray(inputs["L0_attn_Wk"]),
        "wv": np.asarray(inputs["L0_attn_Wv"]),
        "bq": np.asarray(inputs["L0_attn_bq"]),
        "bk": np.asarray(inputs["L0_attn_bk"]),
        "bv": np.asarray(inputs["L0_attn_bv"]),
        "wab": np.ascontiguousarray(np.concatenate(
            [np.asarray(inputs["L0_attn_Walpha"]), np.asarray(inputs["L0_attn_Wbeta"])], axis=1)),
        "bab": np.ascontiguousarray(np.concatenate(
            [np.asarray(inputs["L0_attn_balpha"]), np.asarray(inputs["L0_attn_bbeta"])])),
    }
    base.update(consts)

    nc = build_program(stages=STAGES)
    in_maps = []
    for c in range(NCORE):
        m2exp, mask1 = _mk_core_consts(c)
        m = dict(base)
        m["c_m2exp"] = m2exp
        m["c_mask1"] = mask1
        m["x"] = np.ascontiguousarray(x[:, c * TLOC:(c + 1) * TLOC, :].reshape(NTOK, D))
        in_maps.append(m)
    res = run_bass_kernel_spmd(nc, in_maps, core_ids=list(range(NCORE)))
    out = np.empty((NB, 2048, D), dtype=np.float32)
    for c in range(NCORE):
        out[:, c * TLOC:(c + 1) * TLOC, :] = np.asarray(
            res.results[c]["y"], np.float32).reshape(NB, TLOC, D)
    return out


# version 05
# speedup vs baseline: 1.3915x; 1.0000x over previous
import numpy as np
import ml_dtypes
import concourse.bass as bass
import concourse.mybir as mybir
from concourse.bass_utils import run_bass_kernel_spmd
from contextlib import ExitStack

F32 = mybir.dt.float32
BF16 = mybir.dt.bfloat16
AF = mybir.ActivationFunctionType
ALU = mybir.AluOpType
AX = mybir.AxisListType

P = 128
D = 2048
NB = 4
TLOC = 256
NTOK = NB * TLOC
NT = NTOK // P
KD = D // P
NH = 16
HD = 128
NKV = 4
FFN = 4096
NE = 8
NCORE = 8
EPS = 1e-5
RS = 1.0 / float(np.sqrt(128.0))

ENGS = ("sync", "tensor", "vector", "scalar", "gpsimd")


class Op:
    __slots__ = ("eng", "fn", "reads", "writes", "kind", "needs_inc", "sem",
                 "val", "waits", "idx", "tag", "deps", "raw", "big")

    def __init__(self, eng, fn, reads, writes, kind, tag, big=False):
        self.eng = eng
        self.fn = fn
        self.reads = reads
        self.writes = writes
        self.kind = kind
        self.needs_inc = kind in ("d", "cc")
        self.sem = None
        self.val = None
        self.waits = []
        self.tag = tag
        self.big = big


class Sem:
    def __init__(self, h):
        self.h = h
        self.n = 0

    def inc(self, k):
        self.n += k
        return self.n


class Builder:
    def __init__(self):
        self.ops = []
        self.last_w = {}
        self.readers = {}

    def op(self, eng, fn, reads=(), writes=(), kind="c", tag="", big=False):
        reads = list(reads)
        writes = list(writes)
        for k in [k for k in reads if k and k[0] == "ps"]:
            reads.remove(k)
            if k not in writes:
                writes.append(k)
        o = Op(eng, fn, reads, writes, kind, tag, big)
        o.idx = len(self.ops)
        raw = set()
        for r in o.reads:
            j = self.last_w.get(r)
            if j is not None:
                raw.add(j)
        other = set()
        for w in o.writes:
            j = self.last_w.get(w)
            if j is not None:
                other.add(j)
            for j2 in self.readers.get(w, ()):
                other.add(j2)
        raw.discard(o.idx)
        other.discard(o.idx)
        o.raw = raw
        o.deps = raw | other
        self.ops.append(o)
        for r in o.reads:
            self.readers.setdefault(r, []).append(o.idx)
        for w in o.writes:
            self.last_w[w] = o.idx
            self.readers[w] = []
        return o

    @staticmethod
    def _skip(d, o, is_raw):
        if d.kind != "c" or d.eng != o.eng:
            return False
        if d.eng in ("tensor", "gpsimd", "sync"):
            return True
        if not is_raw:
            return True
        return d.big

    def resolve(self, csem, dsem, ccsem):
        for o in self.ops:
            for j in o.deps:
                d = self.ops[j]
                if self._skip(d, o, j in o.raw):
                    continue
                d.needs_inc = True
        for o in self.ops:
            if not o.needs_inc:
                continue
            if o.kind == "d":
                o.sem = dsem[o.eng]
                o.val = o.sem.inc(16)
            elif o.kind == "cc":
                o.sem = ccsem
                o.val = o.sem.inc(1)
            else:
                o.sem = csem[o.eng]
                o.val = o.sem.inc(1)
        marks = {e: {} for e in ENGS}
        for o in self.ops:
            m = marks[o.eng]
            for j in sorted(o.deps):
                d = self.ops[j]
                if self._skip(d, o, j in o.raw):
                    continue
                key = id(d.sem)
                if m.get(key, -1) >= d.val:
                    continue
                m[key] = d.val
                o.waits.append((d.sem, d.val))

    def emit(self, block):
        per_eng = {e: [] for e in ENGS}
        for o in self.ops:
            per_eng[o.eng].append(o)

        def runner(ops):
            def body(q):
                for o in ops:
                    for (s, v) in o.waits:
                        q.wait_ge(s.h, v)
                    last = o.fn(q)
                    if o.needs_inc:
                        assert last is not None, f"op {o.tag} needs inc but returned None"
                        last.then_inc(o.sem.h, 16 if o.kind == "d" else 1)
            return body
        for e in ENGS:
            getattr(block, e)(runner(per_eng[e]))


def rkeys(name, c0, c1, quantum=1024):
    return [(name, i) for i in range(c0 // quantum, (c1 + quantum - 1) // quantum)]


def stk(c):
    return ("st", c // 16)


def gkeys(c0, c1):
    return [("G", i) for i in range(c0 // 256, (c1 + 255) // 256)]


_STATIC_CONSTS = None


def _static_consts():
    global _STATIC_CONSTS
    if _STATIC_CONSTS is None:
        _STATIC_CONSTS = {
            "c_ident": np.eye(P, dtype=ml_dtypes.bfloat16),
            "c_triu": np.triu(np.ones((P, P), dtype=np.float32)),
            "c_mstrict": (-1e30 * np.tril(np.ones((P, P), dtype=np.float32), -1)).astype(np.float32),
            "c_ones": np.ones((P, P), dtype=np.float32),
            "c_onesrow": np.ones((1, P), dtype=ml_dtypes.bfloat16),
            "c_id8": np.eye(8, dtype=np.float32),
        }
    return _STATIC_CONSTS


def _mk_core_consts(core):
    m2 = np.zeros((8, 8), dtype=np.float32)
    for cpp in range(8):
        for r in range(8):
            if r < cpp < core:
                m2[cpp, r] = 1.0
    m2exp = np.tile(m2.T.reshape(1, 64), (64, 1)).astype(np.float32)
    mask1 = np.zeros((64, 8), dtype=np.float32)
    mask1[:, :core] = 1.0
    return m2exp, mask1


def build_program(debug=(), stages=("ffn0", "dn", "ffn1")):
    nc = bass.Bass("TRN2", target_bir_lowering=False, debug=False)
    dbg = set(debug)
    B = Builder()

    def din(name, shape, dt=F32):
        return nc.dram_tensor(name, shape, dt, kind="ExternalInput").ap()

    x_in = din("x", [NTOK, D])
    qkv_wt = din("qkv_wt", [D, 3072])
    proj_wt = din("proj_wt", [D, D])
    router_wt = din("router_wt", [NE, D])
    ln1_g = din("ln1_g", [D]); ln1_b = din("ln1_b", [D])
    ln2_g = din("ln2_g", [D]); ln2_b = din("ln2_b", [D])
    moe_w1 = din("moe_w1", [NE, D, FFN]).rearrange("e r c -> (e r) c")
    moe_w2 = din("moe_w2", [NE, FFN, D]).rearrange("e r c -> (e r) c")
    wq_d = din("wq", [D, D]); wk_d = din("wk", [D, D]); wv_d = din("wv", [D, D])
    bq_d = din("bq", [D]); bk_d = din("bk", [D]); bv_d = din("bv", [D])
    wab_d = din("wab", [D, 32]); bab_d = din("bab", [32])
    c_ident = din("c_ident", [P, P], BF16)
    c_triu = din("c_triu", [P, P])
    c_mstrict = din("c_mstrict", [P, P])
    c_ones = din("c_ones", [P, P])
    c_onesrow = din("c_onesrow", [1, P], BF16)
    c_id8 = din("c_id8", [8, 8])
    c_m2exp = din("c_m2exp", [64, 64])
    c_mask1 = din("c_mask1", [64, 8])

    y_out = nc.dram_tensor("y", [NTOK, D], F32, kind="ExternalOutput").ap()
    st_exp_h = nc.dram_tensor("st_exp", [64 * P, HD], BF16)
    st_gath_h = nc.dram_tensor("st_gath", [NCORE * 64 * P, HD], BF16)
    cs_exp_h = nc.dram_tensor("cs_exp", [64, 1], F32)
    cs_gath_h = nc.dram_tensor("cs_gath", [NCORE * 64, 1], F32)
    sclT_dram = nc.dram_tensor("sclT_d", [64, 8], F32)
    wcaches = [nc.dram_tensor(f"wcache{i}", [72 * 1024 * 1024], BF16).ap()
               for i in range(3)]

    es = ExitStack()

    def sb(name, pdim, cols, dt):
        return es.enter_context(nc.sbuf_tensor(name, [pdim, cols], dt))

    def mkpsum(name, cols, dt):
        return es.enter_context(nc.psum_tensor(name, [P, cols], dt))

    def mksem(name):
        return Sem(es.enter_context(nc.semaphore(name)))

    s_x = sb("s_x", P, NT * D, F32)
    s_A = sb("s_A", P, 16384, BF16)
    s_Bf = sb("s_Bf", P, 16384, BF16)
    s_C = sb("s_C", P, 16384, BF16)
    s_D = sb("s_D", P, 6656, BF16)
    s_E = sb("s_E", P, 8192, BF16)
    s_G = sb("s_G", P, 768, F32)
    s_stage = sb("s_stage", P, 1024, F32)
    s_cvt = sb("s_cvt", P, 1024, BF16)
    s_stats = sb("s_stats", P, 128, F32)
    s_gam = sb("s_gam", P, 4 * KD, F32)
    s_bq = sb("s_bq", P, KD, F32)
    s_gates = sb("s_gates", P, NT * 8, F32)
    s_brow = sb("s_brow", 1, 544, BF16)
    s_eps1 = sb("s_eps1", P, 1, F32)
    s_eps2 = sb("s_eps2", P, 1, F32)
    s_eps3 = sb("s_eps3", P, 1, F32)
    s_zero = sb("s_zero", P, 1, F32)
    s_one = sb("s_one", P, 1, F32)
    s_id = sb("s_id", P, P, BF16)
    s_triu = sb("s_triu", P, P, F32)
    s_mst = sb("s_mst", P, P, F32)
    s_onesf = sb("s_onesf", P, P, F32)
    s_onesrow = sb("s_onesrow", 1, P, BF16)
    s_router = sb("s_router", P, KD * 8, BF16)
    s_la = sb("s_la", P, NT * NH, F32)
    s_beta = sb("s_beta", P, NT * NH, F32)
    s_cla = sb("s_cla", P, NT * NH, F32)
    s_ncla = sb("s_ncla", P, NT * NH, F32)
    s_td0 = sb("s_td0", P, 64, F32)
    s_ces = sb("s_ces", P, 128, F32)
    s_csum = sb("s_csum", P, 64, F32)
    s_ab = sb("s_ab", P, KD * 32, BF16)
    s_id8 = sb("s_id8", 8, 8, F32)
    s_m2exp = sb("s_m2exp", 64, 64, F32)
    s_mask1 = sb("s_mask1", 64, 8, F32)
    s_csg = sb("s_csg", 64, 8, F32)
    s_L = sb("s_L", 64, 8, F32)
    s_scl = sb("s_scl", 64, 8, F32)
    s_sclT = sb("s_sclT", 8, 64, F32)

    pb = [mkpsum(f"pb{i}", 512, F32) for i in range(6)]
    ptA = mkpsum("ptA", 1024, BF16)
    ptB = mkpsum("ptB", 1024, BF16)
    PTN = {id(ptA): "ptA", id(ptB): "ptB"}

    csem = {e: mksem("c_" + e) for e in ENGS}
    dsem = {e: mksem("d_" + e) for e in ENGS}
    ccsem = mksem("cc")

    def OP(eng, fn, reads=(), writes=(), kind="c", tag="", big=False):
        return B.op(eng, fn, reads, writes, kind, tag, big)

    def DMA(eng, out, in_, reads, writes, tag="", slow=False):
        return OP(eng, lambda q: q.dma_start(out=out, in_=in_,
                                             allow_slow_non_contiguous=slow),
                  reads, writes, "d", tag)

    def pkey(pt_, c0=0):
        nm = PTN.get(id(pt_), pt_)
        return ("ps", nm, c0 // 512)

    bank_rot = [0]

    def next_bank(lo=0, hi=4):
        b = bank_rot[0] % (hi - lo) + lo
        bank_rot[0] += 1
        return b

    tr_rot = [0]

    def next_tr():
        t = tr_rot[0] % 4
        tr_rot[0] += 1
        return (ptA if t < 2 else ptB), (t % 2) * P

    dbg_ctr = [0]

    def dump(name, src_ap, read_keys):
        if name not in dbg:
            return
        shape = list(src_ap.shape)
        ap = nc.dram_tensor("dbg_" + name, shape, src_ap.dtype, kind="ExternalOutput").ap()
        DMA("gpsimd", ap, src_ap, list(read_keys), [("dbgsink", dbg_ctr[0])], tag="dbg_" + name)
        dbg_ctr[0] += 1

    GK = [("G", 0), ("G", 1), ("G", 2)]

    for r in range(NT):
        DMA("sync", s_x[:, r * D:(r + 1) * D], x_in[r * P:(r + 1) * P, :], [], [("x", r)])
    DMA("sync", s_id[:], c_ident, [], [("id",)])
    DMA("sync", s_triu[:], c_triu, [], [("triu",)])
    DMA("sync", s_mst[:], c_mstrict, [], [("mst",)])
    DMA("sync", s_onesf[:], c_ones, [], [("onesf",)])
    DMA("sync", s_onesrow[:], c_onesrow, [], [("onesrow",)])
    DMA("sync", s_id8[:], c_id8, [], [("id8",)])
    DMA("sync", s_m2exp[:], c_m2exp, [], [("m2exp",)])
    DMA("sync", s_mask1[:], c_mask1, [], [("mask1",)])
    for i, t in enumerate([ln1_g, ln1_b, ln2_g, ln2_b]):
        DMA("sync", s_gam[:, i * KD:(i + 1) * KD], t.rearrange("(f p) -> p f", p=P),
            [], [("gam", i)], slow=True)
    DMA("sync", s_bq[:], bq_d.rearrange("(f p) -> p f", p=P), [], [("bq",)], slow=True)
    DMA("sync", s_G[0:1, 0:32], bab_d.rearrange("(one d) -> one d", one=1), GK, GK, tag="babl")
    OP("vector", lambda q: q.tensor_copy(s_brow[0:1, 0:32], s_G[0:1, 0:32]),
       GK, [("brow", 0)], tag="babc")
    for hf in range(4):
        DMA("sync", s_G[0:8, 0:512], router_wt[:, hf * 512:(hf + 1) * 512], GK, GK,
            tag=f"routl{hf}")
        OP("vector", lambda q, hf=hf: q.tensor_copy(
            s_D[0:8, hf * 512:(hf + 1) * 512], s_G[0:8, 0:512]),
           GK, [("D", hf // 2)], tag=f"routc{hf}")
    for f in range(KD):
        pslot, pc = next_tr()
        pk = pkey(pslot, pc)
        OP("tensor", lambda q, f=f, pslot=pslot, pc=pc:
           q.transpose(pslot[:, pc:pc + 8], s_D[0:8, f * P:(f + 1) * P], s_id[0:8, 0:8]),
           [("D", 0), ("D", 1), ("id",)], [pk], tag=f"rtr{f}")
        OP("vector", lambda q, f=f, pslot=pslot, pc=pc:
           q.tensor_copy(s_router[:, f * 8:(f + 1) * 8], pslot[:, pc:pc + 8]),
           [pk], [("router",)], tag=f"rtrev{f}")
    for i in range(KD):
        DMA("sync", s_G[0:P, 0:32], wab_d[i * P:(i + 1) * P, :], GK, GK, tag=f"wabl{i}")
        OP("vector", lambda q, i=i: q.tensor_copy(s_ab[:, i * 32:(i + 1) * 32], s_G[:, 0:32]),
           GK, [("ab", i)], tag=f"wabc{i}")

    def f_eps(q):
        q.memset(s_eps1[:], EPS)
        q.memset(s_eps2[:], 1e-6)
        q.memset(s_eps3[:], 1e-24)
        q.memset(s_zero[:], 0.0)
        return q.memset(s_one[:], 1.0)
    OP("gpsimd", f_eps, [], [("eps",)])

    def ln_stats(src_ap, src_reads, so, tag=""):
        OP("vector", lambda q: q.tensor_reduce(out=s_stats[:, so:so + 1], in_=src_ap,
                                               axis=AX.X, op=ALU.add),
           src_reads, [stk(so), ("lnser",)], tag=tag + "s")
        for c4 in range(4):
            slot = 0
            OP("scalar", lambda q, c4=c4, slot=slot: q.activation(
                out=s_G[:, slot:slot + 512], in_=src_ap[:, c4 * 512:(c4 + 1) * 512],
                func=AF.Square),
               src_reads + gkeys(slot, slot + 512), gkeys(slot, slot + 512) + [("lnser",)],
               tag=tag + f"q{c4}")
            OP("vector", lambda q, c4=c4, slot=slot: q.tensor_reduce(
                out=s_stats[:, so + 16 + c4:so + 17 + c4], in_=s_G[:, slot:slot + 512],
                axis=AX.X, op=ALU.add),
               gkeys(slot, slot + 512), [stk(so + 16 + c4), ("lnser",)], tag=tag + f"r{c4}")

        OP("vector", lambda q: q.tensor_reduce(
            out=s_stats[:, so + 2:so + 3], in_=s_stats[:, so + 16:so + 20],
            axis=AX.X, op=ALU.add),
           [stk(so + 16), stk(so + 17), stk(so + 18), stk(so + 19)], [stk(so + 2)],
           tag=tag + "m0")
        OP("vector", lambda q: q.tensor_scalar(
            out=s_stats[:, so:so + 1], in0=s_stats[:, so:so + 1],
            scalar1=1.0 / D, scalar2=None, op0=ALU.mult),
           [stk(so)], [stk(so)], tag=tag + "m1")
        OP("vector", lambda q: q.tensor_tensor(
            out=s_stats[:, so + 3:so + 4], in0=s_stats[:, so:so + 1],
            in1=s_stats[:, so:so + 1], op=ALU.mult),
           [stk(so)], [stk(so + 3)], tag=tag + "m2")
        OP("vector", lambda q: q.tensor_scalar(
            out=s_stats[:, so + 2:so + 3], in0=s_stats[:, so + 2:so + 3],
            scalar1=1.0 / D, scalar2=None, op0=ALU.mult),
           [stk(so + 2)], [stk(so + 2)], tag=tag + "m3")
        OP("vector", lambda q: q.tensor_tensor(
            out=s_stats[:, so + 1:so + 2], in0=s_stats[:, so + 2:so + 3],
            in1=s_stats[:, so + 3:so + 4], op=ALU.subtract),
           [stk(so + 2), stk(so + 3)], [stk(so + 1)], tag=tag + "m4")
        OP("scalar", lambda q: q.activation(
            out=s_stats[:, so + 1:so + 2], in_=s_stats[:, so + 1:so + 2],
            func=AF.Ln, bias=s_eps1[:]),
           [stk(so + 1), ("eps",)], [stk(so + 1)], tag=tag + "r0")
        OP("scalar", lambda q: q.activation(
            out=s_stats[:, so + 1:so + 2], in_=s_stats[:, so + 1:so + 2],
            func=AF.Exp, scale=-0.5),
           [stk(so + 1)], [stk(so + 1)], tag=tag + "r1")

    stage_rot = [0]
    CVT = ["vector", "gpsimd", "scalar"]
    wc_alloc = [0]
    wc_blocks = {}
    WCCAP = 72 * 1024 * 1024
    ldq_rot = [0]

    def stream_win(name, src_fn, nseg, seg_cols, dst_ap, dst_keys, tag, tap=True,
                   cvt_eng="vector"):
        dst_keys = list(dst_keys)
        cols = nseg * seg_cols
        if name is not None and name in wc_blocks:
            off = wc_blocks[name]
            ti, to = divmod(off, WCCAP)
            blk = wcaches[ti][to:to + P * cols].rearrange("(p c) -> p c", p=P)
            OP("sync", lambda q: q.dma_start(out=dst_ap, in_=blk),
               [("wc", name)] + dst_keys, dst_keys, "d", tag + "R")
            return
        for i in range(nseg):
            srot = stage_rot[0] % 2
            stage_rot[0] += 1
            st_ap = s_stage[:, srot * 512:srot * 512 + seg_cols]
            ldq = "sync"

            def fld(q, st_ap=st_ap, i=i):
                s3 = src_fn(i)
                d = st_ap
                if len(s3.shape) == 3:
                    d = st_ap.rearrange("p (k f) -> p k f", f=s3.shape[2])
                return q.dma_start(out=d, in_=s3)
            OP(ldq, fld, [("stage", srot)], [("stage", srot)], "d", tag + f"l{i}")

            def fcv(q, st_ap=st_ap, i=i, seg_cols=seg_cols):
                d = dst_ap[:, i * seg_cols:(i + 1) * seg_cols]
                if hasattr(q, "tensor_copy"):
                    return q.tensor_copy(d, st_ap)
                return q.activation(out=d, in_=st_ap, func=AF.Copy)
            OP(cvt_eng, fcv, [("stage", srot)] + dst_keys, dst_keys, tag=tag + f"c{i}")
        if name is not None and tap:
            if name not in wc_blocks:
                need = P * cols
                if wc_alloc[0] % WCCAP + need > WCCAP:
                    wc_alloc[0] = (wc_alloc[0] // WCCAP + 1) * WCCAP
                wc_blocks[name] = wc_alloc[0]
                wc_alloc[0] += need
            off = wc_blocks[name]
            ti, to = divmod(off, WCCAP)
            blk = wcaches[ti][to:to + P * cols].rearrange("(p c) -> p c", p=P)
            OP("gpsimd", lambda q: q.dma_start(out=blk, in_=dst_ap),
               dst_keys, [("wc", name)], "d", tag + "T")

    equarter = [0]

    def stream_q(name, src_fn, tag, cvt_eng="vector", tap=True):
        base = (equarter[0] % 2) * 4096
        equarter[0] += 1
        stream_win(name, src_fn, 8, 512, s_E[:, base:base + 4096],
                   rkeys("E", base, base + 4096), tag, tap=tap, cvt_eng=cvt_eng)
        return base

    def ffn_call(call):
        tg = f"f{call}"
        akeys = lambda r: rkeys("A", r * D, (r + 1) * D)
        bkeys_all = rkeys("B", 0, 16384)
        ckeys_all = rkeys("C", 0, 16384)

        for r in range(NT):
            so = (r % 2) * 32
            if r == 0:
                dump(f"xin{call}", s_x[:, 0:D], [("x", 0)])
            ln_stats(s_x[:, r * D:(r + 1) * D], [("x", r)], so, tag=f"{tg}ol{r}")
            if r == 0:
                dump(f"st{call}", s_stats[:, 0:32], [stk(0), stk(16)])
            OP("vector", lambda q, r=r, so=so: q.tensor_scalar(
                out=s_A[:, r * D:(r + 1) * D], in0=s_x[:, r * D:(r + 1) * D],
                scalar1=s_stats[:, so:so + 1], scalar2=s_stats[:, so + 1:so + 2],
                op0=ALU.subtract, op1=ALU.mult),
               [stk(so), stk(so + 1), ("x", r)], akeys(r), big=True, tag=f"{tg}lx{r}")

            OP("vector", lambda q, so=so: q.tensor_scalar(
                out=s_stats[:, so + 6:so + 7], in0=s_stats[:, so + 1:so + 2],
                scalar1=0.5, scalar2=1.0, op0=ALU.mult, op1=ALU.add),
               [stk(so + 1)], [stk(so + 6)], tag=f"{tg}xu0{r}")
            OP("vector", lambda q, so=so: q.tensor_tensor(
                out=s_stats[:, so + 7:so + 8], in0=s_stats[:, so:so + 1],
                in1=s_stats[:, so + 1:so + 2], op=ALU.mult),
               [stk(so), stk(so + 1)], [stk(so + 7)], tag=f"{tg}xu1{r}")
            OP("vector", lambda q, so=so: q.tensor_scalar(
                out=s_stats[:, so + 7:so + 8], in0=s_stats[:, so + 7:so + 8],
                scalar1=0.5, scalar2=None, op0=ALU.mult),
               [stk(so + 7)], [stk(so + 7)], tag=f"{tg}xu2{r}")
            OP("vector", lambda q, r=r, so=so: q.tensor_scalar(
                out=s_x[:, r * D:(r + 1) * D], in0=s_x[:, r * D:(r + 1) * D],
                scalar1=s_stats[:, so + 6:so + 7], scalar2=s_stats[:, so + 7:so + 8],
                op0=ALU.mult, op1=ALU.subtract),
               [stk(so + 6), stk(so + 7), ("x", r)], [("x", r)], big=True, tag=f"{tg}xu3{r}")

            so1 = 64 + (r % 2) * 32
            ln_stats(s_A[:, r * D:(r + 1) * D], akeys(r), so1, tag=f"{tg}l1{r}")
            OP("vector", lambda q, r=r, so1=so1: q.tensor_scalar(
                out=s_C[:, 0:D], in0=s_A[:, r * D:(r + 1) * D],
                scalar1=s_stats[:, so1:so1 + 1], scalar2=s_stats[:, so1 + 1:so1 + 2],
                op0=ALU.subtract, op1=ALU.mult),
               [stk(so1), stk(so1 + 1)] + akeys(r), rkeys("C", 0, D), big=True, tag=f"{tg}1a{r}")

            for f in range(KD):
                pslot, pc = next_tr()
                pk = pkey(pslot, pc)
                OP("tensor", lambda q, f=f, pslot=pslot, pc=pc:
                   q.transpose(pslot[:, pc:pc + P], s_C[:, f * P:(f + 1) * P], s_id[:]),
                   rkeys("C", 0, D) + [("id",)], [pk], tag=f"{tg}t1{r}_{f}")
                OP("vector", lambda q, r=r, f=f, pslot=pslot, pc=pc: q.tensor_scalar(
                    out=s_Bf[:, f * NTOK + r * P:f * NTOK + (r + 1) * P],
                    in0=pslot[:, pc:pc + P],
                    scalar1=s_gam[:, f:f + 1], scalar2=s_gam[:, KD + f:KD + f + 1],
                    op0=ALU.mult, op1=ALU.add),
                   [pk, ("gam", 0), ("gam", 1)], [("B", (f * NTOK + r * P) // 1024)],
                   tag=f"{tg}e1{r}_{f}")

        dump(f"lnx{call}", s_A[:, :], [("A", i) for i in range(16)])
        dump(f"ln1fm{call}", s_Bf[:, :], bkeys_all)

        pairs = [(s_, t_) for s_ in range(NB) for t_ in range(s_ + 1)]
        SC0, WS0, SM0 = 0, 80, 160
        for g in range(NKV):
            qba = stream_q(f"qkvq{g}a", lambda i, g=g: qkv_wt[i * P:(i + 1) * P,
                                                              768 * g:768 * g + 512],
                           f"{tg}qwa{g}", cvt_eng="gpsimd")
            qbb = stream_q(f"qkvq{g}b", lambda i, g=g: qkv_wt[(i + 8) * P:(i + 9) * P,
                                                              768 * g:768 * g + 512],
                           f"{tg}qwb{g}", cvt_eng="gpsimd")
            ekeys = rkeys("E", qba, qba + 4096) + rkeys("E", qbb, qbb + 4096)
            for r in range(NT):
                bank = next_bank()
                def f_qmm(q, r=r, bank=bank, qba=qba, qbb=qbb):
                    for kk in range(KD):
                        eb = (qba if kk < 8 else qbb) + (kk % 8) * 512
                        mm = q.matmul(pb[bank][:, 0:512],
                                      s_Bf[:, kk * NTOK + r * P:kk * NTOK + (r + 1) * P],
                                      s_E[:, eb:eb + 512],
                                      start=(kk == 0), stop=(kk == KD - 1))
                    return mm
                OP("tensor", f_qmm, bkeys_all + ekeys, [pkey(bank)], tag=f"{tg}qm{g}_{r}")
                OP("scalar", lambda q, r=r, bank=bank: q.activation(
                    out=s_D[:, r * 768:r * 768 + 512], in_=pb[bank][:, 0:512], func=AF.Copy),
                   [pkey(bank)], rkeys("D", r * 768, r * 768 + 512), tag=f"{tg}qe{g}_{r}")
            kbb = stream_q(f"qkvk{g}", lambda i, g=g: qkv_wt[i * 256:(i + 1) * 256,
                                                             768 * g + 512:768 * (g + 1)].rearrange(
                               "(k p) f -> p k f", p=P),
                           f"{tg}kw{g}", cvt_eng="gpsimd")
            ekeys2 = rkeys("E", kbb, kbb + 4096)
            for r in range(NT):
                bank = next_bank()
                def f_kvmm(q, r=r, bank=bank, kbb=kbb):
                    for kk in range(KD):
                        mm = q.matmul(pb[bank][:, 0:256],
                                      s_Bf[:, kk * NTOK + r * P:kk * NTOK + (r + 1) * P],
                                      s_E[:, kbb + kk * 256:kbb + (kk + 1) * 256],
                                      start=(kk == 0), stop=(kk == KD - 1))
                    return mm
                OP("tensor", f_kvmm, bkeys_all + ekeys2, [pkey(bank)], tag=f"{tg}km{g}_{r}")
                OP("scalar", lambda q, r=r, bank=bank: q.activation(
                    out=s_D[:, r * 768 + 512:r * 768 + 768], in_=pb[bank][:, 0:256],
                    func=AF.Copy),
                   [pkey(bank)], rkeys("D", r * 768 + 512, r * 768 + 768), tag=f"{tg}ke{g}_{r}")

            dall = rkeys("D", 0, 6144)
            for pi, (s_, t_) in enumerate(pairs):
                for half in range(2):
                    rq = 2 * s_ + half
                    rk = 2 * t_ + half
                    def f_pr(q, rq=rq, rk=rk):
                        qv = s_D[:, rq * 768:rq * 768 + 512].rearrange("p (h d) -> p h d", h=4)
                        kv = s_D[:, rk * 768 + 512:rk * 768 + 640].rearrange(
                            "p (one d) -> p one d", one=1).broadcast_to([P, 4, HD])
                        pr = s_D[:, 6144:6656].rearrange("p (h d) -> p h d", h=4)
                        return q.tensor_tensor(out=pr, in0=qv, in1=kv, op=ALU.mult)
                    OP("vector", f_pr, dall, rkeys("D", 6144, 6656), tag=f"{tg}p{g}_{pi}_{half}")
                    sc = SC0 + half * 40 + pi * 4
                    OP("vector", lambda q, sc=sc: q.tensor_reduce(
                        out=s_G[:, sc:sc + 4],
                        in_=s_D[:, 6144:6656].rearrange("p (h d) -> p h d", h=4),
                        axis=AX.X, op=ALU.add),
                       rkeys("D", 6144, 6656) + GK, GK + [("sc", sc)], tag=f"{tg}r{g}_{pi}_{half}")

            screads = [("sc", SC0 + half * 40 + pi * 4) for half in range(2) for pi in range(10)]

            SMK = screads + GK + [("wsm",)]
            for half in range(2):
                base = SC0 + half * 40
                wbase = WS0 + half * 40
                for s_ in range(1, NB):
                    p0 = s_ * (s_ + 1) // 2
                    OP("vector", lambda q, base=base, p0=p0: q.tensor_copy(
                        s_G[:, SM0:SM0 + 4], s_G[:, base + p0 * 4:base + p0 * 4 + 4]),
                       SMK, GK + [("wsm",)], tag=f"{tg}sa{g}_{half}_{s_}")
                    for t_ in range(1, s_ + 1):
                        OP("vector", lambda q, base=base, p0=p0, t_=t_: q.tensor_tensor(
                            out=s_G[:, SM0:SM0 + 4], in0=s_G[:, SM0:SM0 + 4],
                            in1=s_G[:, base + (p0 + t_) * 4:base + (p0 + t_) * 4 + 4],
                            op=ALU.max),
                           SMK, GK + [("wsm",)], tag=f"{tg}sb{g}_{half}_{s_}_{t_}")

                    def f_subs(q, base=base, wbase=wbase, p0=p0, s_=s_):
                        for t_ in range(s_ + 1):
                            last = q.tensor_tensor(
                                out=s_G[:, wbase + (p0 + t_) * 4:wbase + (p0 + t_) * 4 + 4],
                                in0=s_G[:, base + (p0 + t_) * 4:base + (p0 + t_) * 4 + 4],
                                in1=s_G[:, SM0:SM0 + 4], op=ALU.subtract)
                        return last
                    OP("vector", f_subs, SMK, GK + [("wsm",)], tag=f"{tg}sc{g}_{half}_{s_}")

            def f_exp(q):
                for half in range(2):
                    wbase = WS0 + half * 40
                    for s_ in range(1, NB):
                        p0 = s_ * (s_ + 1) // 2
                        for t_ in range(s_ + 1):
                            c0 = wbase + (p0 + t_) * 4
                            last = q.activation(out=s_G[:, c0:c0 + 4], in_=s_G[:, c0:c0 + 4],
                                                func=AF.Exp, scale=RS)
                return last
            OP("scalar", f_exp, SMK, GK + [("wsm",)], tag=f"{tg}se{g}")

            for half in range(2):
                wbase = WS0 + half * 40
                for s_ in range(1, NB):
                    p0 = s_ * (s_ + 1) // 2
                    cols = [wbase + (p0 + t_) * 4 for t_ in range(s_ + 1)]
                    OP("vector", lambda q, cols=cols: q.tensor_copy(
                        s_G[:, SM0:SM0 + 4], s_G[:, cols[0]:cols[0] + 4]),
                       SMK, GK + [("wsm",)], tag=f"{tg}sd{g}_{half}_{s_}")
                    for t_ in range(1, s_ + 1):
                        OP("vector", lambda q, cols=cols, t_=t_: q.tensor_tensor(
                            out=s_G[:, SM0:SM0 + 4], in0=s_G[:, SM0:SM0 + 4],
                            in1=s_G[:, cols[t_]:cols[t_] + 4], op=ALU.add),
                           SMK, GK + [("wsm",)], tag=f"{tg}sf{g}_{half}_{s_}_{t_}")
                    OP("vector", lambda q: q.reciprocal(s_G[:, SM0 + 4:SM0 + 8],
                                                        s_G[:, SM0:SM0 + 4]),
                       SMK, GK + [("wsm",)], tag=f"{tg}sg2{g}_{half}_{s_}")

                    def f_mul(q, cols=cols, s_=s_):
                        for t_ in range(s_ + 1):
                            last = q.tensor_tensor(out=s_G[:, cols[t_]:cols[t_] + 4],
                                                   in0=s_G[:, cols[t_]:cols[t_] + 4],
                                                   in1=s_G[:, SM0 + 4:SM0 + 8], op=ALU.mult)
                        return last
                    OP("vector", f_mul, SMK, GK + [("wsm",)], tag=f"{tg}sh{g}_{half}_{s_}")

            for half in range(2):
                for s_ in range(NB):
                    rt = 2 * s_ + half
                    ckeys = rkeys("C", rt * D + g * 512, rt * D + (g + 1) * 512)
                    for t_ in range(s_ + 1):
                        rk = 2 * t_ + half
                        pi = s_ * (s_ + 1) // 2 + t_
                        for j in range(4):
                            wcol = WS0 + half * 40 + pi * 4 + j
                            yap = s_C[:, rt * D + (4 * g + j) * HD:rt * D + (4 * g + j + 1) * HD]
                            vap = s_D[:, rk * 768 + 640:rk * 768 + 768]
                            if s_ == 0:
                                OP("vector", lambda q, yap=yap, vap=vap: q.tensor_copy(yap, vap),
                                   dall, ckeys, tag=f"{tg}w0{g}_{half}_{j}")
                            elif t_ == 0:
                                OP("vector", lambda q, yap=yap, vap=vap, wcol=wcol:
                                   q.tensor_scalar(out=yap, in0=vap,
                                                   scalar1=s_G[:, wcol:wcol + 1],
                                                   scalar2=None, op0=ALU.mult),
                                   dall + [("wsm",)] + GK, ckeys, tag=f"{tg}wf{g}_{s_}_{half}_{j}")
                            else:
                                OP("vector", lambda q, yap=yap, vap=vap, wcol=wcol:
                                   q.scalar_tensor_tensor(
                                       out=yap, in0=vap, scalar=s_G[:, wcol:wcol + 1],
                                       in1=yap, op0=ALU.mult, op1=ALU.add),
                                   dall + [("wsm",)] + GK + ckeys, ckeys,
                                   tag=f"{tg}wa{g}_{s_}_{t_}_{half}_{j}")

        dump(f"yattn{call}", s_C[:, :], ckeys_all)

        for r in range(NT):
            for f in range(KD):
                pslot, pc = next_tr()
                pk = pkey(pslot, pc)
                OP("tensor", lambda q, r=r, f=f, pslot=pslot, pc=pc:
                   q.transpose(pslot[:, pc:pc + P], s_C[:, r * D + f * P:r * D + (f + 1) * P],
                               s_id[:]),
                   rkeys("C", r * D + f * P, r * D + (f + 1) * P) + [("id",)], [pk],
                   tag=f"{tg}yt{r}_{f}")
                OP("vector", lambda q, r=r, f=f, pslot=pslot, pc=pc:
                   q.tensor_copy(s_Bf[:, f * NTOK + r * P:f * NTOK + (r + 1) * P],
                                 pslot[:, pc:pc + P]),
                   [pk], [("B", (f * NTOK + r * P) // 1024)], tag=f"{tg}ye{r}_{f}")

        for win in range(4):
            pba = stream_q(f"proj{win}a", lambda i, win=win: proj_wt[i * P:(i + 1) * P,
                                                                     win * 512:(win + 1) * 512],
                           f"{tg}pwa{win}", cvt_eng="gpsimd")
            pbb = stream_q(f"proj{win}b", lambda i, win=win: proj_wt[(i + 8) * P:(i + 9) * P,
                                                                    win * 512:(win + 1) * 512],
                           f"{tg}pwb{win}", cvt_eng="gpsimd")
            ekeys = rkeys("E", pba, pba + 4096) + rkeys("E", pbb, pbb + 4096)
            for r in range(NT):
                bank = next_bank()
                def f_pmm(q, r=r, bank=bank, pba=pba, pbb=pbb):
                    for kk in range(KD):
                        eb = (pba if kk < 8 else pbb) + (kk % 8) * 512
                        mm = q.matmul(pb[bank][:, 0:512],
                                      s_Bf[:, kk * NTOK + r * P:kk * NTOK + (r + 1) * P],
                                      s_E[:, eb:eb + 512],
                                      start=(kk == 0), stop=(kk == KD - 1))
                    return mm
                OP("tensor", f_pmm, bkeys_all + ekeys, [pkey(bank)], tag=f"{tg}pm{win}_{r}")
                xc = r * D + win * 512
                OP("vector", lambda q, bank=bank, xc=xc, r=r: q.scalar_tensor_tensor(
                    out=s_x[:, xc:xc + 512], in0=pb[bank][:, 0:512], scalar=0.5,
                    in1=s_x[:, xc:xc + 512], op0=ALU.mult, op1=ALU.add),
                   [pkey(bank), ("x", r)], [("x", r)], big=True, tag=f"{tg}px{win}_{r}")
                OP("vector", lambda q, bank=bank, xc=xc: q.scalar_tensor_tensor(
                    out=s_A[:, xc:xc + 512], in0=pb[bank][:, 0:512], scalar=1.0,
                    in1=s_A[:, xc:xc + 512], op0=ALU.mult, op1=ALU.add),
                   [pkey(bank)] + rkeys("A", xc, xc + 512), rkeys("A", xc, xc + 512),
                   big=True, tag=f"{tg}pa{win}_{r}")

        dump(f"hba{call}", s_A[:, :], [("A", i) for i in range(16)])

        for r in range(NT):
            so2 = (r % 2) * 32
            ln_stats(s_A[:, r * D:(r + 1) * D], akeys(r), so2, tag=f"{tg}l2{r}")
            OP("vector", lambda q, r=r, so2=so2: q.tensor_scalar(
                out=s_D[:, 0:D], in0=s_A[:, r * D:(r + 1) * D],
                scalar1=s_stats[:, so2:so2 + 1], scalar2=s_stats[:, so2 + 1:so2 + 2],
                op0=ALU.subtract, op1=ALU.mult),
               [stk(so2), stk(so2 + 1)] + akeys(r), rkeys("D", 0, D), big=True, tag=f"{tg}2a{r}")
            for f in range(KD):
                pslot, pc = next_tr()
                pk = pkey(pslot, pc)
                OP("tensor", lambda q, f=f, pslot=pslot, pc=pc:
                   q.transpose(pslot[:, pc:pc + P], s_D[:, f * P:(f + 1) * P], s_id[:]),
                   rkeys("D", 0, D) + [("id",)], [pk], tag=f"{tg}t2{r}_{f}")
                OP("vector", lambda q, r=r, f=f, pslot=pslot, pc=pc: q.tensor_scalar(
                    out=s_C[:, f * NTOK + r * P:f * NTOK + (r + 1) * P],
                    in0=pslot[:, pc:pc + P],
                    scalar1=s_gam[:, 2 * KD + f:2 * KD + f + 1],
                    scalar2=s_gam[:, 3 * KD + f:3 * KD + f + 1],
                    op0=ALU.mult, op1=ALU.add),
                   [pk, ("gam", 2), ("gam", 3)], [("C", (f * NTOK + r * P) // 1024)],
                   tag=f"{tg}e2{r}_{f}")

        dump(f"xmoe{call}", s_C[:, :], ckeys_all)

        for r in range(NT):
            bank = next_bank(4, 6)
            def f_rmm(q, r=r, bank=bank):
                for kk in range(KD):
                    mm = q.matmul(pb[bank][:, 0:8],
                                  s_C[:, kk * NTOK + r * P:kk * NTOK + (r + 1) * P],
                                  s_router[:, kk * 8:(kk + 1) * 8],
                                  start=(kk == 0), stop=(kk == KD - 1))
                return mm
            OP("tensor", f_rmm, ckeys_all + [("router",)], [pkey(bank)], tag=f"{tg}rm{r}")

            def VOP(fn, tg2):
                OP("vector", fn, GK + [pkey(bank), ("gate", r)],
                   GK + [("gate", r)], tag=tg2)
            VOP(lambda q, bank=bank: q.tensor_reduce(
                out=s_G[:, 8:9], in_=pb[bank][:, 0:8], axis=AX.X, op=ALU.max),
                f"{tg}ga{r}")
            VOP(lambda q, bank=bank: q.tensor_scalar(
                out=s_G[:, 0:8], in0=pb[bank][:, 0:8], scalar1=s_G[:, 8:9],
                scalar2=None, op0=ALU.subtract), f"{tg}gb{r}")
            OP("scalar", lambda q: q.activation(out=s_G[:, 0:8], in_=s_G[:, 0:8],
                                                func=AF.Exp),
               GK, GK, tag=f"{tg}gc{r}")
            VOP(lambda q: q.tensor_reduce(out=s_G[:, 9:10], in_=s_G[:, 0:8],
                                          axis=AX.X, op=ALU.add), f"{tg}gd{r}")
            VOP(lambda q: q.reciprocal(s_G[:, 10:11], s_G[:, 9:10]), f"{tg}ge2{r}")
            VOP(lambda q: q.tensor_scalar(out=s_G[:, 16:24], in0=s_G[:, 0:8],
                                          scalar1=s_G[:, 10:11], scalar2=None,
                                          op0=ALU.mult), f"{tg}gf{r}")
            VOP(lambda q: q.tensor_reduce(out=s_G[:, 11:12], in_=s_G[:, 16:24],
                                          axis=AX.X, op=ALU.max), f"{tg}gg{r}")
            VOP(lambda q: q.tensor_scalar(out=s_G[:, 24:32], in0=s_G[:, 16:24],
                                          scalar1=s_G[:, 11:12], scalar2=None,
                                          op0=ALU.is_lt), f"{tg}gh{r}")
            VOP(lambda q: q.tensor_tensor(out=s_G[:, 24:32], in0=s_G[:, 16:24],
                                          in1=s_G[:, 24:32], op=ALU.mult), f"{tg}gi{r}")
            VOP(lambda q: q.tensor_reduce(out=s_G[:, 12:13], in_=s_G[:, 24:32],
                                          axis=AX.X, op=ALU.max), f"{tg}gj{r}")
            VOP(lambda q: q.tensor_scalar(out=s_G[:, 24:32], in0=s_G[:, 16:24],
                                          scalar1=s_G[:, 11:12], scalar2=None,
                                          op0=ALU.is_ge), f"{tg}gk{r}")
            VOP(lambda q: q.tensor_reduce(out=s_G[:, 13:14], in_=s_G[:, 24:32],
                                          axis=AX.X, op=ALU.add), f"{tg}gl{r}")
            VOP(lambda q: q.tensor_scalar(out=s_G[:, 14:15], in0=s_G[:, 13:14],
                                          scalar1=2.0, scalar2=None, op0=ALU.is_ge),
                f"{tg}gm{r}")
            VOP(lambda q: q.tensor_tensor(out=s_G[:, 15:16], in0=s_G[:, 11:12],
                                          in1=s_G[:, 12:13], op=ALU.subtract), f"{tg}gn{r}")
            VOP(lambda q: q.tensor_tensor(out=s_G[:, 15:16], in0=s_G[:, 15:16],
                                          in1=s_G[:, 14:15], op=ALU.mult), f"{tg}go{r}")
            VOP(lambda q: q.tensor_tensor(out=s_G[:, 12:13], in0=s_G[:, 12:13],
                                          in1=s_G[:, 15:16], op=ALU.add), f"{tg}gp{r}")
            VOP(lambda q: q.tensor_scalar(out=s_G[:, 24:32], in0=s_G[:, 16:24],
                                          scalar1=s_G[:, 12:13], scalar2=None,
                                          op0=ALU.is_ge), f"{tg}gq{r}")
            VOP(lambda q: q.tensor_tensor(out=s_G[:, 8:9], in0=s_G[:, 11:12],
                                          in1=s_G[:, 12:13], op=ALU.add), f"{tg}gr{r}")
            VOP(lambda q: q.reciprocal(s_G[:, 9:10], s_G[:, 8:9]), f"{tg}gs{r}")
            VOP(lambda q: q.tensor_scalar(out=s_G[:, 9:10], in0=s_G[:, 9:10],
                                          scalar1=0.5, scalar2=None, op0=ALU.mult),
                f"{tg}gt{r}")
            VOP(lambda q, r=r: q.tensor_tensor(out=s_gates[:, r * 8:(r + 1) * 8],
                                               in0=s_G[:, 16:24], in1=s_G[:, 24:32],
                                               op=ALU.mult), f"{tg}gu{r}")
            VOP(lambda q, r=r: q.tensor_scalar(out=s_gates[:, r * 8:(r + 1) * 8],
                                               in0=s_gates[:, r * 8:(r + 1) * 8],
                                               scalar1=s_G[:, 9:10], scalar2=None,
                                               op0=ALU.mult), f"{tg}gv2{r}")

        dump(f"gates{call}", s_gates[:, :], [("gate", r) for r in range(NT)])

        def h1g_ap(m, c0, c1):
            if m < 16:
                return s_A[:, m * NTOK + c0:m * NTOK + c1]
            return s_Bf[:, (m - 16) * NTOK + c0:(m - 16) * NTOK + c1]

        def h1g_keys(m, c0, c1):
            if m < 16:
                return rkeys("A", m * NTOK + c0, m * NTOK + c1)
            return rkeys("B", (m - 16) * NTOK + c0, (m - 16) * NTOK + c1)

        h1keys = [("A", i) for i in range(16)] + [("B", i) for i in range(16)]

        for e in range(NE):
            for mg in range(16):
                wb = stream_q(f"w1_{e}_{mg}",
                              lambda i, e=e, mg=mg: moe_w1[e * D + i * 256:e * D + (i + 1) * 256,
                                                           mg * 256:(mg + 1) * 256].rearrange(
                                  "(k p) f -> p k f", p=P),
                              f"{tg}w1_{e}_{mg}", cvt_eng="vector")
                eke = rkeys("E", wb, wb + 4096)
                for m in (2 * mg, 2 * mg + 1):
                    for half in range(2):
                        bank = next_bank()
                        def f_h1(q, m=m, half=half, bank=bank, wb=wb):
                            for kk in range(KD):
                                mm = q.matmul(
                                    pb[bank][:, 0:512],
                                    s_E[:, wb + kk * 256 + (m % 2) * 128:
                                         wb + kk * 256 + (m % 2) * 128 + 128],
                                    s_C[:, kk * NTOK + half * 512:kk * NTOK + (half + 1) * 512],
                                    start=(kk == 0), stop=(kk == KD - 1))
                            return mm
                        OP("tensor", f_h1, ckeys_all + eke, [pkey(bank)],
                           tag=f"{tg}h1_{e}_{m}_{half}")
                        OP("scalar", lambda q, m=m, half=half, bank=bank: q.activation(
                            out=h1g_ap(m, half * 512, (half + 1) * 512),
                            in_=pb[bank][:, 0:512], func=AF.Gelu),
                           [pkey(bank)], h1g_keys(m, half * 512, (half + 1) * 512),
                           tag=f"{tg}ge_{e}_{m}_{half}")

            for ow in range(16):
                wb2 = stream_q(f"w2_{e}_{ow}",
                               lambda i, e=e, ow=ow: moe_w2[e * FFN + i * 512:
                                                            e * FFN + (i + 1) * 512,
                                                            ow * 128:(ow + 1) * 128].rearrange(
                                   "(k p) f -> p k f", p=P),
                               f"{tg}w2_{e}_{ow}", cvt_eng="vector")
                wke = rkeys("E", wb2, wb2 + 4096)
                for r in range(NT):
                    bank = next_bank()
                    def f_y(q, r=r, bank=bank, wb2=wb2):
                        for k2 in range(32):
                            mm = q.matmul(
                                pb[bank][:, 0:128],
                                h1g_ap(k2, r * P, (r + 1) * P),
                                s_E[:, wb2 + k2 * 128:wb2 + (k2 + 1) * 128],
                                start=(k2 == 0), stop=(k2 == 31))
                        return mm
                    OP("tensor", f_y, h1keys + wke, [pkey(bank)], tag=f"{tg}y_{e}_{ow}_{r}")
                    trot = 256 + ((ow * NT + r) % 2) * 128
                    OP("scalar", lambda q, e=e, r=r, bank=bank, trot=trot: q.activation(
                        out=s_G[:, trot:trot + 128], in_=pb[bank][:, 0:128], func=AF.Copy,
                        scale=s_gates[:, r * 8 + e:r * 8 + e + 1]),
                       [pkey(bank), ("gate", r)] + gkeys(trot, trot + 128),
                       gkeys(trot, trot + 128), tag=f"{tg}gv_{e}_{ow}_{r}")
                    xc = r * D + ow * 128
                    OP("gpsimd", lambda q, xc=xc, r=r, trot=trot: q.tensor_tensor(
                        out=s_x[:, xc:xc + 128], in0=s_x[:, xc:xc + 128],
                        in1=s_G[:, trot:trot + 128], op=ALU.add),
                       gkeys(trot, trot + 128) + [("x", r)], [("x", r)],
                       tag=f"{tg}xa_{e}_{ow}_{r}")

    def deltanet():
        tg = "dn"
        akeys_all = [("A", i) for i in range(16)]

        for r in range(NT):
            so = (r % 2) * 32
            ln_stats(s_x[:, r * D:(r + 1) * D], [("x", r)], so, tag=f"{tg}l{r}")
            OP("vector", lambda q, r=r, so=so: q.tensor_scalar(
                out=s_D[:, 0:D], in0=s_x[:, r * D:(r + 1) * D],
                scalar1=s_stats[:, so:so + 1], scalar2=s_stats[:, so + 1:so + 2],
                op0=ALU.subtract, op1=ALU.mult),
               [stk(so), stk(so + 1), ("x", r)], rkeys("D", 0, D), big=True, tag=f"{tg}a{r}")
            for f in range(KD):
                pslot, pc = next_tr()
                pk = pkey(pslot, pc)
                OP("tensor", lambda q, f=f, pslot=pslot, pc=pc:
                   q.transpose(pslot[:, pc:pc + P], s_D[:, f * P:(f + 1) * P], s_id[:]),
                   rkeys("D", 0, D) + [("id",)], [pk], tag=f"{tg}t{r}_{f}")
                OP("vector", lambda q, r=r, f=f, pslot=pslot, pc=pc:
                   q.tensor_copy(s_A[:, f * NTOK + r * P:f * NTOK + (r + 1) * P],
                                 pslot[:, pc:pc + P]),
                   [pk], [("A", (f * NTOK + r * P) // 1024)], tag=f"{tg}e{r}_{f}")

        dump("lnd", s_A[:, :], akeys_all)

        abk = [("ab", i) for i in range(KD)]
        for r in range(NT):
            bank = next_bank(4, 6)
            def f_ab(q, r=r, bank=bank):
                for kk in range(KD):
                    q.matmul(pb[bank][:, 0:32],
                             s_A[:, kk * NTOK + r * P:kk * NTOK + (r + 1) * P],
                             s_ab[:, kk * 32:(kk + 1) * 32],
                             start=(kk == 0), stop=False)
                return q.matmul(pb[bank][:, 0:32], s_onesrow[:],
                                s_brow[0:1, 0:32], start=False, stop=True)
            OP("tensor", f_ab, akeys_all + abk + [("onesrow",), ("brow", 0)], [pkey(bank)],
               tag=f"{tg}ab{r}")

            OP("scalar", lambda q, bank=bank: q.activation(
                out=s_G[:, 0:16], in_=pb[bank][:, 0:16], func=AF.Sigmoid),
               [pkey(bank)] + GK, GK, tag=f"{tg}sga{r}")
            OP("scalar", lambda q, r=r: q.activation(
                out=s_la[:, r * 16:(r + 1) * 16], in_=s_G[:, 0:16],
                func=AF.Ln, bias=s_eps2[:]),
               [("eps",)] + GK, GK + [("la", r)], tag=f"{tg}sgb{r}")

            OP("vector", lambda q, r=r, bank=bank: q.tensor_scalar(
                out=s_G[:, 16:32], in0=pb[bank][:, 16:32], scalar1=s_zero[:], scalar2=None,
                op0=ALU.max),
               [pkey(bank), ("eps",)] + GK, GK, tag=f"{tg}s1{r}")

            OP("scalar", lambda q, bank=bank: q.activation(
                out=s_G[:, 32:48], in_=pb[bank][:, 16:32], func=AF.Abs),
               [pkey(bank)] + GK, GK, tag=f"{tg}s2a{r}")
            OP("scalar", lambda q: q.activation(
                out=s_G[:, 32:48], in_=s_G[:, 32:48], func=AF.Exp, scale=-1.0),
               GK, GK, tag=f"{tg}s2b{r}")
            OP("scalar", lambda q: q.activation(
                out=s_G[:, 32:48], in_=s_G[:, 32:48], func=AF.Ln, bias=s_one[:]),
               [("eps",)] + GK, GK, tag=f"{tg}s2c{r}")
            OP("vector", lambda q, r=r: q.tensor_tensor(
                out=s_beta[:, r * 16:(r + 1) * 16], in0=s_G[:, 16:32], in1=s_G[:, 32:48],
                op=ALU.add),
               GK, [("beta", r)], tag=f"{tg}bt{r}")

            bank2 = next_bank(4, 6)
            OP("tensor", lambda q, r=r, bank2=bank2: q.matmul(
                pb[bank2][:, 0:16], s_triu[:], s_la[:, r * 16:(r + 1) * 16],
                start=True, stop=True),
               [("triu",), ("la", r)], [pkey(bank2)], tag=f"{tg}cl{r}")

            def f_ce(q, r=r, bank2=bank2):
                q.tensor_copy(s_cla[:, r * 16:(r + 1) * 16], pb[bank2][:, 0:16])
                return q.tensor_scalar(out=s_ncla[:, r * 16:(r + 1) * 16],
                                       in0=pb[bank2][:, 0:16],
                                       scalar1=-1.0, scalar2=None, op0=ALU.mult)
            OP("vector", f_ce, [pkey(bank2)], [("cla", r)], tag=f"{tg}ne{r}")

        for hg in range(8):
            h0 = 2 * hg
            wqb = stream_q(None, lambda i, hg=hg: wq_d[i * 256:(i + 1) * 256,
                                                       hg * 256:(hg + 1) * 256].rearrange(
                               "(k p) f -> p k f", p=P),
                           f"{tg}wq{hg}", cvt_eng="gpsimd")
            eke = rkeys("E", wqb, wqb + 4096)
            for hp in range(2):
                h = h0 + hp
                for half in range(2):
                    bank = next_bank()
                    def f_qp(q, hp=hp, half=half, bank=bank, wqb=wqb):
                        for kk in range(KD):
                            mm = q.matmul(
                                pb[bank][:, 0:512],
                                s_E[:, wqb + kk * 256 + hp * 128:
                                     wqb + kk * 256 + hp * 128 + 128],
                                s_A[:, kk * NTOK + half * 512:kk * NTOK + (half + 1) * 512],
                                start=(kk == 0), stop=(kk == KD - 1))
                        return mm
                    OP("tensor", f_qp, akeys_all + eke, [pkey(bank)], tag=f"{tg}qp{h}_{half}")
                    OP("vector", lambda q, h=h, half=half, bank=bank: q.tensor_scalar(
                        out=s_Bf[:, h * NTOK + half * 512:h * NTOK + (half + 1) * 512],
                        in0=pb[bank][:, 0:512], scalar1=s_bq[:, h:h + 1], scalar2=None,
                        op0=ALU.add),
                       [pkey(bank), ("bq",)], [("B", (h * NTOK + half * 512) // 1024)],
                       tag=f"{tg}qe{h}_{half}")

            DMA("sync", s_G[0:1, 0:256],
                bk_d.rearrange("(one d) -> one d", one=1)[0:1, hg * 256:(hg + 1) * 256], GK, GK,
                tag=f"{tg}bkl{hg}")
            OP("vector", lambda q: q.tensor_copy(s_brow[0:1, 32:288], s_G[0:1, 0:256]),
               GK, [("brow", 1)], tag=f"{tg}bkc{hg}")
            DMA("sync", s_G[0:1, 0:256],
                bv_d.rearrange("(one d) -> one d", one=1)[0:1, hg * 256:(hg + 1) * 256], GK, GK,
                tag=f"{tg}bvl{hg}")
            OP("vector", lambda q: q.tensor_copy(s_brow[0:1, 288:544], s_G[0:1, 0:256]),
               GK, [("brow", 2)], tag=f"{tg}bvc{hg}")

            wkb = stream_q(None, lambda i, hg=hg: wk_d[i * 256:(i + 1) * 256,
                                                       hg * 256:(hg + 1) * 256].rearrange(
                               "(k p) f -> p k f", p=P),
                           f"{tg}wk{hg}", cvt_eng="gpsimd")
            eke2 = rkeys("E", wkb, wkb + 4096)
            for r in range(NT):
                bank = next_bank()
                def f_kp(q, r=r, bank=bank, wkb=wkb):
                    for kk in range(KD):
                        q.matmul(pb[bank][:, 0:256],
                                 s_A[:, kk * NTOK + r * P:kk * NTOK + (r + 1) * P],
                                 s_E[:, wkb + kk * 256:wkb + (kk + 1) * 256],
                                 start=(kk == 0), stop=False)
                    return q.matmul(pb[bank][:, 0:256], s_onesrow[:], s_brow[0:1, 32:288],
                                    start=False, stop=True)
                OP("tensor", f_kp, akeys_all + eke2 + [("onesrow",), ("brow", 1)], [pkey(bank)],
                   tag=f"{tg}kp{r}")
                OP("scalar", lambda q, r=r, bank=bank: q.activation(
                    out=s_D[:, r * 256:(r + 1) * 256], in_=pb[bank][:, 0:256], func=AF.Copy),
                   [pkey(bank)], rkeys("D", r * 256, (r + 1) * 256), tag=f"{tg}kc{r}")

                OP("vector", lambda q, r=r: q.tensor_tensor(
                    out=s_D[:, 6144:6400].rearrange("p (h d) -> p h d", h=2),
                    in0=s_D[:, r * 256:(r + 1) * 256].rearrange("p (h d) -> p h d", h=2),
                    in1=s_D[:, r * 256:(r + 1) * 256].rearrange("p (h d) -> p h d", h=2),
                    op=ALU.mult),
                   rkeys("D", r * 256, (r + 1) * 256), [("D", 6)], tag=f"{tg}n1a{r}")
                OP("vector", lambda q: q.tensor_reduce(
                    out=s_stats[:, 64:66],
                    in_=s_D[:, 6144:6400].rearrange("p (h d) -> p h d", h=2),
                    axis=AX.X, op=ALU.add),
                   [("D", 6)], [stk(64)], tag=f"{tg}n1b{r}")
                OP("vector", lambda q: q.tensor_scalar(
                    out=s_stats[:, 64:66], in0=s_stats[:, 64:66],
                    scalar1=s_eps3[:], scalar2=None, op0=ALU.max),
                   [("eps",), stk(64)], [stk(64)], tag=f"{tg}n1c{r}")

                OP("scalar", lambda q: q.activation(
                    out=s_stats[:, 80:82], in_=s_stats[:, 64:66], func=AF.Ln),
                   [stk(64)], [stk(80)], tag=f"{tg}n2a{r}")
                OP("scalar", lambda q: q.activation(
                    out=s_stats[:, 80:82], in_=s_stats[:, 80:82], func=AF.Exp, scale=-0.5),
                   [stk(80)], [stk(80)], tag=f"{tg}n2b{r}")

                OP("vector", lambda q, r=r, hg=hg: q.tensor_tensor(
                    out=s_stats[:, 82:84], in0=s_stats[:, 80:82],
                    in1=s_beta[:, r * 16 + 2 * hg:r * 16 + 2 * hg + 2], op=ALU.mult),
                   [stk(80), ("beta", r)], [stk(82)], tag=f"{tg}n3a{r}")

                def f_kn3b(q, r=r):
                    for j in range(2):
                        last = q.tensor_scalar(
                            out=s_D[:, r * 256 + j * 128:r * 256 + (j + 1) * 128],
                            in0=s_D[:, r * 256 + j * 128:r * 256 + (j + 1) * 128],
                            scalar1=s_stats[:, 82 + j:83 + j], scalar2=None, op0=ALU.mult)
                    return last
                OP("vector", f_kn3b, [stk(82)] + rkeys("D", r * 256, (r + 1) * 256),
                   rkeys("D", r * 256, (r + 1) * 256), tag=f"{tg}n3b{r}")

                for j in range(2):
                    pslot, pc = next_tr()
                    pk = pkey(pslot, pc)
                    OP("tensor", lambda q, r=r, j=j, pslot=pslot, pc=pc: q.transpose(
                        pslot[:, pc:pc + P], s_D[:, r * 256 + j * 128:r * 256 + (j + 1) * 128],
                        s_id[:]),
                       rkeys("D", r * 256, (r + 1) * 256) + [("id",)], [pk],
                       tag=f"{tg}bt{r}_{j}")
                    OP("vector", lambda q, r=r, j=j, pslot=pslot, pc=pc: q.tensor_copy(
                        s_D[:, 2048 + j * 1024 + r * P:2048 + j * 1024 + (r + 1) * P],
                        pslot[:, pc:pc + P]),
                       [pk], [("D", (2048 + j * 1024 + r * P) // 1024)], tag=f"{tg}be{r}_{j}")

            wvb = stream_q(None, lambda i, hg=hg: wv_d[i * 256:(i + 1) * 256,
                                                       hg * 256:(hg + 1) * 256].rearrange(
                               "(k p) f -> p k f", p=P),
                           f"{tg}wv{hg}", cvt_eng="gpsimd")
            evk = rkeys("E", wvb, wvb + 4096)
            for r in range(NT):
                bank = next_bank()
                def f_vp(q, r=r, bank=bank, wvb=wvb):
                    for kk in range(KD):
                        q.matmul(pb[bank][:, 0:256],
                                 s_A[:, kk * NTOK + r * P:kk * NTOK + (r + 1) * P],
                                 s_E[:, wvb + kk * 256:wvb + (kk + 1) * 256],
                                 start=(kk == 0), stop=False)
                    return q.matmul(pb[bank][:, 0:256], s_onesrow[:], s_brow[0:1, 288:544],
                                    start=False, stop=True)
                OP("tensor", f_vp, akeys_all + evk + [("onesrow",), ("brow", 2)], [pkey(bank)],
                   tag=f"{tg}vp{r}")
                OP("scalar", lambda q, r=r, bank=bank: q.activation(
                    out=s_D[:, 4096 + r * 256:4096 + (r + 1) * 256], in_=pb[bank][:, 0:256],
                    func=AF.Copy),
                   [pkey(bank)], rkeys("D", 4096 + r * 256, 4096 + (r + 1) * 256),
                   tag=f"{tg}ve{r}")

            for r in range(NT):
                c01 = r & 1
                for j in range(2):
                    h = h0 + j
                    pair = (r >> 1) * 16 + h
                    cbank = 4 if (r * 2 + j) % 2 == 0 else 5
                    abank = 9 - cbank
                    lacol = s_la[:, r * 16 + h:r * 16 + h + 1]
                    nclacol = s_ncla[:, r * 16 + h:r * 16 + h + 1]
                    OP("vector", lambda q, lacol=lacol: q.tensor_scalar(
                        out=s_G[:, 0:128], in0=s_triu[:], scalar1=lacol, scalar2=None,
                        op0=ALU.mult),
                       [("triu",), ("la", r)] + GK, GK, tag=f"{tg}c1_{r}_{h}")
                    OP("tensor", lambda q, cbank=cbank: q.matmul(
                        pb[cbank][:, 0:128], s_onesf[:], s_G[:, 0:128], start=True, stop=True),
                       [("onesf",)] + GK, [pkey(cbank)], tag=f"{tg}c2_{r}_{h}")
                    OP("vector", lambda q, cbank=cbank: q.tensor_tensor(
                        out=s_G[:, 128:256], in0=pb[cbank][:, 0:128], in1=s_mst[:], op=ALU.add),
                       [pkey(cbank), ("mst",)] + GK, GK, tag=f"{tg}c3_{r}_{h}")

                    def f_dt(q, nclacol=nclacol, pair=pair, c01=c01):
                        q.activation(out=s_G[:, 256:384], in_=s_G[:, 128:256],
                                     func=AF.Exp, bias=nclacol)
                        if c01 == 0:
                            q.activation(out=s_td0[:, pair:pair + 1],
                                         in_=s_G[:, 255:256], func=AF.Exp)
                        q.activation(out=s_stats[:, 96:97], in_=s_G[:, 255:256],
                                     func=AF.Exp, bias=nclacol)
                        return q.activation(out=s_ces[:, pair * 2 + c01:pair * 2 + c01 + 1],
                                            in_=s_G[:, 255:256], func=AF.Copy)
                    OP("scalar", f_dt, [("cla", r)] + GK,
                       GK + [("td0", pair), stk(96), ("ce", pair * 2 + c01)],
                       tag=f"{tg}dt_{r}_{h}")
                    OP("tensor", lambda q, j=j, h=h, r=r, abank=abank: q.matmul(
                        pb[abank][:, 0:128],
                        s_D[:, 2048 + j * 1024 + r * P:2048 + j * 1024 + (r + 1) * P],
                        s_Bf[:, h * NTOK + r * P:h * NTOK + (r + 1) * P],
                        start=True, stop=True),
                       [("D", (2048 + j * 1024 + r * P) // 1024),
                        ("B", (h * NTOK + r * P) // 1024)], [pkey(abank)],
                       tag=f"{tg}at_{r}_{h}")
                    OP("vector", lambda q, abank=abank: q.tensor_tensor(
                        out=s_D[:, 6144:6272], in0=pb[abank][:, 0:128],
                        in1=s_G[:, 256:384], op=ALU.mult),
                       [pkey(abank)] + GK, [("D", 6)], tag=f"{tg}am_{r}_{h}")
                    OP("vector", lambda q, r=r, j=j: q.tensor_scalar(
                        out=s_D[:, 6272:6400],
                        in0=s_D[:, r * 256 + j * 128:r * 256 + (j + 1) * 128],
                        scalar1=s_stats[:, 96:97], scalar2=None, op0=ALU.mult),
                       [stk(96)] + rkeys("D", r * 256, (r + 1) * 256), [("D", 6)],
                       tag=f"{tg}kw_{r}_{h}")
                    OP("tensor", lambda q, r=r, j=j, cbank=cbank: q.matmul(
                        pb[cbank][:, 0:128], s_D[:, 6144:6272],
                        s_D[:, 4096 + r * 256 + j * 128:4096 + r * 256 + (j + 1) * 128],
                        start=True, stop=True),
                       [("D", 6), ("D", (4096 + r * 256 + j * 128) // 1024)], [pkey(cbank)],
                       tag=f"{tg}yl_{r}_{h}")
                    xc = r * D + h * HD
                    OP("vector", lambda q, xc=xc, cbank=cbank, r=r: q.scalar_tensor_tensor(
                        out=s_x[:, xc:xc + HD], in0=pb[cbank][:, 0:128], scalar=1.0,
                        in1=s_x[:, xc:xc + HD], op0=ALU.mult, op1=ALU.add),
                       [pkey(cbank), ("x", r)], [("x", r)], tag=f"{tg}yx_{r}_{h}")
                    OP("tensor", lambda q, r=r, j=j, abank=abank: q.matmul(
                        pb[abank][:, 0:128], s_D[:, 6272:6400],
                        s_D[:, 4096 + r * 256 + j * 128:4096 + r * 256 + (j + 1) * 128],
                        start=True, stop=True),
                       [("D", 6), ("D", (4096 + r * 256 + j * 128) // 1024)], [pkey(abank)],
                       tag=f"{tg}dl_{r}_{h}")
                    if c01 == 0:
                        def f_st0(q, pair=pair, abank=abank):
                            q.tensor_copy(s_C[:, pair * P:(pair + 1) * P], pb[abank][:, 0:128])
                            return q.tensor_copy(s_C[:, 8192 + pair * P:8192 + (pair + 1) * P],
                                                 pb[abank][:, 0:128])
                        OP("vector", f_st0, [pkey(abank)],
                           rkeys("C", pair * P, (pair + 1) * P) +
                           rkeys("C", 8192 + pair * P, 8192 + (pair + 1) * P) +
                           [("dT0", pair), ("St", pair)], tag=f"{tg}s0_{r}_{h}")
                    else:
                        OP("scalar", lambda q, pair=pair: q.activation(
                            out=s_stats[:, 97:98], in_=s_ces[:, pair * 2 + 1:pair * 2 + 2],
                            func=AF.Exp),
                           [("ce", pair * 2 + 1)], [stk(97)], tag=f"{tg}t1_{r}_{h}")

                        OP("vector", lambda q, pair=pair: q.tensor_scalar(
                            out=s_D[:, 6400:6528],
                            in0=s_C[:, 8192 + pair * P:8192 + (pair + 1) * P],
                            scalar1=s_stats[:, 97:98], scalar2=None, op0=ALU.mult),
                           [("St", pair), stk(97)] +
                           rkeys("C", 8192 + pair * P, 8192 + (pair + 1) * P),
                           [("D", 6)], tag=f"{tg}s1a_{r}_{h}")
                        OP("vector", lambda q, pair=pair, abank=abank: q.tensor_tensor(
                            out=s_C[:, 8192 + pair * P:8192 + (pair + 1) * P],
                            in0=pb[abank][:, 0:128], in1=s_D[:, 6400:6528], op=ALU.add),
                           [pkey(abank), ("D", 6)] +
                           rkeys("C", 8192 + pair * P, 8192 + (pair + 1) * P),
                           rkeys("C", 8192 + pair * P, 8192 + (pair + 1) * P) +
                           [("St", pair)], tag=f"{tg}s1b_{r}_{h}")
                        OP("vector", lambda q, pair=pair: q.tensor_tensor(
                            out=s_csum[:, pair:pair + 1], in0=s_ces[:, pair * 2:pair * 2 + 1],
                            in1=s_ces[:, pair * 2 + 1:pair * 2 + 2], op=ALU.add),
                           [("ce", pair * 2), ("ce", pair * 2 + 1)], [("cs", pair)],
                           tag=f"{tg}cs_{r}_{h}")

        dump("dnstloc", s_C[:, 8192:16384], [("St", pair) for pair in range(64)])
        dump("dncsum", s_csum[:, 0:64], [("cs", pair) for pair in range(64)])

        stw = [("St", pair) for pair in range(64)]
        DMA("gpsimd", st_exp_h.ap().rearrange("(pair u) d -> u pair d", u=P),
            s_C[:, 8192:16384].rearrange("u (pair d) -> u pair d", d=HD),
            stw + rkeys("C", 8192, 16384), [("stexp",)], tag="stexp")
        csw = [("cs", pair) for pair in range(64)]
        DMA("gpsimd", cs_exp_h.ap().rearrange("(one pair) one2 -> one pair one2", one=1),
            s_csum[0:1, 0:64].rearrange("one (pair one2) -> one pair one2", one2=1),
            csw, [("csexp",)], tag="csexp")
        OP("gpsimd", lambda q: q.collective_compute(
            "AllGather", ALU.bypass, ins=[st_exp_h.ap().opt()], outs=[st_gath_h.ap().opt()],
            replica_groups=[list(range(NCORE))]),
           [("stexp",)], [("stg",)], kind="cc", tag="ccst")
        OP("gpsimd", lambda q: q.collective_compute(
            "AllGather", ALU.bypass, ins=[cs_exp_h.ap().opt()], outs=[cs_gath_h.ap().opt()],
            replica_groups=[list(range(NCORE))]),
           [("csexp",)], [("csgat",)], kind="cc", tag="cccs")

        DMA("gpsimd", s_csg[:].rearrange("p (r one) -> p r one", one=1),
            cs_gath_h.ap().rearrange("(r p) one -> p r one", p=64),
            [("csgat",)], [("csgs",)], tag="csgl", slow=True)

        for rr in range(8):
            OP("vector", lambda q, rr=rr: q.tensor_tensor(
                out=s_G[0:64, 0:8], in0=s_csg[:],
                in1=s_m2exp[:, rr * 8:(rr + 1) * 8], op=ALU.mult),
               [("csgs",), ("m2exp",)] + GK, GK, tag=f"dnLa{rr}")
            OP("vector", lambda q, rr=rr: q.tensor_reduce(
                out=s_L[:, rr:rr + 1], in_=s_G[0:64, 0:8], axis=AX.X, op=ALU.add),
               GK, GK + [("L",)], tag=f"dnLb{rr}")
        OP("scalar", lambda q: q.activation(out=s_scl[:], in_=s_L[:], func=AF.Exp),
           [("L",)], [("scl",)], tag="dnsE")
        OP("vector", lambda q: q.tensor_tensor(out=s_scl[:], in0=s_scl[:], in1=s_mask1[:],
                                               op=ALU.mult),
           [("scl",), ("mask1",)], [("scl",)], tag="dnsM")
        DMA("gpsimd", sclT_dram.ap(), s_scl[:], [("scl",)], [("sclTd",)], tag="sclTd")
        DMA("gpsimd", s_sclT[:], sclT_dram.ap().rearrange("p r -> r p"),
            [("sclTd",)], [("sclT",)], tag="sclTl", slow=True)

        def f_bc(q):
            for rr in range(8):
                last = q.tensor_scalar(out=s_G[0:8, rr * 64:(rr + 1) * 64],
                                       in0=s_sclT[:], scalar1=s_id8[:, rr:rr + 1],
                                       scalar2=None, op0=ALU.mult)
            return last
        OP("vector", f_bc, [("sclT",), ("id8",)] + GK, GK + [("bcm",)], tag="dnbcm")
        for rr in range(8):
            OP("tensor", lambda q, rr=rr: q.matmul(
                pb[4][:, rr * 64:(rr + 1) * 64], s_onesf[0:8, :],
                s_G[0:8, rr * 64:(rr + 1) * 64], start=True, stop=True),
               [("onesf",), ("bcm",)] + GK, [pkey(4)], tag=f"dnbc{rr}")

        for blk in range(16):
            for rr in range(8):
                gb = blk % 2
                DMA("gpsimd",
                    s_A[:, gb * 1024:gb * 1024 + 512].rearrange("u (pair d) -> u pair d", d=HD),
                    st_gath_h.ap()[rr * 8192 + blk * 512:rr * 8192 + (blk + 1) * 512, :]
                    .rearrange("(pair u) d -> u pair d", u=P),
                    [("stg",), ("A", gb)], [("A", gb)], tag=f"stgl{blk}_{rr}")
                for pp in range(4):
                    pair = blk * 4 + pp
                    acc = s_G[:, pp * 128:(pp + 1) * 128]
                    if rr == 0:
                        OP("vector", lambda q, acc=acc, gb=gb, pp=pp, rr=rr, pair=pair:
                           q.tensor_scalar(
                               out=acc,
                               in0=s_A[:, gb * 1024 + pp * 128:gb * 1024 + (pp + 1) * 128],
                               scalar1=pb[4][:, rr * 64 + pair:rr * 64 + pair + 1],
                               scalar2=None, op0=ALU.mult),
                           [("A", gb), pkey(4)] + GK, GK + [("acc", pp)],
                           tag=f"cb{blk}_{rr}_{pp}")
                    else:
                        OP("vector", lambda q, acc=acc, gb=gb, pp=pp, rr=rr, pair=pair:
                           q.scalar_tensor_tensor(
                               out=acc,
                               in0=s_A[:, gb * 1024 + pp * 128:gb * 1024 + (pp + 1) * 128],
                               scalar=pb[4][:, rr * 64 + pair:rr * 64 + pair + 1],
                               in1=acc, op0=ALU.mult, op1=ALU.add),
                           [("A", gb), pkey(4), ("acc", pp)] + GK, GK + [("acc", pp)],
                           tag=f"cb{blk}_{rr}_{pp}")
            for pp in range(4):
                pair = blk * 4 + pp
                OP("vector", lambda q, pp=pp, pair=pair: q.tensor_copy(
                    s_C[:, 8192 + pair * P:8192 + (pair + 1) * P],
                    s_G[:, pp * 128:(pp + 1) * 128]),
                   [("acc", pp)] + GK,
                   rkeys("C", 8192 + pair * P, 8192 + (pair + 1) * P) + [("Stin", pair)],
                   tag=f"cbc{blk}_{pp}")

        dump("dnstin", s_C[:, 8192:16384], [("Stin", pair) for pair in range(64)])
        dump("dnscl", s_scl[:, 0:8], [("scl",)])

        for r in range(NT):
            c01 = r & 1
            for h in range(NH):
                pair = (r >> 1) * 16 + h
                bank = next_bank()
                if c01 == 0:
                    OP("tensor", lambda q, h=h, r=r, bank=bank, pair=pair: q.matmul(
                        pb[bank][:, 0:128],
                        s_Bf[:, h * NTOK + r * P:h * NTOK + (r + 1) * P],
                        s_C[:, 8192 + pair * P:8192 + (pair + 1) * P],
                        start=True, stop=True),
                       [("B", (h * NTOK + r * P) // 1024), ("Stin", pair)] +
                       rkeys("C", 8192 + pair * P, 8192 + (pair + 1) * P),
                       [pkey(bank)], tag=f"{tg}yg_{r}_{h}")
                else:
                    OP("vector", lambda q, pair=pair: q.tensor_scalar(
                        out=s_D[:, 6400:6528],
                        in0=s_C[:, 8192 + pair * P:8192 + (pair + 1) * P],
                        scalar1=s_td0[:, pair:pair + 1], scalar2=None, op0=ALU.mult),
                       [("Stin", pair), ("td0", pair)] +
                       rkeys("C", 8192 + pair * P, 8192 + (pair + 1) * P),
                       [("D", 6)], tag=f"{tg}sTa_{r}_{h}")
                    OP("vector", lambda q, pair=pair: q.tensor_tensor(
                        out=s_D[:, 6400:6528], in0=s_D[:, 6400:6528],
                        in1=s_C[:, pair * P:(pair + 1) * P], op=ALU.add),
                       [("dT0", pair), ("D", 6)] + rkeys("C", pair * P, (pair + 1) * P),
                       [("D", 6)], tag=f"{tg}sTb_{r}_{h}")
                    OP("tensor", lambda q, h=h, r=r, bank=bank: q.matmul(
                        pb[bank][:, 0:128],
                        s_Bf[:, h * NTOK + r * P:h * NTOK + (r + 1) * P],
                        s_D[:, 6400:6528], start=True, stop=True),
                       [("B", (h * NTOK + r * P) // 1024), ("D", 6)], [pkey(bank)],
                       tag=f"{tg}yg_{r}_{h}")
                OP("scalar", lambda q, r=r, h=h: q.activation(
                    out=s_stats[:, 98:99], in_=s_cla[:, r * 16 + h:r * 16 + h + 1], func=AF.Exp),
                   [("cla", r)], [stk(98)], tag=f"{tg}cd_{r}_{h}")
                xc = r * D + h * HD
                OP("vector", lambda q, xc=xc, bank=bank, r=r: q.scalar_tensor_tensor(
                    out=s_x[:, xc:xc + HD], in0=pb[bank][:, 0:128],
                    scalar=s_stats[:, 98:99], in1=s_x[:, xc:xc + HD],
                    op0=ALU.mult, op1=ALU.add),
                   [pkey(bank), stk(98), ("x", r)], [("x", r)], tag=f"{tg}gx_{r}_{h}")

    if "ffn0" in stages:
        ffn_call(0)
    if "dn" in stages:
        deltanet()
    if "ffn1" in stages:
        ffn_call(1)

    for r in range(NT):
        DMA("sync", y_out[r * P:(r + 1) * P, :], s_x[:, r * D:(r + 1) * D],
            [("x", r)], [("yst", r)], tag=f"st{r}")
    OP("sync", lambda q: None, [("yst", r) for r in range(NT)], [], tag="fin")

    B.resolve(csem, dsem, ccsem)
    with es, nc.Block() as block:
        B.emit(block)
    return nc


STAGES = ("ffn0", "dn", "ffn1")


def kernel(**inputs):
    x = np.asarray(inputs["L0_attn_x"])
    consts = _static_consts()
    base = {
        "qkv_wt": np.ascontiguousarray(np.asarray(inputs["L0_ffn_qkv_weight"]).T),
        "proj_wt": np.ascontiguousarray(np.asarray(inputs["L0_ffn_proj_weight"]).T),
        "router_wt": np.ascontiguousarray(np.asarray(inputs["L0_ffn_router_weight"]).T),
        "ln1_g": np.asarray(inputs["L0_ffn_ln1_weight"]),
        "ln1_b": np.asarray(inputs["L0_ffn_ln1_bias"]),
        "ln2_g": np.asarray(inputs["L0_ffn_ln2_weight"]),
        "ln2_b": np.asarray(inputs["L0_ffn_ln2_bias"]),
        "moe_w1": np.asarray(inputs["L0_ffn_moe_w1"]),
        "moe_w2": np.asarray(inputs["L0_ffn_moe_w2"]),
        "wq": np.asarray(inputs["L0_attn_Wq"]),
        "wk": np.asarray(inputs["L0_attn_Wk"]),
        "wv": np.asarray(inputs["L0_attn_Wv"]),
        "bq": np.asarray(inputs["L0_attn_bq"]),
        "bk": np.asarray(inputs["L0_attn_bk"]),
        "bv": np.asarray(inputs["L0_attn_bv"]),
        "wab": np.ascontiguousarray(np.concatenate(
            [np.asarray(inputs["L0_attn_Walpha"]), np.asarray(inputs["L0_attn_Wbeta"])], axis=1)),
        "bab": np.ascontiguousarray(np.concatenate(
            [np.asarray(inputs["L0_attn_balpha"]), np.asarray(inputs["L0_attn_bbeta"])])),
    }
    base.update(consts)

    nc = build_program(stages=STAGES)
    in_maps = []
    for c in range(NCORE):
        m2exp, mask1 = _mk_core_consts(c)
        m = dict(base)
        m["c_m2exp"] = m2exp
        m["c_mask1"] = mask1
        m["x"] = np.ascontiguousarray(x[:, c * TLOC:(c + 1) * TLOC, :].reshape(NTOK, D))
        in_maps.append(m)
    res = run_bass_kernel_spmd(nc, in_maps, core_ids=list(range(NCORE)))
    out = np.empty((NB, 2048, D), dtype=np.float32)
    for c in range(NCORE):
        out[:, c * TLOC:(c + 1) * TLOC, :] = np.asarray(
            res.results[c]["y"], np.float32).reshape(NB, TLOC, D)
    return out


# version 06 (final)
# speedup vs baseline: 1.4127x; 1.0152x over previous
import numpy as np
import ml_dtypes
import concourse.bass as bass
import concourse.mybir as mybir
from concourse.bass_utils import run_bass_kernel_spmd
from contextlib import ExitStack

F32 = mybir.dt.float32
BF16 = mybir.dt.bfloat16
AF = mybir.ActivationFunctionType
ALU = mybir.AluOpType
AX = mybir.AxisListType

P = 128
D = 2048
NB = 4
TLOC = 256
NTOK = NB * TLOC
NT = NTOK // P
KD = D // P
NH = 16
HD = 128
NKV = 4
FFN = 4096
NE = 8
NCORE = 8
EPS = 1e-5
RS = 1.0 / float(np.sqrt(128.0))

ENGS = ("sync", "tensor", "vector", "scalar", "gpsimd")


class Op:
    __slots__ = ("eng", "fn", "reads", "writes", "kind", "needs_inc", "sem",
                 "val", "waits", "idx", "tag", "deps", "raw", "big", "sgrp")

    def __init__(self, eng, fn, reads, writes, kind, tag, big=False, sgrp=None):
        self.eng = eng
        self.fn = fn
        self.reads = reads
        self.writes = writes
        self.kind = kind
        self.needs_inc = kind in ("d", "cc")
        self.sem = None
        self.val = None
        self.waits = []
        self.tag = tag
        self.big = big
        self.sgrp = sgrp


class Sem:
    def __init__(self, h):
        self.h = h
        self.n = 0

    def inc(self, k):
        self.n += k
        return self.n


class Builder:
    def __init__(self):
        self.ops = []
        self.last_w = {}
        self.readers = {}

    def op(self, eng, fn, reads=(), writes=(), kind="c", tag="", big=False,
           sgrp=None):
        reads = list(reads)
        writes = list(writes)
        for k in [k for k in reads if k and k[0] == "ps"]:
            reads.remove(k)
            if k not in writes:
                writes.append(k)
        o = Op(eng, fn, reads, writes, kind, tag, big, sgrp)
        o.idx = len(self.ops)
        raw = set()
        for r in o.reads:
            j = self.last_w.get(r)
            if j is not None:
                raw.add(j)
        other = set()
        for w in o.writes:
            j = self.last_w.get(w)
            if j is not None:
                other.add(j)
            for j2 in self.readers.get(w, ()):
                other.add(j2)
        raw.discard(o.idx)
        other.discard(o.idx)
        o.raw = raw
        o.deps = raw | other
        self.ops.append(o)
        for r in o.reads:
            self.readers.setdefault(r, []).append(o.idx)
        for w in o.writes:
            self.last_w[w] = o.idx
            self.readers[w] = []
        return o

    @staticmethod
    def _skip(d, o, is_raw):
        if d.kind != "c" or d.eng != o.eng:
            return False
        if d.eng in ("tensor", "gpsimd", "sync"):
            return True
        if not is_raw:
            return True
        return d.big

    def resolve(self, csem, dsem, ccsems, sgs):
        cci = [0]
        for o in self.ops:
            for j in o.deps:
                d = self.ops[j]
                if self._skip(d, o, j in o.raw):
                    continue
                d.needs_inc = True
        for o in self.ops:
            if not o.needs_inc:
                continue
            if o.kind == "d":
                o.sem = sgs[o.sgrp] if o.sgrp else dsem[o.eng]
                o.val = o.sem.inc(16)
            elif o.kind == "cc":
                o.sem = ccsems[cci[0]]
                cci[0] += 1
                o.val = o.sem.inc(1)
            else:
                o.sem = csem[o.eng]
                o.val = o.sem.inc(1)
        marks = {e: {} for e in ENGS}
        for o in self.ops:
            m = marks[o.eng]
            for j in sorted(o.deps):
                d = self.ops[j]
                if self._skip(d, o, j in o.raw):
                    continue
                key = id(d.sem)
                if m.get(key, -1) >= d.val:
                    continue
                m[key] = d.val
                o.waits.append((d.sem, d.val))

    def emit(self, block):
        per_eng = {e: [] for e in ENGS}
        for o in self.ops:
            per_eng[o.eng].append(o)

        def runner(ops):
            def body(q):
                for o in ops:
                    for (s, v) in o.waits:
                        q.wait_ge(s.h, v)
                    last = o.fn(q)
                    if o.needs_inc:
                        assert last is not None, f"op {o.tag} needs inc but returned None"
                        last.then_inc(o.sem.h, 16 if o.kind == "d" else 1)
            return body
        for e in ENGS:
            getattr(block, e)(runner(per_eng[e]))


def rkeys(name, c0, c1, quantum=1024):
    return [(name, i) for i in range(c0 // quantum, (c1 + quantum - 1) // quantum)]


def stk(c):
    return ("st", c // 16)


def gkeys(c0, c1):
    return [("G", i) for i in range(c0 // 256, (c1 + 255) // 256)]


_STATIC_CONSTS = None


def _static_consts():
    global _STATIC_CONSTS
    if _STATIC_CONSTS is None:
        _STATIC_CONSTS = {
            "c_ident": np.eye(P, dtype=ml_dtypes.bfloat16),
            "c_triu": np.triu(np.ones((P, P), dtype=np.float32)),
            "c_mstrict": (-1e30 * np.tril(np.ones((P, P), dtype=np.float32), -1)).astype(np.float32),
            "c_ones": np.ones((P, P), dtype=np.float32),
            "c_onesrow": np.ones((1, P), dtype=ml_dtypes.bfloat16),
            "c_id8": np.eye(8, dtype=np.float32),
        }
    return _STATIC_CONSTS


def _mk_core_consts(core):
    m2 = np.zeros((8, 8), dtype=np.float32)
    for cpp in range(8):
        for r in range(8):
            if r < cpp < core:
                m2[cpp, r] = 1.0
    m2exp = np.tile(m2.T.reshape(1, 64), (64, 1)).astype(np.float32)
    mask1 = np.zeros((64, 8), dtype=np.float32)
    mask1[:, :core] = 1.0
    return m2exp, mask1


def build_program(debug=(), stages=("ffn0", "dn", "ffn1")):
    nc = bass.Bass("TRN2", target_bir_lowering=False, debug=False)
    dbg = set(debug)
    B = Builder()

    def din(name, shape, dt=F32):
        return nc.dram_tensor(name, shape, dt, kind="ExternalInput").ap()

    x_in = din("x", [NTOK, D])
    qkv_wt = din("qkv_wt", [D, 3072])
    proj_wt = din("proj_wt", [D, D])
    router_wt = din("router_wt", [NE, D])
    ln1_g = din("ln1_g", [D]); ln1_b = din("ln1_b", [D])
    ln2_g = din("ln2_g", [D]); ln2_b = din("ln2_b", [D])
    moe_w1 = din("moe_w1", [NE, D, FFN]).rearrange("e r c -> (e r) c")
    moe_w2 = din("moe_w2", [NE, FFN, D]).rearrange("e r c -> (e r) c")
    wq_d = din("wq", [D, D]); wk_d = din("wk", [D, D]); wv_d = din("wv", [D, D])
    bq_d = din("bq", [D]); bk_d = din("bk", [D]); bv_d = din("bv", [D])
    wab_d = din("wab", [D, 32]); bab_d = din("bab", [32])
    c_ident = din("c_ident", [P, P], BF16)
    c_triu = din("c_triu", [P, P])
    c_mstrict = din("c_mstrict", [P, P])
    c_ones = din("c_ones", [P, P])
    c_onesrow = din("c_onesrow", [1, P], BF16)
    c_id8 = din("c_id8", [8, 8])
    c_m2exp = din("c_m2exp", [64, 64])
    c_mask1 = din("c_mask1", [64, 8])

    y_out = nc.dram_tensor("y", [NTOK, D], F32, kind="ExternalOutput").ap()
    st_exp_hs = [nc.dram_tensor(f"st_exp{i}", [32 * P, HD], BF16) for i in range(2)]
    st_gath_hs = [nc.dram_tensor(f"st_gath{i}", [NCORE * 32 * P, HD], BF16)
                  for i in range(2)]
    cs_exp_h = nc.dram_tensor("cs_exp", [64, 1], F32)
    cs_gath_h = nc.dram_tensor("cs_gath", [NCORE * 64, 1], F32)
    sclT_dram = nc.dram_tensor("sclT_d", [64, 8], F32)
    wcaches = [nc.dram_tensor(f"wcache{i}", [72 * 1024 * 1024], BF16).ap()
               for i in range(3)]

    es = ExitStack()

    def sb(name, pdim, cols, dt):
        return es.enter_context(nc.sbuf_tensor(name, [pdim, cols], dt))

    def mkpsum(name, cols, dt):
        return es.enter_context(nc.psum_tensor(name, [P, cols], dt))

    def mksem(name):
        return Sem(es.enter_context(nc.semaphore(name)))

    s_x = sb("s_x", P, NT * D, F32)
    s_A = sb("s_A", P, 16384, BF16)
    s_Bf = sb("s_Bf", P, 16384, BF16)
    s_C = sb("s_C", P, 16384, BF16)
    s_D = sb("s_D", P, 6656, BF16)
    s_E = sb("s_E", P, 8192, BF16)
    s_G = sb("s_G", P, 768, F32)
    s_stage = sb("s_stage", P, 1024, F32)
    s_cvt = sb("s_cvt", P, 1024, BF16)
    s_stats = sb("s_stats", P, 128, F32)
    s_gam = sb("s_gam", P, 4 * KD, F32)
    s_bq = sb("s_bq", P, KD, F32)
    s_gates = sb("s_gates", P, NT * 8, F32)
    s_brow = sb("s_brow", 1, 544, BF16)
    s_eps1 = sb("s_eps1", P, 1, F32)
    s_eps2 = sb("s_eps2", P, 1, F32)
    s_eps3 = sb("s_eps3", P, 1, F32)
    s_zero = sb("s_zero", P, 1, F32)
    s_one = sb("s_one", P, 1, F32)
    s_id = sb("s_id", P, P, BF16)
    s_triu = sb("s_triu", P, P, F32)
    s_mst = sb("s_mst", P, P, F32)
    s_onesf = sb("s_onesf", P, P, F32)
    s_onesrow = sb("s_onesrow", 1, P, BF16)
    s_router = sb("s_router", P, KD * 8, BF16)
    s_la = sb("s_la", P, NT * NH, F32)
    s_beta = sb("s_beta", P, NT * NH, F32)
    s_cla = sb("s_cla", P, NT * NH, F32)
    s_ncla = sb("s_ncla", P, NT * NH, F32)
    s_td0 = sb("s_td0", P, 64, F32)
    s_ces = sb("s_ces", P, 128, F32)
    s_csum = sb("s_csum", P, 64, F32)
    s_ab = sb("s_ab", P, KD * 32, BF16)
    s_id8 = sb("s_id8", 8, 8, F32)
    s_m2exp = sb("s_m2exp", 64, 64, F32)
    s_mask1 = sb("s_mask1", 64, 8, F32)
    s_csg = sb("s_csg", 64, 8, F32)
    s_L = sb("s_L", 64, 8, F32)
    s_scl = sb("s_scl", 64, 8, F32)
    s_sclT = sb("s_sclT", 8, 64, F32)

    pb = [mkpsum(f"pb{i}", 512, F32) for i in range(6)]
    ptA = mkpsum("ptA", 1024, BF16)
    ptB = mkpsum("ptB", 1024, BF16)
    PTN = {id(ptA): "ptA", id(ptB): "ptB"}

    csem = {e: mksem("c_" + e) for e in ENGS}
    dsem = {e: mksem("d_" + e) for e in ENGS}
    ccsems = [mksem(f"cc{i}") for i in range(4)]
    sgsems = {n: mksem("sg_" + n) for n in
              ("xg", "stage0", "stage1", "eq0", "eq1", "tap", "msc")}

    def OP(eng, fn, reads=(), writes=(), kind="c", tag="", big=False, sgrp=None):
        return B.op(eng, fn, reads, writes, kind, tag, big, sgrp)

    def DMA(eng, out, in_, reads, writes, tag="", slow=False, sgrp=None):
        return OP(eng, lambda q: q.dma_start(out=out, in_=in_,
                                             allow_slow_non_contiguous=slow),
                  reads, writes, "d", tag, sgrp=sgrp)

    def pkey(pt_, c0=0):
        nm = PTN.get(id(pt_), pt_)
        return ("ps", nm, c0 // 512)

    bank_rot = [0]

    def next_bank(lo=0, hi=4):
        b = bank_rot[0] % (hi - lo) + lo
        bank_rot[0] += 1
        return b

    tr_rot = [0]

    def next_tr():
        t = tr_rot[0] % 4
        tr_rot[0] += 1
        return (ptA if t < 2 else ptB), (t % 2) * P

    dbg_ctr = [0]

    def dump(name, src_ap, read_keys):
        if name not in dbg:
            return
        shape = list(src_ap.shape)
        ap = nc.dram_tensor("dbg_" + name, shape, src_ap.dtype, kind="ExternalOutput").ap()
        DMA("gpsimd", ap, src_ap, list(read_keys), [("dbgsink", dbg_ctr[0])], tag="dbg_" + name)
        dbg_ctr[0] += 1

    GK = [("G", 0), ("G", 1), ("G", 2)]

    for r in range(NT):
        DMA("sync", s_x[:, r * D:(r + 1) * D], x_in[r * P:(r + 1) * P, :], [], [("x", r)])
    DMA("sync", s_id[:], c_ident, [], [("id",)])
    DMA("sync", s_triu[:], c_triu, [], [("triu",)])
    DMA("sync", s_mst[:], c_mstrict, [], [("mst",)])
    DMA("sync", s_onesf[:], c_ones, [], [("onesf",)])
    DMA("sync", s_onesrow[:], c_onesrow, [], [("onesrow",)])
    DMA("sync", s_id8[:], c_id8, [], [("id8",)])
    DMA("sync", s_m2exp[:], c_m2exp, [], [("m2exp",)])
    DMA("sync", s_mask1[:], c_mask1, [], [("mask1",)])
    for i, t in enumerate([ln1_g, ln1_b, ln2_g, ln2_b]):
        DMA("sync", s_gam[:, i * KD:(i + 1) * KD], t.rearrange("(f p) -> p f", p=P),
            [], [("gam", i)], slow=True)
    DMA("sync", s_bq[:], bq_d.rearrange("(f p) -> p f", p=P), [], [("bq",)], slow=True)
    DMA("sync", s_G[0:1, 0:32], bab_d.rearrange("(one d) -> one d", one=1), GK, GK, tag="babl")
    OP("vector", lambda q: q.tensor_copy(s_brow[0:1, 0:32], s_G[0:1, 0:32]),
       GK, [("brow", 0)], tag="babc")
    for hf in range(4):
        DMA("sync", s_G[0:8, 0:512], router_wt[:, hf * 512:(hf + 1) * 512], GK, GK,
            tag=f"routl{hf}")
        OP("vector", lambda q, hf=hf: q.tensor_copy(
            s_D[0:8, hf * 512:(hf + 1) * 512], s_G[0:8, 0:512]),
           GK, [("D", hf // 2)], tag=f"routc{hf}")
    for f in range(KD):
        pslot, pc = next_tr()
        pk = pkey(pslot, pc)
        OP("tensor", lambda q, f=f, pslot=pslot, pc=pc:
           q.transpose(pslot[:, pc:pc + 8], s_D[0:8, f * P:(f + 1) * P], s_id[0:8, 0:8]),
           [("D", 0), ("D", 1), ("id",)], [pk], tag=f"rtr{f}")
        OP("vector", lambda q, f=f, pslot=pslot, pc=pc:
           q.tensor_copy(s_router[:, f * 8:(f + 1) * 8], pslot[:, pc:pc + 8]),
           [pk], [("router",)], tag=f"rtrev{f}")
    for i in range(KD):
        DMA("sync", s_G[0:P, 0:32], wab_d[i * P:(i + 1) * P, :], GK, GK, tag=f"wabl{i}")
        OP("vector", lambda q, i=i: q.tensor_copy(s_ab[:, i * 32:(i + 1) * 32], s_G[:, 0:32]),
           GK, [("ab", i)], tag=f"wabc{i}")

    def f_eps(q):
        q.memset(s_eps1[:], EPS)
        q.memset(s_eps2[:], 1e-6)
        q.memset(s_eps3[:], 1e-24)
        q.memset(s_zero[:], 0.0)
        return q.memset(s_one[:], 1.0)
    OP("gpsimd", f_eps, [], [("eps",)])

    def ln_stats(src_ap, src_reads, so, tag=""):
        OP("vector", lambda q: q.tensor_reduce(out=s_stats[:, so:so + 1], in_=src_ap,
                                               axis=AX.X, op=ALU.add),
           src_reads, [stk(so), ("lnser",)], tag=tag + "s")
        for c4 in range(4):
            slot = 0
            OP("scalar", lambda q, c4=c4, slot=slot: q.activation(
                out=s_G[:, slot:slot + 512], in_=src_ap[:, c4 * 512:(c4 + 1) * 512],
                func=AF.Square),
               src_reads + gkeys(slot, slot + 512), gkeys(slot, slot + 512) + [("lnser",)],
               tag=tag + f"q{c4}")
            OP("vector", lambda q, c4=c4, slot=slot: q.tensor_reduce(
                out=s_stats[:, so + 16 + c4:so + 17 + c4], in_=s_G[:, slot:slot + 512],
                axis=AX.X, op=ALU.add),
               gkeys(slot, slot + 512), [stk(so + 16 + c4), ("lnser",)], tag=tag + f"r{c4}")

        OP("vector", lambda q: q.tensor_reduce(
            out=s_stats[:, so + 2:so + 3], in_=s_stats[:, so + 16:so + 20],
            axis=AX.X, op=ALU.add),
           [stk(so + 16), stk(so + 17), stk(so + 18), stk(so + 19)], [stk(so + 2)],
           tag=tag + "m0")
        OP("vector", lambda q: q.tensor_scalar(
            out=s_stats[:, so:so + 1], in0=s_stats[:, so:so + 1],
            scalar1=1.0 / D, scalar2=None, op0=ALU.mult),
           [stk(so)], [stk(so)], tag=tag + "m1")
        OP("vector", lambda q: q.tensor_tensor(
            out=s_stats[:, so + 3:so + 4], in0=s_stats[:, so:so + 1],
            in1=s_stats[:, so:so + 1], op=ALU.mult),
           [stk(so)], [stk(so + 3)], tag=tag + "m2")
        OP("vector", lambda q: q.tensor_scalar(
            out=s_stats[:, so + 2:so + 3], in0=s_stats[:, so + 2:so + 3],
            scalar1=1.0 / D, scalar2=None, op0=ALU.mult),
           [stk(so + 2)], [stk(so + 2)], tag=tag + "m3")
        OP("vector", lambda q: q.tensor_tensor(
            out=s_stats[:, so + 1:so + 2], in0=s_stats[:, so + 2:so + 3],
            in1=s_stats[:, so + 3:so + 4], op=ALU.subtract),
           [stk(so + 2), stk(so + 3)], [stk(so + 1)], tag=tag + "m4")
        OP("scalar", lambda q: q.activation(
            out=s_stats[:, so + 1:so + 2], in_=s_stats[:, so + 1:so + 2],
            func=AF.Ln, bias=s_eps1[:]),
           [stk(so + 1), ("eps",)], [stk(so + 1)], tag=tag + "r0")
        OP("scalar", lambda q: q.activation(
            out=s_stats[:, so + 1:so + 2], in_=s_stats[:, so + 1:so + 2],
            func=AF.Exp, scale=-0.5),
           [stk(so + 1)], [stk(so + 1)], tag=tag + "r1")

    stage_rot = [0]
    CVT = ["vector", "gpsimd", "scalar"]
    wc_alloc = [0]
    wc_blocks = {}
    WCCAP = 72 * 1024 * 1024
    ldq_rot = [0]

    def stream_win(name, src_fn, nseg, seg_cols, dst_ap, dst_keys, tag, tap=True,
                   cvt_eng="vector", rd_sgrp=None):
        dst_keys = list(dst_keys)
        cols = nseg * seg_cols
        if name is not None and name in wc_blocks:
            off = wc_blocks[name]
            ti, to = divmod(off, WCCAP)
            blk = wcaches[ti][to:to + P * cols].rearrange("(p c) -> p c", p=P)
            OP("sync", lambda q: q.dma_start(out=dst_ap, in_=blk),
               [("wc", name)] + dst_keys, dst_keys, "d", tag + "R", sgrp=rd_sgrp)
            return
        for i in range(nseg):
            srot = stage_rot[0] % 2
            stage_rot[0] += 1
            st_ap = s_stage[:, srot * 512:srot * 512 + seg_cols]
            ldq = "sync"

            def fld(q, st_ap=st_ap, i=i):
                s3 = src_fn(i)
                d = st_ap
                if len(s3.shape) == 3:
                    d = st_ap.rearrange("p (k f) -> p k f", f=s3.shape[2])
                return q.dma_start(out=d, in_=s3)
            OP(ldq, fld, [("stage", srot)], [("stage", srot)], "d", tag + f"l{i}",
               sgrp=f"stage{srot}")

            def fcv(q, st_ap=st_ap, i=i, seg_cols=seg_cols):
                d = dst_ap[:, i * seg_cols:(i + 1) * seg_cols]
                if hasattr(q, "tensor_copy"):
                    return q.tensor_copy(d, st_ap)
                return q.activation(out=d, in_=st_ap, func=AF.Copy)
            OP(cvt_eng, fcv, [("stage", srot)] + dst_keys, dst_keys, tag=tag + f"c{i}")
        if name is not None and tap:
            if name not in wc_blocks:
                need = P * cols
                if wc_alloc[0] % WCCAP + need > WCCAP:
                    wc_alloc[0] = (wc_alloc[0] // WCCAP + 1) * WCCAP
                wc_blocks[name] = wc_alloc[0]
                wc_alloc[0] += need
            off = wc_blocks[name]
            ti, to = divmod(off, WCCAP)
            blk = wcaches[ti][to:to + P * cols].rearrange("(p c) -> p c", p=P)
            OP("gpsimd", lambda q: q.dma_start(out=blk, in_=dst_ap),
               dst_keys, [("wc", name)], "d", tag + "T", sgrp="tap")

    equarter = [0]

    def stream_q(name, src_fn, tag, cvt_eng="vector", tap=True):
        base = (equarter[0] % 2) * 4096
        equarter[0] += 1
        stream_win(name, src_fn, 8, 512, s_E[:, base:base + 4096],
                   rkeys("E", base, base + 4096), tag, tap=tap, cvt_eng=cvt_eng,
                   rd_sgrp=f"eq{base // 4096}")
        return base

    def ffn_call(call):
        tg = f"f{call}"
        akeys = lambda r: rkeys("A", r * D, (r + 1) * D)
        bkeys_all = rkeys("B", 0, 16384)
        ckeys_all = rkeys("C", 0, 16384)

        for r in range(NT):
            so = (r % 2) * 32
            if r == 0:
                dump(f"xin{call}", s_x[:, 0:D], [("x", 0)])
            ln_stats(s_x[:, r * D:(r + 1) * D], [("x", r)], so, tag=f"{tg}ol{r}")
            if r == 0:
                dump(f"st{call}", s_stats[:, 0:32], [stk(0), stk(16)])
            OP("vector", lambda q, r=r, so=so: q.tensor_scalar(
                out=s_A[:, r * D:(r + 1) * D], in0=s_x[:, r * D:(r + 1) * D],
                scalar1=s_stats[:, so:so + 1], scalar2=s_stats[:, so + 1:so + 2],
                op0=ALU.subtract, op1=ALU.mult),
               [stk(so), stk(so + 1), ("x", r)], akeys(r), big=True, tag=f"{tg}lx{r}")

            OP("vector", lambda q, so=so: q.tensor_scalar(
                out=s_stats[:, so + 6:so + 7], in0=s_stats[:, so + 1:so + 2],
                scalar1=0.5, scalar2=1.0, op0=ALU.mult, op1=ALU.add),
               [stk(so + 1)], [stk(so + 6)], tag=f"{tg}xu0{r}")
            OP("vector", lambda q, so=so: q.tensor_tensor(
                out=s_stats[:, so + 7:so + 8], in0=s_stats[:, so:so + 1],
                in1=s_stats[:, so + 1:so + 2], op=ALU.mult),
               [stk(so), stk(so + 1)], [stk(so + 7)], tag=f"{tg}xu1{r}")
            OP("vector", lambda q, so=so: q.tensor_scalar(
                out=s_stats[:, so + 7:so + 8], in0=s_stats[:, so + 7:so + 8],
                scalar1=0.5, scalar2=None, op0=ALU.mult),
               [stk(so + 7)], [stk(so + 7)], tag=f"{tg}xu2{r}")
            OP("vector", lambda q, r=r, so=so: q.tensor_scalar(
                out=s_x[:, r * D:(r + 1) * D], in0=s_x[:, r * D:(r + 1) * D],
                scalar1=s_stats[:, so + 6:so + 7], scalar2=s_stats[:, so + 7:so + 8],
                op0=ALU.mult, op1=ALU.subtract),
               [stk(so + 6), stk(so + 7), ("x", r)], [("x", r)], big=True, tag=f"{tg}xu3{r}")

            so1 = 64 + (r % 2) * 32
            ln_stats(s_A[:, r * D:(r + 1) * D], akeys(r), so1, tag=f"{tg}l1{r}")
            OP("vector", lambda q, r=r, so1=so1: q.tensor_scalar(
                out=s_C[:, 0:D], in0=s_A[:, r * D:(r + 1) * D],
                scalar1=s_stats[:, so1:so1 + 1], scalar2=s_stats[:, so1 + 1:so1 + 2],
                op0=ALU.subtract, op1=ALU.mult),
               [stk(so1), stk(so1 + 1)] + akeys(r), rkeys("C", 0, D), big=True, tag=f"{tg}1a{r}")

            for f in range(KD):
                pslot, pc = next_tr()
                pk = pkey(pslot, pc)
                OP("tensor", lambda q, f=f, pslot=pslot, pc=pc:
                   q.transpose(pslot[:, pc:pc + P], s_C[:, f * P:(f + 1) * P], s_id[:]),
                   rkeys("C", 0, D) + [("id",)], [pk], tag=f"{tg}t1{r}_{f}")
                OP("vector", lambda q, r=r, f=f, pslot=pslot, pc=pc: q.tensor_scalar(
                    out=s_Bf[:, f * NTOK + r * P:f * NTOK + (r + 1) * P],
                    in0=pslot[:, pc:pc + P],
                    scalar1=s_gam[:, f:f + 1], scalar2=s_gam[:, KD + f:KD + f + 1],
                    op0=ALU.mult, op1=ALU.add),
                   [pk, ("gam", 0), ("gam", 1)], [("B", (f * NTOK + r * P) // 1024)],
                   tag=f"{tg}e1{r}_{f}")

        dump(f"lnx{call}", s_A[:, :], [("A", i) for i in range(16)])
        dump(f"ln1fm{call}", s_Bf[:, :], bkeys_all)

        pairs = [(s_, t_) for s_ in range(NB) for t_ in range(s_ + 1)]
        SC0, WS0, SM0 = 0, 80, 160
        for g in range(NKV):
            qba = stream_q(f"qkvq{g}a", lambda i, g=g: qkv_wt[i * P:(i + 1) * P,
                                                              768 * g:768 * g + 512],
                           f"{tg}qwa{g}", cvt_eng="gpsimd")
            qbb = stream_q(f"qkvq{g}b", lambda i, g=g: qkv_wt[(i + 8) * P:(i + 9) * P,
                                                              768 * g:768 * g + 512],
                           f"{tg}qwb{g}", cvt_eng="gpsimd")
            ekeys = rkeys("E", qba, qba + 4096) + rkeys("E", qbb, qbb + 4096)
            for r in range(NT):
                bank = next_bank()
                def f_qmm(q, r=r, bank=bank, qba=qba, qbb=qbb):
                    for kk in range(KD):
                        eb = (qba if kk < 8 else qbb) + (kk % 8) * 512
                        mm = q.matmul(pb[bank][:, 0:512],
                                      s_Bf[:, kk * NTOK + r * P:kk * NTOK + (r + 1) * P],
                                      s_E[:, eb:eb + 512],
                                      start=(kk == 0), stop=(kk == KD - 1))
                    return mm
                OP("tensor", f_qmm, bkeys_all + ekeys, [pkey(bank)], tag=f"{tg}qm{g}_{r}")
                OP("scalar", lambda q, r=r, bank=bank: q.activation(
                    out=s_D[:, r * 768:r * 768 + 512], in_=pb[bank][:, 0:512], func=AF.Copy),
                   [pkey(bank)], rkeys("D", r * 768, r * 768 + 512), tag=f"{tg}qe{g}_{r}")
            kbb = stream_q(f"qkvk{g}", lambda i, g=g: qkv_wt[i * 256:(i + 1) * 256,
                                                             768 * g + 512:768 * (g + 1)].rearrange(
                               "(k p) f -> p k f", p=P),
                           f"{tg}kw{g}", cvt_eng="gpsimd")
            ekeys2 = rkeys("E", kbb, kbb + 4096)
            for r in range(NT):
                bank = next_bank()
                def f_kvmm(q, r=r, bank=bank, kbb=kbb):
                    for kk in range(KD):
                        mm = q.matmul(pb[bank][:, 0:256],
                                      s_Bf[:, kk * NTOK + r * P:kk * NTOK + (r + 1) * P],
                                      s_E[:, kbb + kk * 256:kbb + (kk + 1) * 256],
                                      start=(kk == 0), stop=(kk == KD - 1))
                    return mm
                OP("tensor", f_kvmm, bkeys_all + ekeys2, [pkey(bank)], tag=f"{tg}km{g}_{r}")
                OP("scalar", lambda q, r=r, bank=bank: q.activation(
                    out=s_D[:, r * 768 + 512:r * 768 + 768], in_=pb[bank][:, 0:256],
                    func=AF.Copy),
                   [pkey(bank)], rkeys("D", r * 768 + 512, r * 768 + 768), tag=f"{tg}ke{g}_{r}")

            dall = rkeys("D", 0, 6144)
            for pi, (s_, t_) in enumerate(pairs):
                for half in range(2):
                    rq = 2 * s_ + half
                    rk = 2 * t_ + half
                    def f_pr(q, rq=rq, rk=rk):
                        qv = s_D[:, rq * 768:rq * 768 + 512].rearrange("p (h d) -> p h d", h=4)
                        kv = s_D[:, rk * 768 + 512:rk * 768 + 640].rearrange(
                            "p (one d) -> p one d", one=1).broadcast_to([P, 4, HD])
                        pr = s_D[:, 6144:6656].rearrange("p (h d) -> p h d", h=4)
                        return q.tensor_tensor(out=pr, in0=qv, in1=kv, op=ALU.mult)
                    OP("vector", f_pr, dall, rkeys("D", 6144, 6656), tag=f"{tg}p{g}_{pi}_{half}")
                    sc = SC0 + half * 40 + pi * 4
                    OP("vector", lambda q, sc=sc: q.tensor_reduce(
                        out=s_G[:, sc:sc + 4],
                        in_=s_D[:, 6144:6656].rearrange("p (h d) -> p h d", h=4),
                        axis=AX.X, op=ALU.add),
                       rkeys("D", 6144, 6656) + GK, GK + [("sc", sc)], tag=f"{tg}r{g}_{pi}_{half}")

            screads = [("sc", SC0 + half * 40 + pi * 4) for half in range(2) for pi in range(10)]

            SMK = screads + GK + [("wsm",)]
            for half in range(2):
                base = SC0 + half * 40
                wbase = WS0 + half * 40
                for s_ in range(1, NB):
                    p0 = s_ * (s_ + 1) // 2
                    OP("vector", lambda q, base=base, p0=p0: q.tensor_copy(
                        s_G[:, SM0:SM0 + 4], s_G[:, base + p0 * 4:base + p0 * 4 + 4]),
                       SMK, GK + [("wsm",)], tag=f"{tg}sa{g}_{half}_{s_}")
                    for t_ in range(1, s_ + 1):
                        OP("vector", lambda q, base=base, p0=p0, t_=t_: q.tensor_tensor(
                            out=s_G[:, SM0:SM0 + 4], in0=s_G[:, SM0:SM0 + 4],
                            in1=s_G[:, base + (p0 + t_) * 4:base + (p0 + t_) * 4 + 4],
                            op=ALU.max),
                           SMK, GK + [("wsm",)], tag=f"{tg}sb{g}_{half}_{s_}_{t_}")

                    def f_subs(q, base=base, wbase=wbase, p0=p0, s_=s_):
                        for t_ in range(s_ + 1):
                            last = q.tensor_tensor(
                                out=s_G[:, wbase + (p0 + t_) * 4:wbase + (p0 + t_) * 4 + 4],
                                in0=s_G[:, base + (p0 + t_) * 4:base + (p0 + t_) * 4 + 4],
                                in1=s_G[:, SM0:SM0 + 4], op=ALU.subtract)
                        return last
                    OP("vector", f_subs, SMK, GK + [("wsm",)], tag=f"{tg}sc{g}_{half}_{s_}")

            def f_exp(q):
                for half in range(2):
                    wbase = WS0 + half * 40
                    for s_ in range(1, NB):
                        p0 = s_ * (s_ + 1) // 2
                        for t_ in range(s_ + 1):
                            c0 = wbase + (p0 + t_) * 4
                            last = q.activation(out=s_G[:, c0:c0 + 4], in_=s_G[:, c0:c0 + 4],
                                                func=AF.Exp, scale=RS)
                return last
            OP("scalar", f_exp, SMK, GK + [("wsm",)], tag=f"{tg}se{g}")

            for half in range(2):
                wbase = WS0 + half * 40
                for s_ in range(1, NB):
                    p0 = s_ * (s_ + 1) // 2
                    cols = [wbase + (p0 + t_) * 4 for t_ in range(s_ + 1)]
                    OP("vector", lambda q, cols=cols: q.tensor_copy(
                        s_G[:, SM0:SM0 + 4], s_G[:, cols[0]:cols[0] + 4]),
                       SMK, GK + [("wsm",)], tag=f"{tg}sd{g}_{half}_{s_}")
                    for t_ in range(1, s_ + 1):
                        OP("vector", lambda q, cols=cols, t_=t_: q.tensor_tensor(
                            out=s_G[:, SM0:SM0 + 4], in0=s_G[:, SM0:SM0 + 4],
                            in1=s_G[:, cols[t_]:cols[t_] + 4], op=ALU.add),
                           SMK, GK + [("wsm",)], tag=f"{tg}sf{g}_{half}_{s_}_{t_}")
                    OP("vector", lambda q: q.reciprocal(s_G[:, SM0 + 4:SM0 + 8],
                                                        s_G[:, SM0:SM0 + 4]),
                       SMK, GK + [("wsm",)], tag=f"{tg}sg2{g}_{half}_{s_}")

                    def f_mul(q, cols=cols, s_=s_):
                        for t_ in range(s_ + 1):
                            last = q.tensor_tensor(out=s_G[:, cols[t_]:cols[t_] + 4],
                                                   in0=s_G[:, cols[t_]:cols[t_] + 4],
                                                   in1=s_G[:, SM0 + 4:SM0 + 8], op=ALU.mult)
                        return last
                    OP("vector", f_mul, SMK, GK + [("wsm",)], tag=f"{tg}sh{g}_{half}_{s_}")

            for half in range(2):
                for s_ in range(NB):
                    rt = 2 * s_ + half
                    ckeys = rkeys("C", rt * D + g * 512, rt * D + (g + 1) * 512)
                    for t_ in range(s_ + 1):
                        rk = 2 * t_ + half
                        pi = s_ * (s_ + 1) // 2 + t_
                        for j in range(4):
                            wcol = WS0 + half * 40 + pi * 4 + j
                            yap = s_C[:, rt * D + (4 * g + j) * HD:rt * D + (4 * g + j + 1) * HD]
                            vap = s_D[:, rk * 768 + 640:rk * 768 + 768]
                            if s_ == 0:
                                OP("vector", lambda q, yap=yap, vap=vap: q.tensor_copy(yap, vap),
                                   dall, ckeys, tag=f"{tg}w0{g}_{half}_{j}")
                            elif t_ == 0:
                                OP("vector", lambda q, yap=yap, vap=vap, wcol=wcol:
                                   q.tensor_scalar(out=yap, in0=vap,
                                                   scalar1=s_G[:, wcol:wcol + 1],
                                                   scalar2=None, op0=ALU.mult),
                                   dall + [("wsm",)] + GK, ckeys, tag=f"{tg}wf{g}_{s_}_{half}_{j}")
                            else:
                                OP("vector", lambda q, yap=yap, vap=vap, wcol=wcol:
                                   q.scalar_tensor_tensor(
                                       out=yap, in0=vap, scalar=s_G[:, wcol:wcol + 1],
                                       in1=yap, op0=ALU.mult, op1=ALU.add),
                                   dall + [("wsm",)] + GK + ckeys, ckeys,
                                   tag=f"{tg}wa{g}_{s_}_{t_}_{half}_{j}")

        dump(f"yattn{call}", s_C[:, :], ckeys_all)

        for r in range(NT):
            for f in range(KD):
                pslot, pc = next_tr()
                pk = pkey(pslot, pc)
                OP("tensor", lambda q, r=r, f=f, pslot=pslot, pc=pc:
                   q.transpose(pslot[:, pc:pc + P], s_C[:, r * D + f * P:r * D + (f + 1) * P],
                               s_id[:]),
                   rkeys("C", r * D + f * P, r * D + (f + 1) * P) + [("id",)], [pk],
                   tag=f"{tg}yt{r}_{f}")
                OP("vector", lambda q, r=r, f=f, pslot=pslot, pc=pc:
                   q.tensor_copy(s_Bf[:, f * NTOK + r * P:f * NTOK + (r + 1) * P],
                                 pslot[:, pc:pc + P]),
                   [pk], [("B", (f * NTOK + r * P) // 1024)], tag=f"{tg}ye{r}_{f}")

        for win in range(4):
            pba = stream_q(f"proj{win}a", lambda i, win=win: proj_wt[i * P:(i + 1) * P,
                                                                     win * 512:(win + 1) * 512],
                           f"{tg}pwa{win}", cvt_eng="gpsimd")
            pbb = stream_q(f"proj{win}b", lambda i, win=win: proj_wt[(i + 8) * P:(i + 9) * P,
                                                                    win * 512:(win + 1) * 512],
                           f"{tg}pwb{win}", cvt_eng="gpsimd")
            ekeys = rkeys("E", pba, pba + 4096) + rkeys("E", pbb, pbb + 4096)
            for r in range(NT):
                bank = next_bank()
                def f_pmm(q, r=r, bank=bank, pba=pba, pbb=pbb):
                    for kk in range(KD):
                        eb = (pba if kk < 8 else pbb) + (kk % 8) * 512
                        mm = q.matmul(pb[bank][:, 0:512],
                                      s_Bf[:, kk * NTOK + r * P:kk * NTOK + (r + 1) * P],
                                      s_E[:, eb:eb + 512],
                                      start=(kk == 0), stop=(kk == KD - 1))
                    return mm
                OP("tensor", f_pmm, bkeys_all + ekeys, [pkey(bank)], tag=f"{tg}pm{win}_{r}")
                xc = r * D + win * 512
                OP("vector", lambda q, bank=bank, xc=xc, r=r: q.scalar_tensor_tensor(
                    out=s_x[:, xc:xc + 512], in0=pb[bank][:, 0:512], scalar=0.5,
                    in1=s_x[:, xc:xc + 512], op0=ALU.mult, op1=ALU.add),
                   [pkey(bank), ("x", r)], [("x", r)], big=True, tag=f"{tg}px{win}_{r}")
                OP("vector", lambda q, bank=bank, xc=xc: q.scalar_tensor_tensor(
                    out=s_A[:, xc:xc + 512], in0=pb[bank][:, 0:512], scalar=1.0,
                    in1=s_A[:, xc:xc + 512], op0=ALU.mult, op1=ALU.add),
                   [pkey(bank)] + rkeys("A", xc, xc + 512), rkeys("A", xc, xc + 512),
                   big=True, tag=f"{tg}pa{win}_{r}")

        dump(f"hba{call}", s_A[:, :], [("A", i) for i in range(16)])

        for r in range(NT):
            so2 = (r % 2) * 32
            ln_stats(s_A[:, r * D:(r + 1) * D], akeys(r), so2, tag=f"{tg}l2{r}")
            OP("vector", lambda q, r=r, so2=so2: q.tensor_scalar(
                out=s_D[:, 0:D], in0=s_A[:, r * D:(r + 1) * D],
                scalar1=s_stats[:, so2:so2 + 1], scalar2=s_stats[:, so2 + 1:so2 + 2],
                op0=ALU.subtract, op1=ALU.mult),
               [stk(so2), stk(so2 + 1)] + akeys(r), rkeys("D", 0, D), big=True, tag=f"{tg}2a{r}")
            for f in range(KD):
                pslot, pc = next_tr()
                pk = pkey(pslot, pc)
                OP("tensor", lambda q, f=f, pslot=pslot, pc=pc:
                   q.transpose(pslot[:, pc:pc + P], s_D[:, f * P:(f + 1) * P], s_id[:]),
                   rkeys("D", 0, D) + [("id",)], [pk], tag=f"{tg}t2{r}_{f}")
                OP("vector", lambda q, r=r, f=f, pslot=pslot, pc=pc: q.tensor_scalar(
                    out=s_C[:, f * NTOK + r * P:f * NTOK + (r + 1) * P],
                    in0=pslot[:, pc:pc + P],
                    scalar1=s_gam[:, 2 * KD + f:2 * KD + f + 1],
                    scalar2=s_gam[:, 3 * KD + f:3 * KD + f + 1],
                    op0=ALU.mult, op1=ALU.add),
                   [pk, ("gam", 2), ("gam", 3)], [("C", (f * NTOK + r * P) // 1024)],
                   tag=f"{tg}e2{r}_{f}")

        dump(f"xmoe{call}", s_C[:, :], ckeys_all)

        for r in range(NT):
            bank = next_bank(4, 6)
            def f_rmm(q, r=r, bank=bank):
                for kk in range(KD):
                    mm = q.matmul(pb[bank][:, 0:8],
                                  s_C[:, kk * NTOK + r * P:kk * NTOK + (r + 1) * P],
                                  s_router[:, kk * 8:(kk + 1) * 8],
                                  start=(kk == 0), stop=(kk == KD - 1))
                return mm
            OP("tensor", f_rmm, ckeys_all + [("router",)], [pkey(bank)], tag=f"{tg}rm{r}")

            def VOP(fn, tg2):
                OP("vector", fn, GK + [pkey(bank), ("gate", r)],
                   GK + [("gate", r)], tag=tg2)
            VOP(lambda q, bank=bank: q.tensor_reduce(
                out=s_G[:, 8:9], in_=pb[bank][:, 0:8], axis=AX.X, op=ALU.max),
                f"{tg}ga{r}")
            VOP(lambda q, bank=bank: q.tensor_scalar(
                out=s_G[:, 0:8], in0=pb[bank][:, 0:8], scalar1=s_G[:, 8:9],
                scalar2=None, op0=ALU.subtract), f"{tg}gb{r}")
            OP("scalar", lambda q: q.activation(out=s_G[:, 0:8], in_=s_G[:, 0:8],
                                                func=AF.Exp),
               GK, GK, tag=f"{tg}gc{r}")
            VOP(lambda q: q.tensor_reduce(out=s_G[:, 9:10], in_=s_G[:, 0:8],
                                          axis=AX.X, op=ALU.add), f"{tg}gd{r}")
            VOP(lambda q: q.reciprocal(s_G[:, 10:11], s_G[:, 9:10]), f"{tg}ge2{r}")
            VOP(lambda q: q.tensor_scalar(out=s_G[:, 16:24], in0=s_G[:, 0:8],
                                          scalar1=s_G[:, 10:11], scalar2=None,
                                          op0=ALU.mult), f"{tg}gf{r}")
            VOP(lambda q: q.tensor_reduce(out=s_G[:, 11:12], in_=s_G[:, 16:24],
                                          axis=AX.X, op=ALU.max), f"{tg}gg{r}")
            VOP(lambda q: q.tensor_scalar(out=s_G[:, 24:32], in0=s_G[:, 16:24],
                                          scalar1=s_G[:, 11:12], scalar2=None,
                                          op0=ALU.is_lt), f"{tg}gh{r}")
            VOP(lambda q: q.tensor_tensor(out=s_G[:, 24:32], in0=s_G[:, 16:24],
                                          in1=s_G[:, 24:32], op=ALU.mult), f"{tg}gi{r}")
            VOP(lambda q: q.tensor_reduce(out=s_G[:, 12:13], in_=s_G[:, 24:32],
                                          axis=AX.X, op=ALU.max), f"{tg}gj{r}")
            VOP(lambda q: q.tensor_scalar(out=s_G[:, 24:32], in0=s_G[:, 16:24],
                                          scalar1=s_G[:, 11:12], scalar2=None,
                                          op0=ALU.is_ge), f"{tg}gk{r}")
            VOP(lambda q: q.tensor_reduce(out=s_G[:, 13:14], in_=s_G[:, 24:32],
                                          axis=AX.X, op=ALU.add), f"{tg}gl{r}")
            VOP(lambda q: q.tensor_scalar(out=s_G[:, 14:15], in0=s_G[:, 13:14],
                                          scalar1=2.0, scalar2=None, op0=ALU.is_ge),
                f"{tg}gm{r}")
            VOP(lambda q: q.tensor_tensor(out=s_G[:, 15:16], in0=s_G[:, 11:12],
                                          in1=s_G[:, 12:13], op=ALU.subtract), f"{tg}gn{r}")
            VOP(lambda q: q.tensor_tensor(out=s_G[:, 15:16], in0=s_G[:, 15:16],
                                          in1=s_G[:, 14:15], op=ALU.mult), f"{tg}go{r}")
            VOP(lambda q: q.tensor_tensor(out=s_G[:, 12:13], in0=s_G[:, 12:13],
                                          in1=s_G[:, 15:16], op=ALU.add), f"{tg}gp{r}")
            VOP(lambda q: q.tensor_scalar(out=s_G[:, 24:32], in0=s_G[:, 16:24],
                                          scalar1=s_G[:, 12:13], scalar2=None,
                                          op0=ALU.is_ge), f"{tg}gq{r}")
            VOP(lambda q: q.tensor_tensor(out=s_G[:, 8:9], in0=s_G[:, 11:12],
                                          in1=s_G[:, 12:13], op=ALU.add), f"{tg}gr{r}")
            VOP(lambda q: q.reciprocal(s_G[:, 9:10], s_G[:, 8:9]), f"{tg}gs{r}")
            VOP(lambda q: q.tensor_scalar(out=s_G[:, 9:10], in0=s_G[:, 9:10],
                                          scalar1=0.5, scalar2=None, op0=ALU.mult),
                f"{tg}gt{r}")
            VOP(lambda q, r=r: q.tensor_tensor(out=s_gates[:, r * 8:(r + 1) * 8],
                                               in0=s_G[:, 16:24], in1=s_G[:, 24:32],
                                               op=ALU.mult), f"{tg}gu{r}")
            VOP(lambda q, r=r: q.tensor_scalar(out=s_gates[:, r * 8:(r + 1) * 8],
                                               in0=s_gates[:, r * 8:(r + 1) * 8],
                                               scalar1=s_G[:, 9:10], scalar2=None,
                                               op0=ALU.mult), f"{tg}gv2{r}")

        dump(f"gates{call}", s_gates[:, :], [("gate", r) for r in range(NT)])

        def h1g_ap(m, c0, c1):
            if m < 16:
                return s_A[:, m * NTOK + c0:m * NTOK + c1]
            return s_Bf[:, (m - 16) * NTOK + c0:(m - 16) * NTOK + c1]

        def h1g_keys(m, c0, c1):
            if m < 16:
                return rkeys("A", m * NTOK + c0, m * NTOK + c1)
            return rkeys("B", (m - 16) * NTOK + c0, (m - 16) * NTOK + c1)

        h1keys = [("A", i) for i in range(16)] + [("B", i) for i in range(16)]

        for e in range(NE):
            for mg in range(16):
                wb = stream_q(f"w1_{e}_{mg}",
                              lambda i, e=e, mg=mg: moe_w1[e * D + i * 256:e * D + (i + 1) * 256,
                                                           mg * 256:(mg + 1) * 256].rearrange(
                                  "(k p) f -> p k f", p=P),
                              f"{tg}w1_{e}_{mg}", cvt_eng="vector")
                eke = rkeys("E", wb, wb + 4096)
                for m in (2 * mg, 2 * mg + 1):
                    for half in range(2):
                        bank = next_bank()
                        def f_h1(q, m=m, half=half, bank=bank, wb=wb):
                            for kk in range(KD):
                                mm = q.matmul(
                                    pb[bank][:, 0:512],
                                    s_E[:, wb + kk * 256 + (m % 2) * 128:
                                         wb + kk * 256 + (m % 2) * 128 + 128],
                                    s_C[:, kk * NTOK + half * 512:kk * NTOK + (half + 1) * 512],
                                    start=(kk == 0), stop=(kk == KD - 1))
                            return mm
                        OP("tensor", f_h1, ckeys_all + eke, [pkey(bank)],
                           tag=f"{tg}h1_{e}_{m}_{half}")
                        OP("scalar", lambda q, m=m, half=half, bank=bank: q.activation(
                            out=h1g_ap(m, half * 512, (half + 1) * 512),
                            in_=pb[bank][:, 0:512], func=AF.Gelu),
                           [pkey(bank)], h1g_keys(m, half * 512, (half + 1) * 512),
                           tag=f"{tg}ge_{e}_{m}_{half}")

            for ow in range(16):
                wb2 = stream_q(f"w2_{e}_{ow}",
                               lambda i, e=e, ow=ow: moe_w2[e * FFN + i * 512:
                                                            e * FFN + (i + 1) * 512,
                                                            ow * 128:(ow + 1) * 128].rearrange(
                                   "(k p) f -> p k f", p=P),
                               f"{tg}w2_{e}_{ow}", cvt_eng="vector")
                wke = rkeys("E", wb2, wb2 + 4096)
                for r in range(NT):
                    bank = next_bank()
                    def f_y(q, r=r, bank=bank, wb2=wb2):
                        for k2 in range(32):
                            mm = q.matmul(
                                pb[bank][:, 0:128],
                                h1g_ap(k2, r * P, (r + 1) * P),
                                s_E[:, wb2 + k2 * 128:wb2 + (k2 + 1) * 128],
                                start=(k2 == 0), stop=(k2 == 31))
                        return mm
                    OP("tensor", f_y, h1keys + wke, [pkey(bank)], tag=f"{tg}y_{e}_{ow}_{r}")
                    trot = 256 + ((ow * NT + r) % 2) * 128
                    OP("scalar", lambda q, e=e, r=r, bank=bank, trot=trot: q.activation(
                        out=s_G[:, trot:trot + 128], in_=pb[bank][:, 0:128], func=AF.Copy,
                        scale=s_gates[:, r * 8 + e:r * 8 + e + 1]),
                       [pkey(bank), ("gate", r)] + gkeys(trot, trot + 128),
                       gkeys(trot, trot + 128), tag=f"{tg}gv_{e}_{ow}_{r}")
                    xc = r * D + ow * 128
                    OP("gpsimd", lambda q, xc=xc, r=r, trot=trot: q.tensor_tensor(
                        out=s_x[:, xc:xc + 128], in0=s_x[:, xc:xc + 128],
                        in1=s_G[:, trot:trot + 128], op=ALU.add),
                       gkeys(trot, trot + 128) + [("x", r)], [("x", r)],
                       tag=f"{tg}xa_{e}_{ow}_{r}")

    def deltanet():
        tg = "dn"
        akeys_all = [("A", i) for i in range(16)]

        def st_exchange(H):
            prs = [b * 16 + H * 8 + hr for b in range(4) for hr in range(8)]
            rdk = ([("St", p) for p in prs] +
                   [("C", 8 + 2 * b + H) for b in range(4)])
            for hr in range(8):
                DMA("scalar",
                    st_exp_hs[H].ap()[hr * 512:(hr + 1) * 512, :]
                    .rearrange("(b u) d -> u b d", u=P),
                    s_C[:, 8192:16384].rearrange("u (b hd) -> u b hd", b=4)
                    [:, :, H * 1024 + hr * P:H * 1024 + (hr + 1) * P],
                    rdk, [("stexp", H)], tag=f"stexp{H}_{hr}", sgrp="xg")
            OP("gpsimd", lambda q, H=H: q.collective_compute(
                "AllGather", ALU.bypass, ins=[st_exp_hs[H].ap().opt()],
                outs=[st_gath_hs[H].ap().opt()],
                replica_groups=[list(range(NCORE))]),
               [("stexp", H)], [("stg", H)], kind="cc", tag=f"ccst{H}")

        for r in range(NT):
            so = (r % 2) * 32
            ln_stats(s_x[:, r * D:(r + 1) * D], [("x", r)], so, tag=f"{tg}l{r}")
            OP("vector", lambda q, r=r, so=so: q.tensor_scalar(
                out=s_D[:, 0:D], in0=s_x[:, r * D:(r + 1) * D],
                scalar1=s_stats[:, so:so + 1], scalar2=s_stats[:, so + 1:so + 2],
                op0=ALU.subtract, op1=ALU.mult),
               [stk(so), stk(so + 1), ("x", r)], rkeys("D", 0, D), big=True, tag=f"{tg}a{r}")
            for f in range(KD):
                pslot, pc = next_tr()
                pk = pkey(pslot, pc)
                OP("tensor", lambda q, f=f, pslot=pslot, pc=pc:
                   q.transpose(pslot[:, pc:pc + P], s_D[:, f * P:(f + 1) * P], s_id[:]),
                   rkeys("D", 0, D) + [("id",)], [pk], tag=f"{tg}t{r}_{f}")
                OP("vector", lambda q, r=r, f=f, pslot=pslot, pc=pc:
                   q.tensor_copy(s_A[:, f * NTOK + r * P:f * NTOK + (r + 1) * P],
                                 pslot[:, pc:pc + P]),
                   [pk], [("A", (f * NTOK + r * P) // 1024)], tag=f"{tg}e{r}_{f}")

        dump("lnd", s_A[:, :], akeys_all)

        abk = [("ab", i) for i in range(KD)]
        for r in range(NT):
            bank = next_bank(4, 6)
            def f_ab(q, r=r, bank=bank):
                for kk in range(KD):
                    q.matmul(pb[bank][:, 0:32],
                             s_A[:, kk * NTOK + r * P:kk * NTOK + (r + 1) * P],
                             s_ab[:, kk * 32:(kk + 1) * 32],
                             start=(kk == 0), stop=False)
                return q.matmul(pb[bank][:, 0:32], s_onesrow[:],
                                s_brow[0:1, 0:32], start=False, stop=True)
            OP("tensor", f_ab, akeys_all + abk + [("onesrow",), ("brow", 0)], [pkey(bank)],
               tag=f"{tg}ab{r}")

            OP("scalar", lambda q, bank=bank: q.activation(
                out=s_G[:, 0:16], in_=pb[bank][:, 0:16], func=AF.Sigmoid),
               [pkey(bank)] + GK, GK, tag=f"{tg}sga{r}")
            OP("scalar", lambda q, r=r: q.activation(
                out=s_la[:, r * 16:(r + 1) * 16], in_=s_G[:, 0:16],
                func=AF.Ln, bias=s_eps2[:]),
               [("eps",)] + GK, GK + [("la", r)], tag=f"{tg}sgb{r}")

            OP("vector", lambda q, r=r, bank=bank: q.tensor_scalar(
                out=s_G[:, 16:32], in0=pb[bank][:, 16:32], scalar1=s_zero[:], scalar2=None,
                op0=ALU.max),
               [pkey(bank), ("eps",)] + GK, GK, tag=f"{tg}s1{r}")

            OP("scalar", lambda q, bank=bank: q.activation(
                out=s_G[:, 32:48], in_=pb[bank][:, 16:32], func=AF.Abs),
               [pkey(bank)] + GK, GK, tag=f"{tg}s2a{r}")
            OP("scalar", lambda q: q.activation(
                out=s_G[:, 32:48], in_=s_G[:, 32:48], func=AF.Exp, scale=-1.0),
               GK, GK, tag=f"{tg}s2b{r}")
            OP("scalar", lambda q: q.activation(
                out=s_G[:, 32:48], in_=s_G[:, 32:48], func=AF.Ln, bias=s_one[:]),
               [("eps",)] + GK, GK, tag=f"{tg}s2c{r}")
            OP("vector", lambda q, r=r: q.tensor_tensor(
                out=s_beta[:, r * 16:(r + 1) * 16], in0=s_G[:, 16:32], in1=s_G[:, 32:48],
                op=ALU.add),
               GK, [("beta", r)], tag=f"{tg}bt{r}")

            bank2 = next_bank(4, 6)
            OP("tensor", lambda q, r=r, bank2=bank2: q.matmul(
                pb[bank2][:, 0:16], s_triu[:], s_la[:, r * 16:(r + 1) * 16],
                start=True, stop=True),
               [("triu",), ("la", r)], [pkey(bank2)], tag=f"{tg}cl{r}")

            def f_ce(q, r=r, bank2=bank2):
                q.tensor_copy(s_cla[:, r * 16:(r + 1) * 16], pb[bank2][:, 0:16])
                return q.tensor_scalar(out=s_ncla[:, r * 16:(r + 1) * 16],
                                       in0=pb[bank2][:, 0:16],
                                       scalar1=-1.0, scalar2=None, op0=ALU.mult)
            OP("vector", f_ce, [pkey(bank2)], [("cla", r)], tag=f"{tg}ne{r}")

        for hg in range(8):
            h0 = 2 * hg
            wqb = stream_q(None, lambda i, hg=hg: wq_d[i * 256:(i + 1) * 256,
                                                       hg * 256:(hg + 1) * 256].rearrange(
                               "(k p) f -> p k f", p=P),
                           f"{tg}wq{hg}", cvt_eng="gpsimd")
            eke = rkeys("E", wqb, wqb + 4096)
            for hp in range(2):
                h = h0 + hp
                for half in range(2):
                    bank = next_bank()
                    def f_qp(q, hp=hp, half=half, bank=bank, wqb=wqb):
                        for kk in range(KD):
                            mm = q.matmul(
                                pb[bank][:, 0:512],
                                s_E[:, wqb + kk * 256 + hp * 128:
                                     wqb + kk * 256 + hp * 128 + 128],
                                s_A[:, kk * NTOK + half * 512:kk * NTOK + (half + 1) * 512],
                                start=(kk == 0), stop=(kk == KD - 1))
                        return mm
                    OP("tensor", f_qp, akeys_all + eke, [pkey(bank)], tag=f"{tg}qp{h}_{half}")
                    OP("vector", lambda q, h=h, half=half, bank=bank: q.tensor_scalar(
                        out=s_Bf[:, h * NTOK + half * 512:h * NTOK + (half + 1) * 512],
                        in0=pb[bank][:, 0:512], scalar1=s_bq[:, h:h + 1], scalar2=None,
                        op0=ALU.add),
                       [pkey(bank), ("bq",)], [("B", (h * NTOK + half * 512) // 1024)],
                       tag=f"{tg}qe{h}_{half}")

            DMA("sync", s_G[0:1, 0:256],
                bk_d.rearrange("(one d) -> one d", one=1)[0:1, hg * 256:(hg + 1) * 256], GK, GK,
                tag=f"{tg}bkl{hg}")
            OP("vector", lambda q: q.tensor_copy(s_brow[0:1, 32:288], s_G[0:1, 0:256]),
               GK, [("brow", 1)], tag=f"{tg}bkc{hg}")
            DMA("sync", s_G[0:1, 0:256],
                bv_d.rearrange("(one d) -> one d", one=1)[0:1, hg * 256:(hg + 1) * 256], GK, GK,
                tag=f"{tg}bvl{hg}")
            OP("vector", lambda q: q.tensor_copy(s_brow[0:1, 288:544], s_G[0:1, 0:256]),
               GK, [("brow", 2)], tag=f"{tg}bvc{hg}")

            wkb = stream_q(None, lambda i, hg=hg: wk_d[i * 256:(i + 1) * 256,
                                                       hg * 256:(hg + 1) * 256].rearrange(
                               "(k p) f -> p k f", p=P),
                           f"{tg}wk{hg}", cvt_eng="gpsimd")
            eke2 = rkeys("E", wkb, wkb + 4096)
            for r in range(NT):
                bank = next_bank()
                def f_kp(q, r=r, bank=bank, wkb=wkb):
                    for kk in range(KD):
                        q.matmul(pb[bank][:, 0:256],
                                 s_A[:, kk * NTOK + r * P:kk * NTOK + (r + 1) * P],
                                 s_E[:, wkb + kk * 256:wkb + (kk + 1) * 256],
                                 start=(kk == 0), stop=False)
                    return q.matmul(pb[bank][:, 0:256], s_onesrow[:], s_brow[0:1, 32:288],
                                    start=False, stop=True)
                OP("tensor", f_kp, akeys_all + eke2 + [("onesrow",), ("brow", 1)], [pkey(bank)],
                   tag=f"{tg}kp{r}")
                OP("scalar", lambda q, r=r, bank=bank: q.activation(
                    out=s_D[:, r * 256:(r + 1) * 256], in_=pb[bank][:, 0:256], func=AF.Copy),
                   [pkey(bank)], rkeys("D", r * 256, (r + 1) * 256), tag=f"{tg}kc{r}")

                OP("vector", lambda q, r=r: q.tensor_tensor(
                    out=s_D[:, 6144:6400].rearrange("p (h d) -> p h d", h=2),
                    in0=s_D[:, r * 256:(r + 1) * 256].rearrange("p (h d) -> p h d", h=2),
                    in1=s_D[:, r * 256:(r + 1) * 256].rearrange("p (h d) -> p h d", h=2),
                    op=ALU.mult),
                   rkeys("D", r * 256, (r + 1) * 256), [("D", 6)], tag=f"{tg}n1a{r}")
                OP("vector", lambda q: q.tensor_reduce(
                    out=s_stats[:, 64:66],
                    in_=s_D[:, 6144:6400].rearrange("p (h d) -> p h d", h=2),
                    axis=AX.X, op=ALU.add),
                   [("D", 6)], [stk(64)], tag=f"{tg}n1b{r}")
                OP("vector", lambda q: q.tensor_scalar(
                    out=s_stats[:, 64:66], in0=s_stats[:, 64:66],
                    scalar1=s_eps3[:], scalar2=None, op0=ALU.max),
                   [("eps",), stk(64)], [stk(64)], tag=f"{tg}n1c{r}")

                OP("scalar", lambda q: q.activation(
                    out=s_stats[:, 80:82], in_=s_stats[:, 64:66], func=AF.Ln),
                   [stk(64)], [stk(80)], tag=f"{tg}n2a{r}")
                OP("scalar", lambda q: q.activation(
                    out=s_stats[:, 80:82], in_=s_stats[:, 80:82], func=AF.Exp, scale=-0.5),
                   [stk(80)], [stk(80)], tag=f"{tg}n2b{r}")

                OP("vector", lambda q, r=r, hg=hg: q.tensor_tensor(
                    out=s_stats[:, 82:84], in0=s_stats[:, 80:82],
                    in1=s_beta[:, r * 16 + 2 * hg:r * 16 + 2 * hg + 2], op=ALU.mult),
                   [stk(80), ("beta", r)], [stk(82)], tag=f"{tg}n3a{r}")

                def f_kn3b(q, r=r):
                    for j in range(2):
                        last = q.tensor_scalar(
                            out=s_D[:, r * 256 + j * 128:r * 256 + (j + 1) * 128],
                            in0=s_D[:, r * 256 + j * 128:r * 256 + (j + 1) * 128],
                            scalar1=s_stats[:, 82 + j:83 + j], scalar2=None, op0=ALU.mult)
                    return last
                OP("vector", f_kn3b, [stk(82)] + rkeys("D", r * 256, (r + 1) * 256),
                   rkeys("D", r * 256, (r + 1) * 256), tag=f"{tg}n3b{r}")

                for j in range(2):
                    pslot, pc = next_tr()
                    pk = pkey(pslot, pc)
                    OP("tensor", lambda q, r=r, j=j, pslot=pslot, pc=pc: q.transpose(
                        pslot[:, pc:pc + P], s_D[:, r * 256 + j * 128:r * 256 + (j + 1) * 128],
                        s_id[:]),
                       rkeys("D", r * 256, (r + 1) * 256) + [("id",)], [pk],
                       tag=f"{tg}bt{r}_{j}")
                    OP("vector", lambda q, r=r, j=j, pslot=pslot, pc=pc: q.tensor_copy(
                        s_D[:, 2048 + j * 1024 + r * P:2048 + j * 1024 + (r + 1) * P],
                        pslot[:, pc:pc + P]),
                       [pk], [("D", (2048 + j * 1024 + r * P) // 1024)], tag=f"{tg}be{r}_{j}")

            wvb = stream_q(None, lambda i, hg=hg: wv_d[i * 256:(i + 1) * 256,
                                                       hg * 256:(hg + 1) * 256].rearrange(
                               "(k p) f -> p k f", p=P),
                           f"{tg}wv{hg}", cvt_eng="gpsimd")
            evk = rkeys("E", wvb, wvb + 4096)
            for r in range(NT):
                bank = next_bank()
                def f_vp(q, r=r, bank=bank, wvb=wvb):
                    for kk in range(KD):
                        q.matmul(pb[bank][:, 0:256],
                                 s_A[:, kk * NTOK + r * P:kk * NTOK + (r + 1) * P],
                                 s_E[:, wvb + kk * 256:wvb + (kk + 1) * 256],
                                 start=(kk == 0), stop=False)
                    return q.matmul(pb[bank][:, 0:256], s_onesrow[:], s_brow[0:1, 288:544],
                                    start=False, stop=True)
                OP("tensor", f_vp, akeys_all + evk + [("onesrow",), ("brow", 2)], [pkey(bank)],
                   tag=f"{tg}vp{r}")
                OP("scalar", lambda q, r=r, bank=bank: q.activation(
                    out=s_D[:, 4096 + r * 256:4096 + (r + 1) * 256], in_=pb[bank][:, 0:256],
                    func=AF.Copy),
                   [pkey(bank)], rkeys("D", 4096 + r * 256, 4096 + (r + 1) * 256),
                   tag=f"{tg}ve{r}")

            for r in range(NT):
                c01 = r & 1
                for j in range(2):
                    h = h0 + j
                    pair = (r >> 1) * 16 + h
                    cbank = 4 if (r * 2 + j) % 2 == 0 else 5
                    abank = 9 - cbank
                    lacol = s_la[:, r * 16 + h:r * 16 + h + 1]
                    nclacol = s_ncla[:, r * 16 + h:r * 16 + h + 1]
                    OP("vector", lambda q, lacol=lacol: q.tensor_scalar(
                        out=s_G[:, 0:128], in0=s_triu[:], scalar1=lacol, scalar2=None,
                        op0=ALU.mult),
                       [("triu",), ("la", r)] + GK, GK, tag=f"{tg}c1_{r}_{h}")
                    OP("tensor", lambda q, cbank=cbank: q.matmul(
                        pb[cbank][:, 0:128], s_onesf[:], s_G[:, 0:128], start=True, stop=True),
                       [("onesf",)] + GK, [pkey(cbank)], tag=f"{tg}c2_{r}_{h}")
                    OP("vector", lambda q, cbank=cbank: q.tensor_tensor(
                        out=s_G[:, 128:256], in0=pb[cbank][:, 0:128], in1=s_mst[:], op=ALU.add),
                       [pkey(cbank), ("mst",)] + GK, GK, tag=f"{tg}c3_{r}_{h}")

                    def f_dt(q, nclacol=nclacol, pair=pair, c01=c01):
                        q.activation(out=s_G[:, 256:384], in_=s_G[:, 128:256],
                                     func=AF.Exp, bias=nclacol)
                        if c01 == 0:
                            q.activation(out=s_td0[:, pair:pair + 1],
                                         in_=s_G[:, 255:256], func=AF.Exp)
                        q.activation(out=s_stats[:, 96:97], in_=s_G[:, 255:256],
                                     func=AF.Exp, bias=nclacol)
                        return q.activation(out=s_ces[:, pair * 2 + c01:pair * 2 + c01 + 1],
                                            in_=s_G[:, 255:256], func=AF.Copy)
                    OP("scalar", f_dt, [("cla", r)] + GK,
                       GK + [("td0", pair), stk(96), ("ce", pair * 2 + c01)],
                       tag=f"{tg}dt_{r}_{h}")
                    OP("tensor", lambda q, j=j, h=h, r=r, abank=abank: q.matmul(
                        pb[abank][:, 0:128],
                        s_D[:, 2048 + j * 1024 + r * P:2048 + j * 1024 + (r + 1) * P],
                        s_Bf[:, h * NTOK + r * P:h * NTOK + (r + 1) * P],
                        start=True, stop=True),
                       [("D", (2048 + j * 1024 + r * P) // 1024),
                        ("B", (h * NTOK + r * P) // 1024)], [pkey(abank)],
                       tag=f"{tg}at_{r}_{h}")
                    OP("vector", lambda q, abank=abank: q.tensor_tensor(
                        out=s_D[:, 6144:6272], in0=pb[abank][:, 0:128],
                        in1=s_G[:, 256:384], op=ALU.mult),
                       [pkey(abank)] + GK, [("D", 6)], tag=f"{tg}am_{r}_{h}")
                    OP("vector", lambda q, r=r, j=j: q.tensor_scalar(
                        out=s_D[:, 6272:6400],
                        in0=s_D[:, r * 256 + j * 128:r * 256 + (j + 1) * 128],
                        scalar1=s_stats[:, 96:97], scalar2=None, op0=ALU.mult),
                       [stk(96)] + rkeys("D", r * 256, (r + 1) * 256), [("D", 6)],
                       tag=f"{tg}kw_{r}_{h}")
                    OP("tensor", lambda q, r=r, j=j, cbank=cbank: q.matmul(
                        pb[cbank][:, 0:128], s_D[:, 6144:6272],
                        s_D[:, 4096 + r * 256 + j * 128:4096 + r * 256 + (j + 1) * 128],
                        start=True, stop=True),
                       [("D", 6), ("D", (4096 + r * 256 + j * 128) // 1024)], [pkey(cbank)],
                       tag=f"{tg}yl_{r}_{h}")
                    xc = r * D + h * HD
                    OP("vector", lambda q, xc=xc, cbank=cbank, r=r: q.scalar_tensor_tensor(
                        out=s_x[:, xc:xc + HD], in0=pb[cbank][:, 0:128], scalar=1.0,
                        in1=s_x[:, xc:xc + HD], op0=ALU.mult, op1=ALU.add),
                       [pkey(cbank), ("x", r)], [("x", r)], tag=f"{tg}yx_{r}_{h}")
                    OP("tensor", lambda q, r=r, j=j, abank=abank: q.matmul(
                        pb[abank][:, 0:128], s_D[:, 6272:6400],
                        s_D[:, 4096 + r * 256 + j * 128:4096 + r * 256 + (j + 1) * 128],
                        start=True, stop=True),
                       [("D", 6), ("D", (4096 + r * 256 + j * 128) // 1024)], [pkey(abank)],
                       tag=f"{tg}dl_{r}_{h}")
                    if c01 == 0:
                        def f_st0(q, pair=pair, abank=abank):
                            q.tensor_copy(s_C[:, pair * P:(pair + 1) * P], pb[abank][:, 0:128])
                            return q.tensor_copy(s_C[:, 8192 + pair * P:8192 + (pair + 1) * P],
                                                 pb[abank][:, 0:128])
                        OP("vector", f_st0, [pkey(abank)],
                           rkeys("C", pair * P, (pair + 1) * P) +
                           rkeys("C", 8192 + pair * P, 8192 + (pair + 1) * P) +
                           [("dT0", pair), ("St", pair)], tag=f"{tg}s0_{r}_{h}")
                    else:
                        OP("scalar", lambda q, pair=pair: q.activation(
                            out=s_stats[:, 97:98], in_=s_ces[:, pair * 2 + 1:pair * 2 + 2],
                            func=AF.Exp),
                           [("ce", pair * 2 + 1)], [stk(97)], tag=f"{tg}t1_{r}_{h}")

                        OP("vector", lambda q, pair=pair: q.tensor_scalar(
                            out=s_D[:, 6400:6528],
                            in0=s_C[:, 8192 + pair * P:8192 + (pair + 1) * P],
                            scalar1=s_stats[:, 97:98], scalar2=None, op0=ALU.mult),
                           [("St", pair), stk(97)] +
                           rkeys("C", 8192 + pair * P, 8192 + (pair + 1) * P),
                           [("D", 6)], tag=f"{tg}s1a_{r}_{h}")
                        OP("vector", lambda q, pair=pair, abank=abank: q.tensor_tensor(
                            out=s_C[:, 8192 + pair * P:8192 + (pair + 1) * P],
                            in0=pb[abank][:, 0:128], in1=s_D[:, 6400:6528], op=ALU.add),
                           [pkey(abank), ("D", 6)] +
                           rkeys("C", 8192 + pair * P, 8192 + (pair + 1) * P),
                           rkeys("C", 8192 + pair * P, 8192 + (pair + 1) * P) +
                           [("St", pair)], tag=f"{tg}s1b_{r}_{h}")
                        OP("vector", lambda q, pair=pair: q.tensor_tensor(
                            out=s_csum[:, pair:pair + 1], in0=s_ces[:, pair * 2:pair * 2 + 1],
                            in1=s_ces[:, pair * 2 + 1:pair * 2 + 2], op=ALU.add),
                           [("ce", pair * 2), ("ce", pair * 2 + 1)], [("cs", pair)],
                           tag=f"{tg}cs_{r}_{h}")

            if hg == 3:
                st_exchange(0)

        dump("dnstloc", s_C[:, 8192:16384], [("St", pair) for pair in range(64)])
        dump("dncsum", s_csum[:, 0:64], [("cs", pair) for pair in range(64)])

        csw = [("cs", pair) for pair in range(64)]
        DMA("gpsimd", cs_exp_h.ap().rearrange("(one pair) one2 -> one pair one2", one=1),
            s_csum[0:1, 0:64].rearrange("one (pair one2) -> one pair one2", one2=1),
            csw, [("csexp",)], tag="csexp", sgrp="msc")
        OP("gpsimd", lambda q: q.collective_compute(
            "AllGather", ALU.bypass, ins=[cs_exp_h.ap().opt()], outs=[cs_gath_h.ap().opt()],
            replica_groups=[list(range(NCORE))]),
           [("csexp",)], [("csgat",)], kind="cc", tag="cccs")
        st_exchange(1)

        DMA("gpsimd", s_csg[:].rearrange("p (r one) -> p r one", one=1),
            cs_gath_h.ap().rearrange("(r p) one -> p r one", p=64),
            [("csgat",)], [("csgs",)], tag="csgl", slow=True, sgrp="msc")

        for rr in range(8):
            OP("vector", lambda q, rr=rr: q.tensor_tensor(
                out=s_G[0:64, 0:8], in0=s_csg[:],
                in1=s_m2exp[:, rr * 8:(rr + 1) * 8], op=ALU.mult),
               [("csgs",), ("m2exp",)] + GK, GK, tag=f"dnLa{rr}")
            OP("vector", lambda q, rr=rr: q.tensor_reduce(
                out=s_L[:, rr:rr + 1], in_=s_G[0:64, 0:8], axis=AX.X, op=ALU.add),
               GK, GK + [("L",)], tag=f"dnLb{rr}")
        OP("scalar", lambda q: q.activation(out=s_scl[:], in_=s_L[:], func=AF.Exp),
           [("L",)], [("scl",)], tag="dnsE")
        OP("vector", lambda q: q.tensor_tensor(out=s_scl[:], in0=s_scl[:], in1=s_mask1[:],
                                               op=ALU.mult),
           [("scl",), ("mask1",)], [("scl",)], tag="dnsM")
        DMA("gpsimd", sclT_dram.ap(), s_scl[:], [("scl",)], [("sclTd",)], tag="sclTd",
            sgrp="msc")
        DMA("gpsimd", s_sclT[:], sclT_dram.ap().rearrange("p r -> r p"),
            [("sclTd",)], [("sclT",)], tag="sclTl", slow=True, sgrp="msc")

        def f_bc(q):
            for rr in range(8):
                last = q.tensor_scalar(out=s_G[0:8, rr * 64:(rr + 1) * 64],
                                       in0=s_sclT[:], scalar1=s_id8[:, rr:rr + 1],
                                       scalar2=None, op0=ALU.mult)
            return last
        OP("vector", f_bc, [("sclT",), ("id8",)] + GK, GK + [("bcm",)], tag="dnbcm")
        for rr in range(8):
            OP("tensor", lambda q, rr=rr: q.matmul(
                pb[4][:, rr * 64:(rr + 1) * 64], s_onesf[0:8, :],
                s_G[0:8, rr * 64:(rr + 1) * 64], start=True, stop=True),
               [("onesf",), ("bcm",)] + GK, [pkey(4)], tag=f"dnbc{rr}")

        def phase_b(H):
            for r in range(NT):
                c01 = r & 1
                for h in range(H * 8, H * 8 + 8):
                    pair = (r >> 1) * 16 + h
                    bank = next_bank()
                    if c01 == 0:
                        OP("tensor", lambda q, h=h, r=r, bank=bank, pair=pair: q.matmul(
                            pb[bank][:, 0:128],
                            s_Bf[:, h * NTOK + r * P:h * NTOK + (r + 1) * P],
                            s_C[:, 8192 + pair * P:8192 + (pair + 1) * P],
                            start=True, stop=True),
                           [("B", (h * NTOK + r * P) // 1024), ("Stin", pair)] +
                           rkeys("C", 8192 + pair * P, 8192 + (pair + 1) * P),
                           [pkey(bank)], tag=f"{tg}yg_{r}_{h}")
                    else:
                        OP("vector", lambda q, pair=pair: q.tensor_scalar(
                            out=s_D[:, 6400:6528],
                            in0=s_C[:, 8192 + pair * P:8192 + (pair + 1) * P],
                            scalar1=s_td0[:, pair:pair + 1], scalar2=None, op0=ALU.mult),
                           [("Stin", pair), ("td0", pair)] +
                           rkeys("C", 8192 + pair * P, 8192 + (pair + 1) * P),
                           [("D", 6)], tag=f"{tg}sTa_{r}_{h}")
                        OP("vector", lambda q, pair=pair: q.tensor_tensor(
                            out=s_D[:, 6400:6528], in0=s_D[:, 6400:6528],
                            in1=s_C[:, pair * P:(pair + 1) * P], op=ALU.add),
                           [("dT0", pair), ("D", 6)] + rkeys("C", pair * P, (pair + 1) * P),
                           [("D", 6)], tag=f"{tg}sTb_{r}_{h}")
                        OP("tensor", lambda q, h=h, r=r, bank=bank: q.matmul(
                            pb[bank][:, 0:128],
                            s_Bf[:, h * NTOK + r * P:h * NTOK + (r + 1) * P],
                            s_D[:, 6400:6528], start=True, stop=True),
                           [("B", (h * NTOK + r * P) // 1024), ("D", 6)], [pkey(bank)],
                           tag=f"{tg}yg_{r}_{h}")
                    OP("scalar", lambda q, r=r, h=h: q.activation(
                        out=s_stats[:, 98:99], in_=s_cla[:, r * 16 + h:r * 16 + h + 1], func=AF.Exp),
                       [("cla", r)], [stk(98)], tag=f"{tg}cd_{r}_{h}")
                    xc = r * D + h * HD
                    OP("vector", lambda q, xc=xc, bank=bank, r=r: q.scalar_tensor_tensor(
                        out=s_x[:, xc:xc + HD], in0=pb[bank][:, 0:128],
                        scalar=s_stats[:, 98:99], in1=s_x[:, xc:xc + HD],
                        op0=ALU.mult, op1=ALU.add),
                       [pkey(bank), stk(98), ("x", r)], [("x", r)], tag=f"{tg}gx_{r}_{h}")

        stg_lc = [0]
        for H in range(2):
            for blkX in range(8):
                slots = []
                for rr in range(8):
                    sl = (stg_lc[0] % 8) * 2048
                    stg_lc[0] += 1
                    slots.append(sl)
                    DMA(("sync", "scalar")[rr % 2],
                        s_A[:, sl:sl + 512].rearrange("u (pair d) -> u pair d", d=HD),
                        st_gath_hs[H].ap()[rr * 4096 + blkX * 512:
                                           rr * 4096 + (blkX + 1) * 512, :]
                        .rearrange("(pair u) d -> u pair d", u=P),
                        [("stg", H), ("A", sl // 1024)],
                        [("A", sl // 1024), ("ldb", H, blkX)],
                        tag=f"stgl{H}_{blkX}_{rr}", sgrp="xg")
                for rr in range(8):
                    sl = slots[rr]
                    for pp in range(4):
                        pair = pp * 16 + H * 8 + blkX
                        acc = s_G[:, pp * 128:(pp + 1) * 128]
                        if rr == 0:
                            OP("vector", lambda q, acc=acc, sl=sl, pp=pp, rr=rr, pair=pair:
                               q.tensor_scalar(
                                   out=acc,
                                   in0=s_A[:, sl + pp * 128:sl + (pp + 1) * 128],
                                   scalar1=pb[4][:, rr * 64 + pair:rr * 64 + pair + 1],
                                   scalar2=None, op0=ALU.mult),
                               [("A", sl // 1024), ("ldb", H, blkX), pkey(4)] + GK,
                               GK + [("acc", pp)], tag=f"cb{H}_{blkX}_{rr}_{pp}")
                        else:
                            OP("vector", lambda q, acc=acc, sl=sl, pp=pp, rr=rr, pair=pair:
                               q.scalar_tensor_tensor(
                                   out=acc,
                                   in0=s_A[:, sl + pp * 128:sl + (pp + 1) * 128],
                                   scalar=pb[4][:, rr * 64 + pair:rr * 64 + pair + 1],
                                   in1=acc, op0=ALU.mult, op1=ALU.add),
                               [("A", sl // 1024), ("ldb", H, blkX), pkey(4),
                                ("acc", pp)] + GK,
                               GK + [("acc", pp)], tag=f"cb{H}_{blkX}_{rr}_{pp}")
                for pp in range(4):
                    pair = pp * 16 + H * 8 + blkX
                    OP("vector", lambda q, pp=pp, pair=pair: q.tensor_copy(
                        s_C[:, 8192 + pair * P:8192 + (pair + 1) * P],
                        s_G[:, pp * 128:(pp + 1) * 128]),
                       [("acc", pp)] + GK,
                       rkeys("C", 8192 + pair * P, 8192 + (pair + 1) * P) + [("Stin", pair)],
                       tag=f"cbc{H}_{blkX}_{pp}")
            phase_b(H)

        dump("dnstin", s_C[:, 8192:16384], [("Stin", pair) for pair in range(64)])
        dump("dnscl", s_scl[:, 0:8], [("scl",)])


    if "ffn0" in stages:
        ffn_call(0)
    if "dn" in stages:
        deltanet()
    if "ffn1" in stages:
        ffn_call(1)

    for r in range(NT):
        DMA("sync", y_out[r * P:(r + 1) * P, :], s_x[:, r * D:(r + 1) * D],
            [("x", r)], [("yst", r)], tag=f"st{r}")
    OP("sync", lambda q: None, [("yst", r) for r in range(NT)], [], tag="fin")

    B.resolve(csem, dsem, ccsems, sgsems)
    with es, nc.Block() as block:
        B.emit(block)
    return nc


STAGES = ("ffn0", "dn", "ffn1")


def kernel(**inputs):
    x = np.asarray(inputs["L0_attn_x"])
    consts = _static_consts()
    base = {
        "qkv_wt": np.ascontiguousarray(np.asarray(inputs["L0_ffn_qkv_weight"]).T),
        "proj_wt": np.ascontiguousarray(np.asarray(inputs["L0_ffn_proj_weight"]).T),
        "router_wt": np.ascontiguousarray(np.asarray(inputs["L0_ffn_router_weight"]).T),
        "ln1_g": np.asarray(inputs["L0_ffn_ln1_weight"]),
        "ln1_b": np.asarray(inputs["L0_ffn_ln1_bias"]),
        "ln2_g": np.asarray(inputs["L0_ffn_ln2_weight"]),
        "ln2_b": np.asarray(inputs["L0_ffn_ln2_bias"]),
        "moe_w1": np.asarray(inputs["L0_ffn_moe_w1"]),
        "moe_w2": np.asarray(inputs["L0_ffn_moe_w2"]),
        "wq": np.asarray(inputs["L0_attn_Wq"]),
        "wk": np.asarray(inputs["L0_attn_Wk"]),
        "wv": np.asarray(inputs["L0_attn_Wv"]),
        "bq": np.asarray(inputs["L0_attn_bq"]),
        "bk": np.asarray(inputs["L0_attn_bk"]),
        "bv": np.asarray(inputs["L0_attn_bv"]),
        "wab": np.ascontiguousarray(np.concatenate(
            [np.asarray(inputs["L0_attn_Walpha"]), np.asarray(inputs["L0_attn_Wbeta"])], axis=1)),
        "bab": np.ascontiguousarray(np.concatenate(
            [np.asarray(inputs["L0_attn_balpha"]), np.asarray(inputs["L0_attn_bbeta"])])),
    }
    base.update(consts)

    nc = build_program(stages=STAGES)
    in_maps = []
    for c in range(NCORE):
        m2exp, mask1 = _mk_core_consts(c)
        m = dict(base)
        m["c_m2exp"] = m2exp
        m["c_mask1"] = mask1
        m["x"] = np.ascontiguousarray(x[:, c * TLOC:(c + 1) * TLOC, :].reshape(NTOK, D))
        in_maps.append(m)
    res = run_bass_kernel_spmd(nc, in_maps, core_ids=list(range(NCORE)))
    out = np.empty((NB, 2048, D), dtype=np.float32)
    for c in range(NCORE):
        out[:, c * TLOC:(c + 1) * TLOC, :] = np.asarray(
            res.results[c]["y"], np.float32).reshape(NB, TLOC, D)
    return out
